# version 12
# baseline (speedup 1.0000x reference)
import math
import numpy as np
import ml_dtypes

import concourse.bass as bass
import concourse.mybir as mybir
import concourse.bass_utils as _bu
from concourse.bass_utils import run_bass_kernel_spmd

if not getattr(_bu, "_ldw_patched", False):
    _orig_run_command = _bu.run_command

    def _run_command_ldw(argv, **kw):
        argv = [a.replace("--enable-ldw-opt=false", "--enable-ldw-opt=true")
                if isinstance(a, str) else a for a in argv]
        return _orig_run_command(argv, **kw)

    _bu.run_command = _run_command_ldw
    _bu._ldw_patched = True

F32 = mybir.dt.float32
BF16 = mybir.dt.bfloat16
AF = mybir.ActivationFunctionType
ALU = mybir.AluOpType
AX = mybir.AxisListType

D = 2048
HQ, HKV, DH = 16, 4, 128
KVW = HKV * DH
LN_EPS = 1e-5
RMS_EPS = float(np.finfo(np.float32).eps)
SM_SCALE = 1.0 / math.sqrt(DH)

ENGS = ("tensor", "vector", "scalar", "gpsimd", "sync")
CHAIN = ("vector", "scalar")


class KB:

    def __init__(self, nc):
        self.nc = nc
        self.blocks = []
        self.cur = None
        self.sems = {}
        self.cnt = {}
        self.hw = None
        self.block_start = None
        self.dma_sems = set()
        self.last = {}

    def sem(self, name, dma=False):
        if name not in self.sems:
            self.sems[name] = self.nc.alloc_semaphore(name)
            self.cnt[name] = 0
            if dma:
                self.dma_sems.add(name)
        return self.sems[name]

    def block(self):
        self.cur = {e: [] for e in ENGS}
        self.blocks.append(self.cur)
        self.hw = {}
        self.block_start = {s: self.cnt[s] for s in self.cnt if s not in self.dma_sems}

    def emit(self, eng, fn, deps=(), inc=None, dma=False):
        if dma and inc is not None:
            self.sem(inc, dma=True)
            deps = tuple(deps) + ((inc, self.cnt[inc]),)
        if eng in CHAIN and not dma:
            deps = tuple(deps) + (self.last.get(eng),)
            assert inc is not None
        if inc is not None:
            self.sem(inc, dma=dma)
        waits = []
        for d in deps:
            if d is None:
                continue
            s, v = d
            self.sem(s)
            if v <= 0 or v <= self.block_start.get(s, 0):
                continue
            k = (eng, s)
            if self.hw.get(k, 0) >= v:
                continue
            self.hw[k] = v
            waits.append((s, v))
        tok = None
        if inc is not None:
            self.cnt[inc] += 16 if dma else 1
            tok = (inc, self.cnt[inc])
            if eng in CHAIN:
                self.last[eng] = tok
        self.cur[eng].append((waits, fn, inc, 16 if dma else 1))
        return tok

    def realize(self):
        for blk in self.blocks:
            with self.nc.Block() as block:
                for eng in ENGS:
                    items = blk[eng]
                    if not items:
                        continue

                    def make_body(items=items):
                        def body(e):
                            for waits, fn, inc, by in items:
                                for (s, v) in waits:
                                    e.wait_ge(self.sems[s], v)
                                ins = fn(e)
                                if inc is not None:
                                    ins.then_inc(self.sems[inc], by)
                        return body

                    getattr(block, eng)(make_body())


def build_nc(T=2048, HF=8192):
    H2 = 2 * HF
    NQT = T // 128
    HQT = NQT // 2
    Th = T // 2
    TBQ = min(4, HQT)
    NTB = HQT // TBQ
    GQ = min(4, HQT)
    NG = HQT // GQ
    QCW = min(512, Th)
    NQC = Th // QCW
    NHT = HF // 128
    NJ = HF // 512
    SWG = D // 8
    SWP = HF // 8

    nc = bass.Bass("TRN2", target_bir_lowering=False)

    x_h = nc.dram_tensor("x", [T, D], F32, kind="ExternalInput")
    wq_h = nc.dram_tensor("wq", [D, D], F32, kind="ExternalInput")
    wk_h = nc.dram_tensor("wk", [D, KVW], F32, kind="ExternalInput")
    wv_h = nc.dram_tensor("wv", [D, KVW], F32, kind="ExternalInput")
    wo_h = nc.dram_tensor("wo", [D, D], F32, kind="ExternalInput")
    wgu_h = nc.dram_tensor("wgu_s", [H2, SWG], F32, kind="ExternalInput")
    wpj_h = nc.dram_tensor("wproj_s", [D, SWP], F32, kind="ExternalInput")
    cos_h = nc.dram_tensor("costab", [T, 64], F32, kind="ExternalInput")
    sin_h = nc.dram_tensor("sintab", [T, 64], F32, kind="ExternalInput")
    id_h = nc.dram_tensor("ident", [128, 128], BF16, kind="ExternalInput")
    mk_h = nc.dram_tensor("maskc", [128, 896], BF16, kind="ExternalInput")
    out_h = nc.dram_tensor("out", [T, D], F32, kind="ExternalOutput")

    shg_h = nc.dram_tensor("shg", [SWG, H2], BF16, kind="Internal")
    shp_h = nc.dram_tensor("shp", [SWP, D], BF16, kind="Internal")
    wtg_h = nc.dram_tensor("wtg", [D, H2], BF16, kind="Internal")
    wtp_h = nc.dram_tensor("wtp", [HF, D], BF16, kind="Internal")
    att_h = nc.dram_tensor("attb", [T, D], F32, kind="Internal")

    kt = nc.alloc_sbuf_tensor("kt", [128, HKV * T], BF16)
    vb = nc.alloc_sbuf_tensor("vb", [128, NQT * HKV * 132], BF16)
    cos_sb = nc.alloc_sbuf_tensor("cos_sb", [128, NQT * 64], F32)
    sin_sb = nc.alloc_sbuf_tensor("sin_sb", [128, NQT * 64], F32)
    ident = nc.alloc_sbuf_tensor("ident_sb", [128, 128], BF16)
    maskc = nc.alloc_sbuf_tensor("maskc_sb", [128, 896], BF16)
    xnt = nc.alloc_sbuf_tensor("xnt", [128, 16 * Th], BF16)
    big = nc.alloc_sbuf_tensor("big", [128, 32768], BF16)
    wstr = nc.alloc_sbuf_tensor("wstr", [128, 16384], BF16)
    xf = nc.alloc_sbuf_tensor("xf", [128, 4096], F32)
    outs = nc.alloc_sbuf_tensor("outs", [128, 2048], F32)
    sl = nc.alloc_sbuf_tensor("sl", [128, 2048], BF16)
    lns = nc.alloc_sbuf_tensor("lns", [128, 16], F32)
    fss = nc.alloc_sbuf_tensor("fss", [128, 16], F32)
    qsf = nc.alloc_sbuf_tensor("qsf", [128, 64], F32)
    qsb = nc.alloc_sbuf_tensor("qsb", [128, 64], BF16)
    qtb = nc.alloc_sbuf_tensor("qtb", [128, 64], BF16)
    qtn = nc.alloc_sbuf_tensor("qtn", [128, 64], BF16)
    rss = nc.alloc_sbuf_tensor("rss", [128, 8], F32)

    ps = nc.alloc_psum_tensor("ps", [128, 4096], F32)

    def bank(i, w=512):
        return ps[:, i * 512:i * 512 + w]

    kb = KB(nc)

    WSEMS = ("wg0", "wg1", "wg2", "wg3", "wp0", "wp1", "wp2", "wp3",
             "aw0", "aw1", "wo0", "wo1", "qi0", "qi1")

    def WGRD():
        return tuple((n, kb.cnt.get(n, 0)) for n in WSEMS)

    def XGRD():
        return (("x0", kb.cnt.get("x0", 0)), ("x1", kb.cnt.get("x1", 0)),
                ("ab0", kb.cnt.get("ab0", 0)), ("ab1", kb.cnt.get("ab1", 0)))

    EOFF = 16384
    QTT = 24576
    JOFF = QTT + Th
    XOFF = JOFF + 2048
    assert XOFF + 2048 <= 32768
    WQI = (8192, 10240)
    WD1, WD2 = 12288, 14336

    BG = {}
    ST = {}
    WL = {}

    kb.block()
    kb.emit("sync", lambda e: e.dma_start(out=cos_sb[:, :], in_=cos_h.ap().rearrange("(q p) c -> p q c", p=128)), inc="ld0", dma=True)
    kb.emit("sync", lambda e: e.dma_start(out=sin_sb[:, :], in_=sin_h.ap().rearrange("(q p) c -> p q c", p=128)), inc="ld0", dma=True)
    kb.emit("sync", lambda e: e.dma_start(out=ident[:, :], in_=id_h.ap()), inc="ld0", dma=True)
    kb.emit("sync", lambda e: e.dma_start(out=maskc[:, :], in_=mk_h.ap()), inc="ld0", dma=True)
    TAB = ("ld0", 64)
    kb.emit("vector", lambda e: e.memset(vb.rearrange("p (q k c) -> p q k c", k=HKV, c=132)[:, :, :, 128:132], 1.0), inc="dv")

    def quant_chunk(src_ap, in_tok):
        ng = 32
        v3 = src_ap.rearrange("p (g c) -> p g c", c=64)
        kb.emit("vector", lambda e: e.tensor_reduce(qsf[:, 0:ng], v3, AX.X, ALU.add, apply_absolute_value=True), deps=[in_tok], inc="dv")
        kb.emit("vector", lambda e: e.tensor_scalar(qsf[:, 0:ng], qsf[:, 0:ng], 1.0 / 64.0, 1e-8, ALU.mult, ALU.max), inc="dv")
        kb.emit("vector", lambda e: e.tensor_copy(qsb[:, 0:ng], qsf[:, 0:ng]), inc="dv")
        kb.emit("vector", lambda e: e.tensor_scalar_mul(qtb[:, 0:ng], qsb[:, 0:ng], 0.5), inc="dv")
        kb.emit("vector", lambda e: e.tensor_scalar_mul(qtn[:, 0:ng], qtb[:, 0:ng], -1.0), inc="dv")
        d1 = wstr[:, WD1:WD1 + 2048]
        d2 = wstr[:, WD2:WD2 + 2048]
        tpb = qtb[:, 0:ng][:, :, None].broadcast_to([128, ng, 64])
        tnb = qtn[:, 0:ng][:, :, None].broadcast_to([128, ng, 64])
        sbv = qsb[:, 0:ng][:, :, None].broadcast_to([128, ng, 64])
        kb.emit("vector", lambda e: e.tensor_tensor(d1.rearrange("p (g c) -> p g c", c=64), v3, tpb, ALU.is_gt), deps=[ST.get("wstg")], inc="dv")
        kb.emit("vector", lambda e: e.tensor_tensor(d2.rearrange("p (g c) -> p g c", c=64), v3, tnb, ALU.is_lt), inc="dv")
        kb.emit("vector", lambda e: e.tensor_tensor(d1, d1, d2, ALU.subtract), inc="dv")
        te = kb.emit("vector", lambda e: e.tensor_tensor(v3, d1.rearrange("p (g c) -> p g c", c=64), sbv, ALU.mult), inc="dv")
        return te

    NBG = H2 // 1024
    NBP = 16 // (2048 // SWP)
    NWB = NBG + NBP

    def w_batch_emit(bi):
        slot = bi % 2
        qo = WQI[slot]
        src = wstr[:, qo:qo + 2048]
        if bi < NBG:
            b = bi
            K = SWG // 128

            def ld(e, b=b, qo=qo):
                return e.dma_start(out=wstr[:, qo:qo + 2048].rearrange("p (s c) -> p s c", c=SWG),
                                   in_=wgu_h.ap()[b * 1024:(b + 1) * 1024, :].rearrange("(s p) c -> p s c", p=128))
        else:
            b = bi - NBG
            nbp = 2048 // SWP
            K = SWP // 128

            def ld(e, b=b, qo=qo, nbp=nbp):
                return e.dma_start(out=wstr[:, qo:qo + 2048].rearrange("p (s c) -> p s c", c=SWP),
                                   in_=wpj_h.ap()[b * nbp * 128:(b + 1) * nbp * 128, :].rearrange("(s p) c -> p s c", p=128))
        tin = kb.emit("gpsimd", ld, deps=(ST.get(("wcons", slot)), WL.get(2), WL.get(3)), inc="qi%d" % slot, dma=True)
        te = quant_chunk(src, tin)
        stage = wstr[:, WD1:WD1 + 2048]
        stgv = stage.rearrange("p (k s c) -> p k s c", k=K, c=128)
        nS = 16 // K
        evs = []
        for tt in range(16):
            pb = 6 + (tt // 4) % 2
            last_mm = kb.emit("tensor", lambda e, tt=tt, pb=pb, src=src: e.matmul(
                bank(pb)[:, (tt % 4) * 128:(tt % 4) * 128 + 128],
                src[:, tt * 128:(tt + 1) * 128], ident[:, :],
                start=True, stop=True),
                deps=[te, TAB, BG.get(pb)], inc="pe")
            if tt % 4 == 3:
                tt0 = tt - 3
                if K >= 4:
                    s0, k0 = tt0 // K, tt0 % K
                    o = stgv[:, k0:k0 + 4, s0:s0 + 1, :]
                    i = bank(pb).rearrange("p (t o c) -> p t o c", t=4, o=1)
                else:
                    s0 = tt0 // K
                    o = stgv.transpose([0, 2, 1, 3])[:, s0:s0 + 2, :, :]
                    i = bank(pb).rearrange("p (s k c) -> p s k c", s=2, k=2)

                def ev(e, o=o, i=i):
                    return e.copy(o, i)
                tv = kb.emit("scalar", ev, deps=[last_mm, ST.get("wstg")], inc="ac")
                BG[pb] = tv
                evs.append(tv)
        ST[("wcons", slot)] = last_mm
        if bi < NBG:
            def wr(e, b=b, K=K, stage=stage):
                o = shg_h.ap()[:, b * 1024:(b + 1) * 1024].rearrange("(k p) h -> p k h", p=128)
                return e.dma_start(out=o, in_=stage.rearrange("p (k sc) -> p k sc", k=K))
        else:
            def wr(e, b=b, K=K, nbp=nbp, stage=stage):
                o = shp_h.ap()[:, b * nbp * 128:(b + 1) * nbp * 128].rearrange("(k p) c -> p k c", p=128)
                return e.dma_start(out=o, in_=stage.rearrange("p (k sc) -> p k sc", k=K))
        ST["wstg"] = kb.emit("sync", wr, deps=evs, inc="ws", dma=True)

    def a_qtile(half, lq):
        gq = half * HQT + lq

        def ldx(e, gq=gq):
            return e.dma_start(out=outs[:, :],
                               in_=x_h.ap()[gq * 128:(gq + 1) * 128, :])
        xtok = kb.emit("sync", ldx, deps=(ST.get("xac"), ST.get(("og", 0)), ST.get(("og", 1))), inc="xa", dma=True)
        xs = outs[:, :]
        tcp = kb.emit("scalar", lambda e, xs=xs: e.activation(big[:, JOFF:JOFF + 2048], xs, AF.Copy, accum_out=lns[:, 0:1]), deps=[xtok, ST.get("junk")], inc="ac")
        tsq = kb.emit("scalar", lambda e, xs=xs: e.activation(big[:, JOFF:JOFF + 2048], xs, AF.Square, accum_out=lns[:, 1:2]), deps=[xtok], inc="ac")
        kb.emit("vector", lambda e: e.tensor_scalar(lns[:, 2:3], lns[:, 0:1], 1.0 / D, None, ALU.mult), deps=[tcp], inc="dv")
        kb.emit("vector", lambda e: e.tensor_scalar(lns[:, 3:4], lns[:, 1:2], 1.0 / D, None, ALU.mult), deps=[tsq], inc="dv")
        kb.emit("vector", lambda e: e.tensor_tensor(lns[:, 4:5], lns[:, 2:3], lns[:, 2:3], ALU.mult), inc="dv")
        tm4 = kb.emit("vector", lambda e: e.scalar_tensor_tensor(lns[:, 5:6], lns[:, 3:4], LN_EPS, lns[:, 4:5], ALU.add, ALU.subtract), inc="dv")
        tl1 = kb.emit("scalar", lambda e: e.activation(lns[:, 6:7], lns[:, 5:6], AF.Ln), deps=[tm4], inc="ac")
        tl2 = kb.emit("scalar", lambda e: e.activation(lns[:, 7:8], lns[:, 6:7], AF.Exp, scale=-0.5), inc="ac")
        txn = kb.emit("vector", lambda e, xs=xs: e.tensor_scalar(big[:, XOFF:XOFF + 2048], xs, lns[:, 2:3], lns[:, 7:8], ALU.subtract, ALU.mult), deps=[tl2], inc="dv")
        ST["junk"] = txn
        ST["xac"] = txn
        tv = None
        for dt in range(16):
            pb = 6 + (dt // 4) % 2
            tmm = kb.emit("tensor", lambda e, dt=dt, pb=pb: e.matmul(
                bank(pb)[:, (dt % 4) * 128:(dt % 4) * 128 + 128],
                big[:, XOFF + dt * 128:XOFF + (dt + 1) * 128], ident[:, :],
                start=True, stop=True), deps=[txn, TAB, BG.get(pb)], inc="pe")
            if dt % 4 == 3:
                dt0 = dt - 3

                def evx(e, pb=pb, dt0=dt0, lq=lq):
                    o = xnt.rearrange("p (dt c) -> p dt c", dt=16)[:, dt0:dt0 + 4, lq * 128:(lq + 1) * 128]
                    return e.copy(o, bank(pb).rearrange("p (t c) -> p t c", t=4))
                tv = kb.emit("scalar", evx, deps=[tmm], inc="ac")
                BG[pb] = tv
        ST[("xnq", half, lq)] = tv
        ST[("xnt", half)] = tv

    def w_feed(dram_ap, dst_cols, width):
        par = ST.get("cstp", 0)
        ST["cstp"] = 1 - par
        stg = xf[:, 2048 + par * 1024: 2048 + par * 1024 + width]

        def ld(e, dram_ap=dram_ap, stg=stg):
            return e.dma_start(out=stg, in_=dram_ap)
        lt = kb.emit("sync", ld, deps=(ST.get(("cst", par)),
                                       ("ab0", kb.cnt.get("ab0", 0)), ("ab1", kb.cnt.get("ab1", 0)),
                                       ST.get(("ar", 0)), ST.get(("ar", 1))), inc="aw%d" % par, dma=True)
        reg = dst_cols[0] // 4096
        ct = kb.emit("scalar", lambda e, stg=stg, dst_cols=dst_cols: e.copy(wstr[:, dst_cols[0]:dst_cols[1]], stg),
                     deps=[lt, WL.get(reg)], inc="ac")
        ST[("cst", par)] = ct
        return ct

    def b_phase(half, gsel=None):
        for g in (range(NG) if gsel is None else [gsel]):
            q0 = g * GQ
            for dqc in range(2):
                for dtb in range(4):
                    s2 = (dqc * 4 + dtb) % 2
                    ctoks = []
                    for dtl in range(4):
                        dt = dtb * 4 + dtl
                        ctoks.append(w_feed(wq_h.ap()[dt * 128:(dt + 1) * 128, dqc * 1024:(dqc + 1) * 1024],
                                            (s2 * 4096 + dtl * 1024, s2 * 4096 + (dtl + 1) * 1024), 1024))
                    for dtl in range(4):
                        dt = dtb * 4 + dtl
                        for q in range(GQ):
                            for hb in range(2):
                                tok = kb.emit("tensor", lambda e, dt=dt, dtl=dtl, q=q, hb=hb, s2=s2, q0=q0: e.matmul(
                                    bank(q * 2 + hb),
                                    xnt[:, dt * Th + (q0 + q) * 128: dt * Th + (q0 + q + 1) * 128],
                                    wstr[:, s2 * 4096 + dtl * 1024 + hb * 512: s2 * 4096 + dtl * 1024 + (hb + 1) * 512],
                                    start=(dt == 0), stop=(dt == 15)),
                                    deps=[ctoks[dtl], ST[("xnq", half, q0 + q)]] + ([BG.get(q * 2 + hb)] if dt == 0 else []),
                                    inc="pe")
                    WL[s2] = tok
                for q in range(GQ):
                    lq = q0 + q
                    gq = half * HQT + lq
                    pv = ps[:, q * 1024:(q + 1) * 1024].rearrange("p (h x) -> p h x", h=8)
                    pv4 = ps[:, q * 1024:(q + 1) * 1024].rearrange("p (h t x) -> p h t x", h=8, t=2)
                    cosv = cos_sb[:, gq * 64:(gq + 1) * 64][:, None, None, :].broadcast_to([128, 8, 2, 64])
                    sinv = sin_sb[:, gq * 64:(gq + 1) * 64][:, None, :].broadcast_to([128, 8, 64])
                    t1v = xf[:, 0:1024].rearrange("p (h x) -> p h x", h=8)
                    t1v4 = xf[:, 0:1024].rearrange("p (h t x) -> p h t x", h=8, t=2)
                    t2v = xf[:, 1024:2048].rearrange("p (h x) -> p h x", h=8)
                    qcd = big[:, lq * 2048 + dqc * 1024: lq * 2048 + (dqc + 1) * 1024].rearrange("p (h x) -> p h x", h=8)
                    kb.emit("vector", lambda e, pv4=pv4, cosv=cosv, t1v4=t1v4: e.tensor_tensor(t1v4, pv4, cosv, ALU.mult), deps=[tok, TAB] + list(XGRD()), inc="dv")
                    kb.emit("vector", lambda e, pv=pv, sinv=sinv, t2v=t2v: e.tensor_tensor(t2v[:, :, 0:64], pv[:, :, 64:128], sinv, ALU.mult), inc="dv")
                    kb.emit("vector", lambda e, pv=pv, sinv=sinv, t2v=t2v: e.tensor_tensor(t2v[:, :, 64:128], pv[:, :, 0:64], sinv, ALU.mult), inc="dv")
                    kb.emit("vector", lambda e, qcd=qcd, t1v=t1v, t2v=t2v: e.tensor_tensor(qcd[:, :, 0:64], t1v[:, :, 0:64], t2v[:, :, 0:64], ALU.subtract), inc="dv")
                    r5 = kb.emit("vector", lambda e, qcd=qcd, t1v=t1v, t2v=t2v: e.tensor_tensor(qcd[:, :, 64:128], t1v[:, :, 64:128], t2v[:, :, 64:128], ALU.add), inc="dv")
                    BG[q * 2] = r5
                    BG[q * 2 + 1] = r5
                    ST["ropescr"] = r5
            for dtb in range(4):
                s2 = dtb % 2
                ctoks = []
                for dtl in range(4):
                    dt = dtb * 4 + dtl
                    t1 = w_feed(wk_h.ap()[dt * 128:(dt + 1) * 128, :],
                                (s2 * 4096 + dtl * 1024, s2 * 4096 + dtl * 1024 + 512), 512)
                    t2 = w_feed(wv_h.ap()[dt * 128:(dt + 1) * 128, :],
                                (s2 * 4096 + dtl * 1024 + 512, s2 * 4096 + (dtl + 1) * 1024), 512)
                    ctoks.append(t2)
                for dtl in range(4):
                    dt = dtb * 4 + dtl
                    for q in range(GQ):
                        for kvh in range(2):
                            tok = kb.emit("tensor", lambda e, dt=dt, dtl=dtl, q=q, kvh=kvh, s2=s2, q0=q0: e.matmul(
                                bank(q * 2 + kvh),
                                xnt[:, dt * Th + (q0 + q) * 128: dt * Th + (q0 + q + 1) * 128],
                                wstr[:, s2 * 4096 + dtl * 1024 + kvh * 512: s2 * 4096 + dtl * 1024 + (kvh + 1) * 512],
                                start=(dt == 0), stop=(dt == 15)),
                                deps=[ctoks[dtl]] + ([BG.get(q * 2 + kvh)] if dt == 0 else []),
                                inc="pe")
                WL[s2] = tok
            for q in range(GQ):
                lq = q0 + q
                gq = half * HQT + lq
                pkv = bank(q * 2).rearrange("p (h x) -> p h x", h=HKV)
                pk4 = bank(q * 2).rearrange("p (h t x) -> p h t x", h=HKV, t=2)
                cosv = cos_sb[:, gq * 64:(gq + 1) * 64][:, None, None, :].broadcast_to([128, HKV, 2, 64])
                sinv = sin_sb[:, gq * 64:(gq + 1) * 64][:, None, :].broadcast_to([128, HKV, 64])
                t1v = xf[:, 0:512].rearrange("p (h x) -> p h x", h=HKV)
                t1v4 = xf[:, 0:512].rearrange("p (h t x) -> p h t x", h=HKV, t=2)
                t2v = xf[:, 1024:1536].rearrange("p (h x) -> p h x", h=HKV)
                kdst = sl[:, 0:512].rearrange("p (h x) -> p h x", h=HKV)
                kb.emit("vector", lambda e, pk4=pk4, cosv=cosv, t1v4=t1v4: e.tensor_tensor(t1v4, pk4, cosv, ALU.mult), deps=[tok] + list(XGRD()), inc="dv")
                kb.emit("vector", lambda e, pkv=pkv, sinv=sinv, t2v=t2v: e.tensor_tensor(t2v[:, :, 0:64], pkv[:, :, 64:128], sinv, ALU.mult), inc="dv")
                r3 = kb.emit("vector", lambda e, pkv=pkv, sinv=sinv, t2v=t2v: e.tensor_tensor(t2v[:, :, 64:128], pkv[:, :, 0:64], sinv, ALU.mult), inc="dv")
                kb.emit("vector", lambda e, kdst=kdst, t1v=t1v, t2v=t2v: e.tensor_tensor(kdst[:, :, 0:64], t1v[:, :, 0:64], t2v[:, :, 0:64], ALU.subtract), deps=[ST.get("ktm")], inc="dv")
                r5 = kb.emit("vector", lambda e, kdst=kdst, t1v=t1v, t2v=t2v: e.tensor_tensor(kdst[:, :, 64:128], t1v[:, :, 64:128], t2v[:, :, 64:128], ALU.add), inc="dv")
                ST["ropescr"] = r5
                vdst = vb[:, gq * HKV * 132: (gq + 1) * HKV * 132].rearrange("p (h x) -> p h x", x=132)[:, :, 0:128]
                tvv = kb.emit("scalar", lambda e, vdst=vdst, q=q: e.copy(vdst, bank(q * 2 + 1).rearrange("p (h x) -> p h x", h=HKV)), deps=[tok], inc="ac")
                BG[q * 2 + 1] = tvv
                ST["vbw"] = tvv
                last_mm = None
                for kv in range(HKV):
                    last_mm = kb.emit("tensor", lambda e, kv=kv, q=q: e.matmul(
                        bank(q * 2)[:, kv * 128:(kv + 1) * 128],
                        sl[:, kv * 128:(kv + 1) * 128], ident[:, :],
                        start=True, stop=True), deps=[r5, r3], inc="pe")

                def evk(e, gq=gq, q=q):
                    o = kt.rearrange("p (k t) -> p k t", k=HKV)[:, :, gq * 128:(gq + 1) * 128]
                    return e.tensor_copy(o, bank(q * 2).rearrange("p (k c) -> p k c", k=HKV))
                tke = kb.emit("vector", evk, deps=[last_mm], inc="dv")
                BG[q * 2] = tke
                ST["ktm"] = tke
        ST[("bdone", half)] = ST["ktm"]

    def c_phase(half):
        e_gate = {}
        scores_last = None
        bdone = ST[("bdone", half)]
        vbw = ST["vbw"]
        for head in range(HQ):
            kv = head % HKV
            ev_last = None
            for lq0 in range(0, HQT, 4):
                n = min(4, HQT - lq0)
                pb = 6 + (lq0 // 4) % 2
                last_mm = None
                for i in range(n):
                    lq = lq0 + i
                    last_mm = kb.emit("tensor", lambda e, lq=lq, head=head, pb=pb, i=i: e.matmul(
                        bank(pb)[:, i * 128:(i + 1) * 128],
                        big[:, lq * 2048 + head * 128: lq * 2048 + (head + 1) * 128], ident[:, :],
                        start=True, stop=True),
                        deps=[BG.get(pb), ST.get(("ctx", head - 1, lq)), bdone, TAB], inc="pe")

                def evq(e, pb=pb, lq0=lq0, n=n):
                    return e.tensor_copy(big[:, QTT + lq0 * 128: QTT + (lq0 + n) * 128], bank(pb)[:, 0:n * 128])
                ev_last = kb.emit("vector", evq, deps=[last_mm, scores_last], inc="dv")
                BG[pb] = ev_last
            for qc in range(NQC):
                gqc = half * NQC + qc
                nkt = (gqc + 1) * QCW // 128
                kt0_cross = gqc * QCW // 128
                for ktile in range(nkt):
                    pb = ktile % 4
                    tmm = kb.emit("tensor", lambda e, ktile=ktile, kv=kv, qc=qc, pb=pb: e.matmul(
                        bank(pb, QCW),
                        kt[:, kv * T + ktile * 128: kv * T + (ktile + 1) * 128],
                        big[:, QTT + qc * QCW: QTT + (qc + 1) * QCW],
                        start=True, stop=True),
                        deps=[ev_last, BG.get(pb)], inc="pe")
                    scores_last = tmm
                    tex = kb.emit("scalar", lambda e, ktile=ktile, pb=pb: e.activation(
                        big[:, EOFF + ktile * QCW: EOFF + (ktile + 1) * QCW], bank(pb, QCW), AF.Exp, scale=SM_SCALE),
                        deps=[tmm, e_gate.get(("e", ktile))], inc="ac")
                    BG[pb] = tex
                    etok = tex
                    if ktile >= kt0_cross:
                        o_rel = (ktile - kt0_cross) * 128
                        etok = kb.emit("vector", lambda e, ktile=ktile, o_rel=o_rel: e.tensor_tensor(
                            big[:, EOFF + ktile * QCW: EOFF + (ktile + 1) * QCW],
                            big[:, EOFF + ktile * QCW: EOFF + (ktile + 1) * QCW],
                            maskc[:, 384 - o_rel: 384 - o_rel + QCW], ALU.mult), deps=[tex, TAB], inc="dv")
                    e_gate[("m", ktile)] = etok
                for ql in range(QCW // 128):
                    lq = qc * (QCW // 128) + ql
                    gq = half * HQT + lq
                    cb = 4 + ql % 2
                    for ktile in range(gq + 1):
                        tok = kb.emit("tensor", lambda e, ktile=ktile, ql=ql, kv=kv, gq=gq, cb=cb, qc=qc: e.matmul(
                            bank(cb)[:, 0:129],
                            big[:, EOFF + ktile * QCW + ql * 128: EOFF + ktile * QCW + (ql + 1) * 128],
                            vb[:, ktile * HKV * 132 + kv * 132: ktile * HKV * 132 + kv * 132 + 129],
                            start=(ktile == 0), stop=(ktile == gq)),
                            deps=[e_gate.get(("m", ktile))] + ([BG.get(cb), vbw] if ktile == 0 else []),
                            inc="pe")
                    kb.emit("vector", lambda e, cb=cb, ql=ql: e.reciprocal(rss[:, ql:ql + 1], bank(cb)[:, 128:129]), deps=[tok], inc="dv")
                    tcx = kb.emit("vector", lambda e, cb=cb, ql=ql, lq=lq, head=head: e.tensor_scalar(
                        big[:, lq * 2048 + head * 128: lq * 2048 + (head + 1) * 128],
                        bank(cb)[:, 0:128], rss[:, ql:ql + 1], None, ALU.mult), inc="dv")
                    BG[cb] = tcx
                    ST[("ctx", head, lq)] = tcx
                lastq = qc * (QCW // 128) + (QCW // 128) - 1
                for ktile in range(nkt):
                    e_gate[("e", ktile)] = ST[("ctx", head, lastq)]

    def d_phase(half):
        for g in range(NG):
            q0 = g * GQ
            ct_done = []
            for q in range(GQ):
                lq = q0 + q
                for dt in range(16):
                    pb = 6 + (dt // 4) % 2
                    tmm = kb.emit("tensor", lambda e, lq=lq, dt=dt, pb=pb: e.matmul(
                        bank(pb)[:, (dt % 4) * 128:(dt % 4) * 128 + 128],
                        big[:, lq * 2048 + dt * 128: lq * 2048 + (dt + 1) * 128], ident[:, :],
                        start=True, stop=True),
                        deps=[BG.get(pb), ST.get(("ctx", HQ - 1, lq)), TAB], inc="pe")
                    if dt % 4 == 3:
                        dt0 = dt - 3

                        def evc(e, pb=pb, dt0=dt0, q=q):
                            return e.tensor_copy(big[:, EOFF + q * 2048 + dt0 * 128: EOFF + q * 2048 + (dt0 + 4) * 128], bank(pb))
                        tv = kb.emit("vector", evc, deps=[tmm, ST.get(("ctc", q))], inc="dv")
                        BG[pb] = tv
                        ct_done.append(tv)
            for dc in range(2):
                for dtb in range(4):
                    s2 = (dc * 4 + dtb) % 2
                    ctoks = []
                    for dtl in range(4):
                        dt = dtb * 4 + dtl
                        ctoks.append(w_feed(wo_h.ap()[dt * 128:(dt + 1) * 128, dc * 1024:(dc + 1) * 1024],
                                            (s2 * 4096 + dtl * 1024, s2 * 4096 + (dtl + 1) * 1024), 1024))
                    for dtl in range(4):
                        dt = dtb * 4 + dtl
                        for q in range(GQ):
                            for hb in range(2):
                                tok = kb.emit("tensor", lambda e, dt=dt, dtl=dtl, q=q, hb=hb, s2=s2: e.matmul(
                                    bank(q * 2 + hb),
                                    big[:, EOFF + q * 2048 + dt * 128: EOFF + q * 2048 + (dt + 1) * 128],
                                    wstr[:, s2 * 4096 + dtl * 1024 + hb * 512: s2 * 4096 + dtl * 1024 + (hb + 1) * 512],
                                    start=(dt == 0), stop=(dt == 15)),
                                    deps=[ctoks[dtl]] + (ct_done + [BG.get(q * 2 + hb)] if dt == 0 else []),
                                    inc="pe")
                    WL[s2] = tok
                for q in range(GQ):
                    gq = half * HQT + q0 + q
                    s2 = q % 2
                    o2 = outs[:, s2 * 1024:(s2 + 1) * 1024]
                    tv = kb.emit("vector", lambda e, q=q, o2=o2: e.tensor_copy(o2, ps[:, q * 1024:(q + 1) * 1024]),
                                 deps=[tok, ST.get(("og", s2))], inc="dv")
                    BG[q * 2] = tv
                    BG[q * 2 + 1] = tv
                    if dc == 1:
                        ST[("ctc", q)] = tv

                    def sta(e, gq=gq, dc=dc, o2=o2):
                        return e.dma_start(out=att_h.ap()[gq * 128:(gq + 1) * 128, dc * 1024:(dc + 1) * 1024], in_=o2)
                    ST[("og", s2)] = kb.emit("sync", sta, deps=[tv], inc="fs%d" % s2, dma=True)

    def f_phase(half):
        WSD = ("cc", 1)
        WSDP = ("cc", 2)
        FSD = (("fs0", kb.cnt.get("fs0", 0)), ("fs1", kb.cnt.get("fs1", 0)))
        for tb in range(NTB):
            q0 = tb * TBQ
            ev_gate = {}
            h_tok = {}
            tok = None
            for j in range(NJ):
                for gu in range(2):
                    p2 = (2 * j + gu) % 2
                    sA, sB = 2 * p2, 2 * p2 + 1
                    hcol = gu * HF + j * 512

                    def ldw1(e, hcol=hcol, sA=sA):
                        return e.dma_start(out=wstr[:, sA * 4096:(sA + 1) * 4096].rearrange("p (dt c) -> p dt c", dt=8),
                                           in_=wtg_h.ap()[0:1024, hcol:hcol + 512].rearrange("(dt p) c -> p dt c", p=128))

                    def ldw2(e, hcol=hcol, sB=sB):
                        return e.dma_start(out=wstr[:, sB * 4096:(sB + 1) * 4096].rearrange("p (dt c) -> p dt c", dt=8),
                                           in_=wtg_h.ap()[1024:2048, hcol:hcol + 512].rearrange("(dt p) c -> p dt c", p=128))
                    wgA = kb.emit("sync", ldw1, deps=(WL.get(sA), WSD) + WGRD(), inc="wg%d" % sA, dma=True)
                    wgB = kb.emit("scalar", ldw2, deps=(WL.get(sB), WSD) + WGRD(), inc="wg%d" % sB, dma=True)
                    for dt in range(16):
                        sl_i = sA if dt < 8 else sB
                        wt = wgA if dt < 8 else wgB
                        for q in range(TBQ):
                            bnk = q * 2 + gu
                            tok = kb.emit("tensor", lambda e, dt=dt, q=q, bnk=bnk, sl_i=sl_i, q0=q0: e.matmul(
                                bank(bnk),
                                xnt[:, dt * Th + (q0 + q) * 128: dt * Th + (q0 + q + 1) * 128],
                                wstr[:, sl_i * 4096 + (dt % 8) * 512: sl_i * 4096 + (dt % 8 + 1) * 512],
                                start=(dt == 0), stop=(dt == 15)),
                                deps=[wt] + ([BG.get(bnk), ST.get(("xnq", half, q0 + q))] if dt == 0 else []),
                                inc="pe")
                        if dt == 7:
                            WL[sA] = tok
                    WL[sB] = tok
                for q in range(TBQ):
                    tsl = kb.emit("scalar", lambda e, q=q: e.activation(sl[:, q * 512:(q + 1) * 512], bank(q * 2), AF.Silu), deps=[tok, ev_gate.get(("sl", q))], inc="ac")
                    th = kb.emit("vector", lambda e, q=q, j=j: e.tensor_tensor(
                        big[:, q * HF + j * 512: q * HF + (j + 1) * 512],
                        sl[:, q * 512:(q + 1) * 512], bank(q * 2 + 1), ALU.mult), deps=[tsl, tok], inc="dv")
                    BG[q * 2] = tsl
                    BG[q * 2 + 1] = th
                    ev_gate[("sl", q)] = th
                    h_tok[(q, j)] = th
            hnt_done = []
            for q in range(TBQ):
                hq = big[:, q * HF:(q + 1) * HF]
                deps_h = [h_tok[(q, j)] for j in range(NJ)]
                tsq = kb.emit("scalar", lambda e, hq=hq, q=q: e.activation(wstr[:, 0:HF], hq, AF.Square, accum_out=fss[:, q:q + 1]),
                              deps=deps_h + [ST.get("fjunk")] + [WL.get(i) for i in range(4) if i * 4096 < HF], inc="ac")
                ST["fjunk"] = tsq
                for i in range(4):
                    if i * 4096 < HF:
                        WL[i] = tsq
                tms = kb.emit("vector", lambda e, q=q: e.tensor_scalar(fss[:, 4 + q:5 + q], fss[:, q:q + 1], 1.0 / HF, RMS_EPS, ALU.mult, ALU.add), deps=[tsq], inc="dv")
                kb.emit("scalar", lambda e, q=q: e.activation(fss[:, 4 + q:5 + q], fss[:, 4 + q:5 + q], AF.Ln), deps=[tms], inc="ac")
                ti = kb.emit("scalar", lambda e, q=q: e.activation(fss[:, 8 + q:9 + q], fss[:, 4 + q:5 + q], AF.Exp, scale=-0.5), inc="ac")
                thn = kb.emit("vector", lambda e, hq=hq, q=q: e.tensor_scalar(hq, hq, fss[:, 8 + q:9 + q], None, ALU.mult), deps=[ti], inc="dv")
                for ht in range(NHT):
                    pb = 6 + (ht // 4) % 2
                    tmm = kb.emit("tensor", lambda e, hq=hq, ht=ht, pb=pb: e.matmul(
                        bank(pb)[:, (ht % 4) * 128:(ht % 4) * 128 + 128],
                        hq[:, ht * 128:(ht + 1) * 128], ident[:, :],
                        start=True, stop=True),
                        deps=[thn, BG.get(pb)], inc="pe")
                    if ht % 4 == 3:
                        ht0 = ht - 3

                        def evh(e, pb=pb, ht0=ht0, hq=hq):
                            return e.tensor_copy(hq[:, ht0 * 128:(ht0 + 4) * 128], bank(pb))
                        tv = kb.emit("vector", evh, deps=[tmm], inc="dv")
                        BG[pb] = tv
                        hnt_done.append(tv)
            xtok = {}
            attok = {}
            NHG = NHT // 4
            for dc in range(2):
                for q in range(min(2, TBQ)):
                    gq = half * HQT + q0 + q

                    def ldx2(e, gq=gq, q=q, dc=dc):
                        return e.dma_start(out=xf[:, (q % 2) * 1024:(q % 2 + 1) * 1024],
                                           in_=x_h.ap()[gq * 128:(gq + 1) * 128, dc * 1024:(dc + 1) * 1024])

                    def lda(e, gq=gq, q=q, dc=dc):
                        return e.dma_start(out=xf[:, 2048 + (q % 2) * 1024: 2048 + (q % 2 + 1) * 1024],
                                           in_=att_h.ap()[gq * 128:(gq + 1) * 128, dc * 1024:(dc + 1) * 1024])
                    xtok[q] = kb.emit("sync", ldx2, deps=(ST.get("ropescr"), ST.get(("xr", q % 2))), inc="x%d" % (q % 2), dma=True)
                    attok[q] = kb.emit("sync", lda, deps=(ST.get(("ar", q % 2)), ST.get(("cst", 0)), ST.get(("cst", 1))) + FSD, inc="ab%d" % (q % 2), dma=True)
                for hg in range(NHG):
                    s4 = (dc * NHG + hg) % 4

                    def ldp2(e, hg=hg, dc=dc, s4=s4):
                        return e.dma_start(out=wstr[:, s4 * 4096:(s4 + 1) * 4096].rearrange("p (ht c) -> p ht c", ht=4),
                                           in_=wtp_h.ap()[hg * 512:(hg + 1) * 512, dc * 1024:(dc + 1) * 1024].rearrange("(ht p) c -> p ht c", p=128))
                    wpt = kb.emit("sync" if s4 % 2 == 0 else "scalar", ldp2, deps=(WL.get(s4), WSDP) + WGRD(), inc="wp%d" % s4, dma=True)
                    for htl in range(4):
                        ht = hg * 4 + htl
                        for q in range(TBQ):
                            for hb in range(2):
                                tok = kb.emit("tensor", lambda e, htl=htl, ht=ht, q=q, hb=hb, s4=s4: e.matmul(
                                    bank(q * 2 + hb),
                                    big[:, q * HF + ht * 128: q * HF + (ht + 1) * 128],
                                    wstr[:, s4 * 4096 + htl * 1024 + hb * 512: s4 * 4096 + htl * 1024 + (hb + 1) * 512],
                                    start=(ht == 0), stop=(ht == NHT - 1)),
                                    deps=[wpt] + (hnt_done + [BG.get(q * 2 + hb)] if ht == 0 else []),
                                    inc="pe")
                    WL[s4] = tok
                for q in range(TBQ):
                    gq = half * HQT + q0 + q
                    s2 = q % 2
                    o2 = outs[:, s2 * 1024:(s2 + 1) * 1024]
                    ta = kb.emit("vector", lambda e, s2=s2, q=q, o2=o2: e.tensor_tensor(
                        o2, ps[:, q * 1024:(q + 1) * 1024],
                        xf[:, s2 * 1024:(s2 + 1) * 1024], ALU.add),
                        deps=[tok, xtok[q], ST.get(("og", s2))], inc="dv")
                    tb2 = kb.emit("vector", lambda e, s2=s2, o2=o2: e.tensor_tensor(
                        o2, o2, xf[:, 2048 + s2 * 1024: 2048 + (s2 + 1) * 1024], ALU.add),
                        deps=[attok[q]], inc="dv")
                    BG[q * 2] = tb2
                    BG[q * 2 + 1] = tb2
                    ST[("xr", s2)] = ta
                    ST[("ar", s2)] = tb2

                    def sto(e, gq=gq, dc=dc, o2=o2):
                        return e.dma_start(out=out_h.ap()[gq * 128:(gq + 1) * 128, dc * 1024:(dc + 1) * 1024], in_=o2)
                    ST[("og", s2)] = kb.emit("gpsimd", sto, deps=[tb2], inc="os%d" % s2, dma=True)
                    if q + 2 < TBQ:
                        gq2 = half * HQT + q0 + q + 2

                        def ldx3(e, gq2=gq2, s2=s2, dc=dc):
                            return e.dma_start(out=xf[:, s2 * 1024:(s2 + 1) * 1024],
                                               in_=x_h.ap()[gq2 * 128:(gq2 + 1) * 128, dc * 1024:(dc + 1) * 1024])

                        def lda3(e, gq2=gq2, s2=s2, dc=dc):
                            return e.dma_start(out=xf[:, 2048 + s2 * 1024: 2048 + (s2 + 1) * 1024],
                                               in_=att_h.ap()[gq2 * 128:(gq2 + 1) * 128, dc * 1024:(dc + 1) * 1024])
                        xtok[q + 2] = kb.emit("sync", ldx3, deps=(ta,), inc="x%d" % s2, dma=True)
                        attok[q + 2] = kb.emit("sync", lda3, deps=(tb2,) + FSD, inc="ab%d" % s2, dma=True)

    per = max(1, (NWB + HQT - 1) // HQT)
    RG = [[0, 1, 2, 3, 4, 5, 6, 7]]
    kb.sem("cc")
    kb.dma_sems.add("cc")
    wi = 0
    for g in range(NG):
        for lq in range(g * GQ, (g + 1) * GQ):
            for _ in range(per):
                if wi < NWB:
                    w_batch_emit(wi)
                    wi += 1
                    if wi == NBG:
                        kb.emit("gpsimd", lambda e: e.collective_compute("AllGather", ALU.bypass, RG, [shg_h.ap().opt()], [wtg_h.ap().opt()]), deps=[("ws", kb.cnt["ws"])], inc="cc")
            a_qtile(0, lq)
        if g == NG - 1:
            while wi < NWB:
                w_batch_emit(wi)
                wi += 1
        b_phase(0, gsel=g)
    kb.emit("gpsimd", lambda e: e.collective_compute("AllGather", ALU.bypass, RG, [shp_h.ap().opt()], [wtp_h.ap().opt()]), deps=[("ws", kb.cnt["ws"])], inc="cc")
    c_phase(0)
    d_phase(0)
    f_phase(0)
    kb.block()
    for lq in range(HQT):
        a_qtile(1, lq)
    b_phase(1)
    c_phase(1)
    d_phase(1)
    f_phase(1)
    kb.block()
    kb.emit("sync", lambda e: e.nop(), deps=[("os0", kb.cnt["os0"]), ("os1", kb.cnt.get("os1", 0)),
                                             ("fs0", kb.cnt.get("fs0", 0)), ("fs1", kb.cnt.get("fs1", 0)),
                                             ("cc", 2), ("ws", kb.cnt["ws"])])
    kb.realize()
    return nc


_NC_CACHE = {}


def _get_nc(T, HF):
    key = (T, HF)
    if key not in _NC_CACHE:
        _NC_CACHE[key] = build_nc(T, HF)
    return _NC_CACHE[key]


def _host_tables(T):
    bf = ml_dtypes.bfloat16
    inv = 1.0 / (10000.0 ** (np.arange(0, DH, 2, dtype=np.float64) / DH))
    ang = np.arange(T, dtype=np.float64)[:, None] * inv[None, :]
    cos = np.cos(ang).astype(np.float32)
    sin = np.sin(ang).astype(np.float32)
    costab = np.ascontiguousarray(cos)
    sintab = np.ascontiguousarray(sin)
    ident = np.eye(128, dtype=bf)
    k_i = np.arange(128)[:, None]
    t_i = np.arange(896)[None, :]
    maskc = ((t_i - 384) >= k_i).astype(bf)
    return costab, sintab, ident, maskc


def _deinterleave_cols(w, block=128):
    d_out = w.shape[1]
    perm = []
    for b0 in range(0, d_out, block):
        perm.extend(range(b0, b0 + block, 2))
        perm.extend(range(b0 + 1, b0 + block, 2))
    return np.ascontiguousarray(w[:, perm])


def _permute_wo_rows(wo):
    rows = []
    for h in range(HQ):
        src = 4 * (h % 4) + h // 4
        rows.extend(range(src * 128, (src + 1) * 128))
    return np.ascontiguousarray(wo[rows, :])


def kernel(L0_attn_x, L0_attn_Wq, L0_attn_Wk, L0_attn_Wv, L0_attn_Wo,
           L0_ffn_w_gate_up, L0_ffn_w_proj):
    x = np.ascontiguousarray(np.asarray(L0_attn_x, dtype=np.float32))
    wq = _deinterleave_cols(np.asarray(L0_attn_Wq, dtype=np.float32))
    wk = _deinterleave_cols(np.asarray(L0_attn_Wk, dtype=np.float32))
    wv = np.ascontiguousarray(np.asarray(L0_attn_Wv, dtype=np.float32))
    wo = _permute_wo_rows(np.asarray(L0_attn_Wo, dtype=np.float32))
    wgu = np.ascontiguousarray(np.asarray(L0_ffn_w_gate_up, dtype=np.float32))
    wpj = np.ascontiguousarray(np.asarray(L0_ffn_w_proj, dtype=np.float32))
    B, T, _ = x.shape
    HF = wpj.shape[1]
    costab, sintab, ident, maskc = _host_tables(T)
    nc = _get_nc(T, HF)
    swg, swp = D // 8, HF // 8
    in_maps = [dict(x=x[b], wq=wq, wk=wk, wv=wv, wo=wo,
                    wgu_s=np.ascontiguousarray(wgu[:, b * swg:(b + 1) * swg]),
                    wproj_s=np.ascontiguousarray(wpj[:, b * swp:(b + 1) * swp]),
                    costab=costab, sintab=sintab, ident=ident, maskc=maskc)
               for b in range(B)]
    res = run_bass_kernel_spmd(nc, in_maps, core_ids=list(range(B)))
    return np.stack([np.asarray(res.results[b]["out"]) for b in range(B)], axis=0)


# version 13
# speedup vs baseline: 1.0421x; 1.0421x over previous
import math
import numpy as np
import ml_dtypes

import concourse.bass as bass
import concourse.mybir as mybir
import concourse.bass_utils as _bu
from concourse.bass_utils import run_bass_kernel_spmd

if not getattr(_bu, "_ldw_patched", False):
    _orig_run_command = _bu.run_command

    def _run_command_ldw(argv, **kw):
        argv = [a.replace("--enable-ldw-opt=false", "--enable-ldw-opt=true")
                if isinstance(a, str) else a for a in argv]
        return _orig_run_command(argv, **kw)

    _bu.run_command = _run_command_ldw
    _bu._ldw_patched = True

F32 = mybir.dt.float32
BF16 = mybir.dt.bfloat16
AF = mybir.ActivationFunctionType
ALU = mybir.AluOpType
AX = mybir.AxisListType

D = 2048
HQ, HKV, DH = 16, 4, 128
KVW = HKV * DH
LN_EPS = 1e-5
RMS_EPS = float(np.finfo(np.float32).eps)
SM_SCALE = 1.0 / math.sqrt(DH)

ENGS = ("tensor", "vector", "scalar", "gpsimd", "sync")
CHAIN = ("vector", "scalar")


class KB:

    def __init__(self, nc):
        self.nc = nc
        self.blocks = []
        self.cur = None
        self.sems = {}
        self.cnt = {}
        self.hw = None
        self.block_start = None
        self.dma_sems = set()
        self.last = {}

    def sem(self, name, dma=False):
        if name not in self.sems:
            self.sems[name] = self.nc.alloc_semaphore(name)
            self.cnt[name] = 0
            if dma:
                self.dma_sems.add(name)
        return self.sems[name]

    def block(self):
        self.cur = {e: [] for e in ENGS}
        self.blocks.append(self.cur)
        self.hw = {}
        self.block_start = {s: self.cnt[s] for s in self.cnt if s not in self.dma_sems}

    def emit(self, eng, fn, deps=(), inc=None, dma=False):
        if dma and inc is not None:
            self.sem(inc, dma=True)
            deps = tuple(deps) + ((inc, self.cnt[inc]),)
        if eng in CHAIN and not dma:
            deps = tuple(deps) + (self.last.get(eng),)
            assert inc is not None
        if inc is not None:
            self.sem(inc, dma=dma)
        waits = []
        for d in deps:
            if d is None:
                continue
            s, v = d
            self.sem(s)
            if v <= 0 or v <= self.block_start.get(s, 0):
                continue
            k = (eng, s)
            if self.hw.get(k, 0) >= v:
                continue
            self.hw[k] = v
            waits.append((s, v))
        tok = None
        if inc is not None:
            self.cnt[inc] += 16 if dma else 1
            tok = (inc, self.cnt[inc])
            if eng in CHAIN:
                self.last[eng] = tok
        self.cur[eng].append((waits, fn, inc, 16 if dma else 1))
        return tok

    def realize(self):
        for blk in self.blocks:
            with self.nc.Block() as block:
                for eng in ENGS:
                    items = blk[eng]
                    if not items:
                        continue

                    def make_body(items=items):
                        def body(e):
                            for waits, fn, inc, by in items:
                                for (s, v) in waits:
                                    e.wait_ge(self.sems[s], v)
                                ins = fn(e)
                                if inc is not None:
                                    ins.then_inc(self.sems[inc], by)
                        return body

                    getattr(block, eng)(make_body())


def build_nc(T=2048, HF=8192):
    H2 = 2 * HF
    NQT = T // 128
    HQT = NQT // 2
    Th = T // 2
    TBQ = min(4, HQT)
    NTB = HQT // TBQ
    GQ = min(4, HQT)
    NG = HQT // GQ
    QCW = min(512, Th)
    NQC = Th // QCW
    NHT = HF // 128
    NJ = HF // 512
    SWG = D // 8
    SWP = HF // 8

    nc = bass.Bass("TRN2", target_bir_lowering=False)

    x_h = nc.dram_tensor("x", [T, D], F32, kind="ExternalInput")
    wq_h = nc.dram_tensor("wq", [D, D], F32, kind="ExternalInput")
    wk_h = nc.dram_tensor("wk", [D, KVW], F32, kind="ExternalInput")
    wv_h = nc.dram_tensor("wv", [D, KVW], F32, kind="ExternalInput")
    wo_h = nc.dram_tensor("wo", [D, D], F32, kind="ExternalInput")
    wgu_h = nc.dram_tensor("wgu_s", [H2, SWG], F32, kind="ExternalInput")
    wpj_h = nc.dram_tensor("wproj_s", [D, SWP], F32, kind="ExternalInput")
    cos_h = nc.dram_tensor("costab", [T, 64], F32, kind="ExternalInput")
    sin_h = nc.dram_tensor("sintab", [T, 64], F32, kind="ExternalInput")
    id_h = nc.dram_tensor("ident", [128, 128], BF16, kind="ExternalInput")
    mk_h = nc.dram_tensor("maskc", [128, 896], BF16, kind="ExternalInput")
    out_h = nc.dram_tensor("out", [T, D], F32, kind="ExternalOutput")

    shg_h = nc.dram_tensor("shg", [SWG, H2], BF16, kind="Internal")
    shp_h = nc.dram_tensor("shp", [SWP, D], BF16, kind="Internal")
    wtg_h = nc.dram_tensor("wtg", [D, H2], BF16, kind="Internal")
    wtp_h = nc.dram_tensor("wtp", [HF, D], BF16, kind="Internal")
    att_h = nc.dram_tensor("attb", [T, D], F32, kind="Internal")

    kt = nc.alloc_sbuf_tensor("kt", [128, HKV * T], BF16)
    vb = nc.alloc_sbuf_tensor("vb", [128, NQT * HKV * 132], BF16)
    cos_sb = nc.alloc_sbuf_tensor("cos_sb", [128, NQT * 64], F32)
    sin_sb = nc.alloc_sbuf_tensor("sin_sb", [128, NQT * 64], F32)
    ident = nc.alloc_sbuf_tensor("ident_sb", [128, 128], BF16)
    maskc = nc.alloc_sbuf_tensor("maskc_sb", [128, 896], BF16)
    xnt = nc.alloc_sbuf_tensor("xnt", [128, 16 * Th], BF16)
    big = nc.alloc_sbuf_tensor("big", [128, 32768], BF16)
    wstr = nc.alloc_sbuf_tensor("wstr", [128, 16384], BF16)
    xf = nc.alloc_sbuf_tensor("xf", [128, 4096], F32)
    outs = nc.alloc_sbuf_tensor("outs", [128, 2048], F32)
    sl = nc.alloc_sbuf_tensor("sl", [128, 2048], BF16)
    lns = nc.alloc_sbuf_tensor("lns", [128, 16], F32)
    fss = nc.alloc_sbuf_tensor("fss", [128, 16], F32)
    qsf = nc.alloc_sbuf_tensor("qsf", [128, 64], F32)
    qsb = nc.alloc_sbuf_tensor("qsb", [128, 64], BF16)
    qtb = nc.alloc_sbuf_tensor("qtb", [128, 64], BF16)
    qtn = nc.alloc_sbuf_tensor("qtn", [128, 64], BF16)
    rss = nc.alloc_sbuf_tensor("rss", [128, 8], F32)

    ps = nc.alloc_psum_tensor("ps", [128, 4096], F32)

    def bank(i, w=512):
        return ps[:, i * 512:i * 512 + w]

    kb = KB(nc)

    WSEMS = ("wg0", "wg1", "wg2", "wg3", "wp0", "wp1", "wp2", "wp3",
             "aw0", "aw1", "wo0", "wo1", "qi0", "qi1")

    def WGRD():
        return tuple((n, kb.cnt.get(n, 0)) for n in WSEMS)

    def XGRD():
        return (("x0", kb.cnt.get("x0", 0)), ("x1", kb.cnt.get("x1", 0)),
                ("ab0", kb.cnt.get("ab0", 0)), ("ab1", kb.cnt.get("ab1", 0)))

    EOFF = 16384
    QTT = 24576
    JOFF = QTT + Th
    XOFF = JOFF + 2048
    assert XOFF + 2048 <= 32768
    WQI = (8192, 10240)
    WD1, WD2 = 12288, 14336

    BG = {}
    ST = {}
    WL = {}

    kb.block()
    kb.emit("sync", lambda e: e.dma_start(out=cos_sb[:, :], in_=cos_h.ap().rearrange("(q p) c -> p q c", p=128)), inc="ld0", dma=True)
    kb.emit("sync", lambda e: e.dma_start(out=sin_sb[:, :], in_=sin_h.ap().rearrange("(q p) c -> p q c", p=128)), inc="ld0", dma=True)
    kb.emit("sync", lambda e: e.dma_start(out=ident[:, :], in_=id_h.ap()), inc="ld0", dma=True)
    kb.emit("sync", lambda e: e.dma_start(out=maskc[:, :], in_=mk_h.ap()), inc="ld0", dma=True)
    TAB = ("ld0", 64)
    kb.emit("vector", lambda e: e.memset(vb.rearrange("p (q k c) -> p q k c", k=HKV, c=132)[:, :, :, 128:132], 1.0), inc="dv")

    def quant_chunk(src_ap, in_tok):
        ng = 32
        v3 = src_ap.rearrange("p (g c) -> p g c", c=64)
        kb.emit("vector", lambda e: e.tensor_reduce(qsf[:, 0:ng], v3, AX.X, ALU.add, apply_absolute_value=True), deps=[in_tok], inc="dv")
        kb.emit("vector", lambda e: e.tensor_scalar(qsf[:, 0:ng], qsf[:, 0:ng], 1.0 / 64.0, 1e-8, ALU.mult, ALU.max), inc="dv")
        kb.emit("vector", lambda e: e.tensor_copy(qsb[:, 0:ng], qsf[:, 0:ng]), inc="dv")
        kb.emit("vector", lambda e: e.tensor_scalar_mul(qtb[:, 0:ng], qsb[:, 0:ng], 0.5), inc="dv")
        kb.emit("vector", lambda e: e.tensor_scalar_mul(qtn[:, 0:ng], qtb[:, 0:ng], -1.0), inc="dv")
        d1 = wstr[:, WD1:WD1 + 2048]
        d2 = wstr[:, WD2:WD2 + 2048]
        tpb = qtb[:, 0:ng][:, :, None].broadcast_to([128, ng, 64])
        tnb = qtn[:, 0:ng][:, :, None].broadcast_to([128, ng, 64])
        sbv = qsb[:, 0:ng][:, :, None].broadcast_to([128, ng, 64])
        kb.emit("vector", lambda e: e.tensor_tensor(d1.rearrange("p (g c) -> p g c", c=64), v3, tpb, ALU.is_gt), deps=[ST.get("wstg")], inc="dv")
        kb.emit("vector", lambda e: e.tensor_tensor(d2.rearrange("p (g c) -> p g c", c=64), v3, tnb, ALU.is_lt), inc="dv")
        kb.emit("vector", lambda e: e.tensor_tensor(d1, d1, d2, ALU.subtract), inc="dv")
        te = kb.emit("vector", lambda e: e.tensor_tensor(v3, d1.rearrange("p (g c) -> p g c", c=64), sbv, ALU.mult), inc="dv")
        return te

    NBG = H2 // 1024
    NBP = 16 // (2048 // SWP)
    NWB = NBG + NBP

    def w_batch_emit(bi):
        slot = bi % 2
        qo = WQI[slot]
        src = wstr[:, qo:qo + 2048]
        if bi < NBG:
            b = bi
            K = SWG // 128

            def ld(e, b=b, qo=qo):
                return e.dma_start(out=wstr[:, qo:qo + 2048].rearrange("p (s c) -> p s c", c=SWG),
                                   in_=wgu_h.ap()[b * 1024:(b + 1) * 1024, :].rearrange("(s p) c -> p s c", p=128))
        else:
            b = bi - NBG
            nbp = 2048 // SWP
            K = SWP // 128

            def ld(e, b=b, qo=qo, nbp=nbp):
                return e.dma_start(out=wstr[:, qo:qo + 2048].rearrange("p (s c) -> p s c", c=SWP),
                                   in_=wpj_h.ap()[b * nbp * 128:(b + 1) * nbp * 128, :].rearrange("(s p) c -> p s c", p=128))
        tin = kb.emit("gpsimd", ld, deps=(ST.get(("wcons", slot)), WL.get(2), WL.get(3)), inc="qi%d" % slot, dma=True)
        te = quant_chunk(src, tin)
        stage = wstr[:, WD1:WD1 + 2048]
        stgv = stage.rearrange("p (k s c) -> p k s c", k=K, c=128)
        nS = 16 // K
        evs = []
        for tt in range(16):
            pb = 6 + (tt // 4) % 2
            last_mm = kb.emit("tensor", lambda e, tt=tt, pb=pb, src=src: e.matmul(
                bank(pb)[:, (tt % 4) * 128:(tt % 4) * 128 + 128],
                src[:, tt * 128:(tt + 1) * 128], ident[:, :],
                start=True, stop=True),
                deps=[te, TAB, BG.get(pb)], inc="pe")
            if tt % 4 == 3:
                tt0 = tt - 3
                if K >= 4:
                    s0, k0 = tt0 // K, tt0 % K
                    o = stgv[:, k0:k0 + 4, s0:s0 + 1, :]
                    i = bank(pb).rearrange("p (t o c) -> p t o c", t=4, o=1)
                else:
                    s0 = tt0 // K
                    o = stgv.transpose([0, 2, 1, 3])[:, s0:s0 + 2, :, :]
                    i = bank(pb).rearrange("p (s k c) -> p s k c", s=2, k=2)

                def ev(e, o=o, i=i):
                    return e.copy(o, i)
                tv = kb.emit("scalar", ev, deps=[last_mm, ST.get("wstg")], inc="ac")
                BG[pb] = tv
                evs.append(tv)
        ST[("wcons", slot)] = last_mm
        if bi < NBG:
            def wr(e, b=b, K=K, stage=stage):
                o = shg_h.ap()[:, b * 1024:(b + 1) * 1024].rearrange("(k p) h -> p k h", p=128)
                return e.dma_start(out=o, in_=stage.rearrange("p (k sc) -> p k sc", k=K))
        else:
            def wr(e, b=b, K=K, nbp=nbp, stage=stage):
                o = shp_h.ap()[:, b * nbp * 128:(b + 1) * nbp * 128].rearrange("(k p) c -> p k c", p=128)
                return e.dma_start(out=o, in_=stage.rearrange("p (k sc) -> p k sc", k=K))
        ST["wstg"] = kb.emit("sync", wr, deps=evs, inc="ws", dma=True)

    def a_qtile(half, lq):
        gq = half * HQT + lq

        def ldx(e, gq=gq):
            return e.dma_start(out=outs[:, :],
                               in_=x_h.ap()[gq * 128:(gq + 1) * 128, :])
        xtok = kb.emit("sync", ldx, deps=(ST.get("xac"), ST.get(("og", 0)), ST.get(("og", 1))), inc="xa", dma=True)
        xs = outs[:, :]
        tcp = kb.emit("scalar", lambda e, xs=xs: e.activation(big[:, JOFF:JOFF + 2048], xs, AF.Copy, accum_out=lns[:, 0:1]), deps=[xtok, ST.get("junk")], inc="ac")
        tsq = kb.emit("scalar", lambda e, xs=xs: e.activation(big[:, JOFF:JOFF + 2048], xs, AF.Square, accum_out=lns[:, 1:2]), deps=[xtok], inc="ac")
        kb.emit("vector", lambda e: e.tensor_scalar(lns[:, 2:3], lns[:, 0:1], 1.0 / D, None, ALU.mult), deps=[tcp], inc="dv")
        kb.emit("vector", lambda e: e.tensor_scalar(lns[:, 3:4], lns[:, 1:2], 1.0 / D, None, ALU.mult), deps=[tsq], inc="dv")
        kb.emit("vector", lambda e: e.tensor_tensor(lns[:, 4:5], lns[:, 2:3], lns[:, 2:3], ALU.mult), inc="dv")
        tm4 = kb.emit("vector", lambda e: e.scalar_tensor_tensor(lns[:, 5:6], lns[:, 3:4], LN_EPS, lns[:, 4:5], ALU.add, ALU.subtract), inc="dv")
        tl1 = kb.emit("scalar", lambda e: e.activation(lns[:, 6:7], lns[:, 5:6], AF.Ln), deps=[tm4], inc="ac")
        tl2 = kb.emit("scalar", lambda e: e.activation(lns[:, 7:8], lns[:, 6:7], AF.Exp, scale=-0.5), inc="ac")
        txn = kb.emit("vector", lambda e, xs=xs: e.tensor_scalar(big[:, XOFF:XOFF + 2048], xs, lns[:, 2:3], lns[:, 7:8], ALU.subtract, ALU.mult), deps=[tl2], inc="dv")
        ST["junk"] = txn
        ST["xac"] = txn
        tv = None
        for dt in range(16):
            pb = 6 + (dt // 4) % 2
            tmm = kb.emit("tensor", lambda e, dt=dt, pb=pb: e.matmul(
                bank(pb)[:, (dt % 4) * 128:(dt % 4) * 128 + 128],
                big[:, XOFF + dt * 128:XOFF + (dt + 1) * 128], ident[:, :],
                start=True, stop=True), deps=[txn, TAB, BG.get(pb)], inc="pe")
            if dt % 4 == 3:
                dt0 = dt - 3

                def evx(e, pb=pb, dt0=dt0, lq=lq):
                    o = xnt.rearrange("p (dt c) -> p dt c", dt=16)[:, dt0:dt0 + 4, lq * 128:(lq + 1) * 128]
                    return e.copy(o, bank(pb).rearrange("p (t c) -> p t c", t=4))
                tv = kb.emit("scalar", evx, deps=[tmm], inc="ac")
                BG[pb] = tv
        ST[("xnq", half, lq)] = tv
        ST[("xnt", half)] = tv

    def w_feed(dram_ap, dst_cols, width):
        par = ST.get("cstp", 0)
        ST["cstp"] = 1 - par
        stg = xf[:, 2048 + par * 1024: 2048 + par * 1024 + width]

        def ld(e, dram_ap=dram_ap, stg=stg):
            return e.dma_start(out=stg, in_=dram_ap)
        lt = kb.emit("sync", ld, deps=(ST.get(("cst", par)),
                                       ("ab0", kb.cnt.get("ab0", 0)), ("ab1", kb.cnt.get("ab1", 0)),
                                       ST.get(("ar", 0)), ST.get(("ar", 1))), inc="aw%d" % par, dma=True)
        reg = dst_cols[0] // 4096
        ct = kb.emit("scalar", lambda e, stg=stg, dst_cols=dst_cols: e.copy(wstr[:, dst_cols[0]:dst_cols[1]], stg),
                     deps=[lt, WL.get(reg)], inc="ac")
        ST[("cst", par)] = ct
        return ct

    def b_phase(half, gsel=None):
        for g in (range(NG) if gsel is None else [gsel]):
            q0 = g * GQ
            for dqc in range(2):
                for dtb in range(4):
                    s2 = (dqc * 4 + dtb) % 2
                    ctoks = []
                    for dtl in range(4):
                        dt = dtb * 4 + dtl
                        ctoks.append(w_feed(wq_h.ap()[dt * 128:(dt + 1) * 128, dqc * 1024:(dqc + 1) * 1024],
                                            (s2 * 4096 + dtl * 1024, s2 * 4096 + (dtl + 1) * 1024), 1024))
                    for dtl in range(4):
                        dt = dtb * 4 + dtl
                        for q in range(GQ):
                            for hb in range(2):
                                tok = kb.emit("tensor", lambda e, dt=dt, dtl=dtl, q=q, hb=hb, s2=s2, q0=q0: e.matmul(
                                    bank(q * 2 + hb),
                                    xnt[:, dt * Th + (q0 + q) * 128: dt * Th + (q0 + q + 1) * 128],
                                    wstr[:, s2 * 4096 + dtl * 1024 + hb * 512: s2 * 4096 + dtl * 1024 + (hb + 1) * 512],
                                    start=(dt == 0), stop=(dt == 15)),
                                    deps=[ctoks[dtl], ST[("xnq", half, q0 + q)]] + ([BG.get(q * 2 + hb)] if dt == 0 else []),
                                    inc="pe")
                    WL[s2] = tok
                for q in range(GQ):
                    lq = q0 + q
                    gq = half * HQT + lq
                    pv = ps[:, q * 1024:(q + 1) * 1024].rearrange("p (h x) -> p h x", h=8)
                    pv4 = ps[:, q * 1024:(q + 1) * 1024].rearrange("p (h t x) -> p h t x", h=8, t=2)
                    cosv = cos_sb[:, gq * 64:(gq + 1) * 64][:, None, None, :].broadcast_to([128, 8, 2, 64])
                    sinv = sin_sb[:, gq * 64:(gq + 1) * 64][:, None, :].broadcast_to([128, 8, 64])
                    t1v = xf[:, 0:1024].rearrange("p (h x) -> p h x", h=8)
                    t1v4 = xf[:, 0:1024].rearrange("p (h t x) -> p h t x", h=8, t=2)
                    t2v = xf[:, 1024:2048].rearrange("p (h x) -> p h x", h=8)
                    qcd = big[:, lq * 2048 + dqc * 1024: lq * 2048 + (dqc + 1) * 1024].rearrange("p (h x) -> p h x", h=8)
                    kb.emit("vector", lambda e, pv4=pv4, cosv=cosv, t1v4=t1v4: e.tensor_tensor(t1v4, pv4, cosv, ALU.mult), deps=[tok, TAB] + list(XGRD()), inc="dv")
                    kb.emit("vector", lambda e, pv=pv, sinv=sinv, t2v=t2v: e.tensor_tensor(t2v[:, :, 0:64], pv[:, :, 64:128], sinv, ALU.mult), inc="dv")
                    kb.emit("vector", lambda e, pv=pv, sinv=sinv, t2v=t2v: e.tensor_tensor(t2v[:, :, 64:128], pv[:, :, 0:64], sinv, ALU.mult), inc="dv")
                    kb.emit("vector", lambda e, qcd=qcd, t1v=t1v, t2v=t2v: e.tensor_tensor(qcd[:, :, 0:64], t1v[:, :, 0:64], t2v[:, :, 0:64], ALU.subtract), inc="dv")
                    r5 = kb.emit("vector", lambda e, qcd=qcd, t1v=t1v, t2v=t2v: e.tensor_tensor(qcd[:, :, 64:128], t1v[:, :, 64:128], t2v[:, :, 64:128], ALU.add), inc="dv")
                    BG[q * 2] = r5
                    BG[q * 2 + 1] = r5
                    ST["ropescr"] = r5
            for dtb in range(4):
                s2 = dtb % 2
                ctoks = []
                for dtl in range(4):
                    dt = dtb * 4 + dtl
                    t1 = w_feed(wk_h.ap()[dt * 128:(dt + 1) * 128, :],
                                (s2 * 4096 + dtl * 1024, s2 * 4096 + dtl * 1024 + 512), 512)
                    t2 = w_feed(wv_h.ap()[dt * 128:(dt + 1) * 128, :],
                                (s2 * 4096 + dtl * 1024 + 512, s2 * 4096 + (dtl + 1) * 1024), 512)
                    ctoks.append(t2)
                for dtl in range(4):
                    dt = dtb * 4 + dtl
                    for q in range(GQ):
                        for kvh in range(2):
                            tok = kb.emit("tensor", lambda e, dt=dt, dtl=dtl, q=q, kvh=kvh, s2=s2, q0=q0: e.matmul(
                                bank(q * 2 + kvh),
                                xnt[:, dt * Th + (q0 + q) * 128: dt * Th + (q0 + q + 1) * 128],
                                wstr[:, s2 * 4096 + dtl * 1024 + kvh * 512: s2 * 4096 + dtl * 1024 + (kvh + 1) * 512],
                                start=(dt == 0), stop=(dt == 15)),
                                deps=[ctoks[dtl]] + ([BG.get(q * 2 + kvh)] if dt == 0 else []),
                                inc="pe")
                WL[s2] = tok
            for q in range(GQ):
                lq = q0 + q
                gq = half * HQT + lq
                pkv = bank(q * 2).rearrange("p (h x) -> p h x", h=HKV)
                pk4 = bank(q * 2).rearrange("p (h t x) -> p h t x", h=HKV, t=2)
                cosv = cos_sb[:, gq * 64:(gq + 1) * 64][:, None, None, :].broadcast_to([128, HKV, 2, 64])
                sinv = sin_sb[:, gq * 64:(gq + 1) * 64][:, None, :].broadcast_to([128, HKV, 64])
                t1v = xf[:, 0:512].rearrange("p (h x) -> p h x", h=HKV)
                t1v4 = xf[:, 0:512].rearrange("p (h t x) -> p h t x", h=HKV, t=2)
                t2v = xf[:, 1024:1536].rearrange("p (h x) -> p h x", h=HKV)
                kdst = sl[:, 0:512].rearrange("p (h x) -> p h x", h=HKV)
                kb.emit("vector", lambda e, pk4=pk4, cosv=cosv, t1v4=t1v4: e.tensor_tensor(t1v4, pk4, cosv, ALU.mult), deps=[tok] + list(XGRD()), inc="dv")
                kb.emit("vector", lambda e, pkv=pkv, sinv=sinv, t2v=t2v: e.tensor_tensor(t2v[:, :, 0:64], pkv[:, :, 64:128], sinv, ALU.mult), inc="dv")
                r3 = kb.emit("vector", lambda e, pkv=pkv, sinv=sinv, t2v=t2v: e.tensor_tensor(t2v[:, :, 64:128], pkv[:, :, 0:64], sinv, ALU.mult), inc="dv")
                kb.emit("vector", lambda e, kdst=kdst, t1v=t1v, t2v=t2v: e.tensor_tensor(kdst[:, :, 0:64], t1v[:, :, 0:64], t2v[:, :, 0:64], ALU.subtract), deps=[ST.get("ktm")], inc="dv")
                r5 = kb.emit("vector", lambda e, kdst=kdst, t1v=t1v, t2v=t2v: e.tensor_tensor(kdst[:, :, 64:128], t1v[:, :, 64:128], t2v[:, :, 64:128], ALU.add), inc="dv")
                ST["ropescr"] = r5
                vdst = vb[:, gq * HKV * 132: (gq + 1) * HKV * 132].rearrange("p (h x) -> p h x", x=132)[:, :, 0:128]
                tvv = kb.emit("scalar", lambda e, vdst=vdst, q=q: e.copy(vdst, bank(q * 2 + 1).rearrange("p (h x) -> p h x", h=HKV)), deps=[tok], inc="ac")
                BG[q * 2 + 1] = tvv
                ST["vbw"] = tvv
                last_mm = None
                for kv in range(HKV):
                    last_mm = kb.emit("tensor", lambda e, kv=kv, q=q: e.matmul(
                        bank(q * 2)[:, kv * 128:(kv + 1) * 128],
                        sl[:, kv * 128:(kv + 1) * 128], ident[:, :],
                        start=True, stop=True), deps=[r5, r3], inc="pe")

                def evk(e, gq=gq, q=q):
                    o = kt.rearrange("p (k t) -> p k t", k=HKV)[:, :, gq * 128:(gq + 1) * 128]
                    return e.tensor_copy(o, bank(q * 2).rearrange("p (k c) -> p k c", k=HKV))
                tke = kb.emit("vector", evk, deps=[last_mm], inc="dv")
                BG[q * 2] = tke
                ST["ktm"] = tke
        ST[("bdone", half)] = ST["ktm"]

    def c_phase(half):
        e_gate = {}
        scores_last = None
        bdone = ST[("bdone", half)]
        vbw = ST["vbw"]
        for head in range(HQ):
            kv = head % HKV
            ev_last = None
            for lq0 in range(0, HQT, 4):
                n = min(4, HQT - lq0)
                pb = 6 + (lq0 // 4) % 2
                last_mm = None
                for i in range(n):
                    lq = lq0 + i
                    last_mm = kb.emit("tensor", lambda e, lq=lq, head=head, pb=pb, i=i: e.matmul(
                        bank(pb)[:, i * 128:(i + 1) * 128],
                        big[:, lq * 2048 + head * 128: lq * 2048 + (head + 1) * 128], ident[:, :],
                        start=True, stop=True),
                        deps=[BG.get(pb), ST.get(("ctx", head - 1, lq)), bdone, TAB], inc="pe")

                def evq(e, pb=pb, lq0=lq0, n=n):
                    return e.tensor_copy(big[:, QTT + lq0 * 128: QTT + (lq0 + n) * 128], bank(pb)[:, 0:n * 128])
                ev_last = kb.emit("vector", evq, deps=[last_mm, scores_last], inc="dv")
                BG[pb] = ev_last
            for qc in range(NQC):
                gqc = half * NQC + qc
                nkt = (gqc + 1) * QCW // 128
                kt0_cross = gqc * QCW // 128
                for ktile in range(nkt):
                    pb = ktile % 4
                    tmm = kb.emit("tensor", lambda e, ktile=ktile, kv=kv, qc=qc, pb=pb: e.matmul(
                        bank(pb, QCW),
                        kt[:, kv * T + ktile * 128: kv * T + (ktile + 1) * 128],
                        big[:, QTT + qc * QCW: QTT + (qc + 1) * QCW],
                        start=True, stop=True),
                        deps=[ev_last, BG.get(pb)], inc="pe")
                    scores_last = tmm
                    tex = kb.emit("scalar", lambda e, ktile=ktile, pb=pb: e.activation(
                        big[:, EOFF + ktile * QCW: EOFF + (ktile + 1) * QCW], bank(pb, QCW), AF.Exp, scale=SM_SCALE),
                        deps=[tmm, e_gate.get(("e", ktile))], inc="ac")
                    BG[pb] = tex
                    etok = tex
                    if ktile >= kt0_cross:
                        o_rel = (ktile - kt0_cross) * 128
                        etok = kb.emit("vector", lambda e, ktile=ktile, o_rel=o_rel: e.tensor_tensor(
                            big[:, EOFF + ktile * QCW: EOFF + (ktile + 1) * QCW],
                            big[:, EOFF + ktile * QCW: EOFF + (ktile + 1) * QCW],
                            maskc[:, 384 - o_rel: 384 - o_rel + QCW], ALU.mult), deps=[tex, TAB], inc="dv")
                    e_gate[("m", ktile)] = etok
                for ql in range(QCW // 128):
                    lq = qc * (QCW // 128) + ql
                    gq = half * HQT + lq
                    cb = 4 + ql % 2
                    for ktile in range(gq + 1):
                        tok = kb.emit("tensor", lambda e, ktile=ktile, ql=ql, kv=kv, gq=gq, cb=cb, qc=qc: e.matmul(
                            bank(cb)[:, 0:129],
                            big[:, EOFF + ktile * QCW + ql * 128: EOFF + ktile * QCW + (ql + 1) * 128],
                            vb[:, ktile * HKV * 132 + kv * 132: ktile * HKV * 132 + kv * 132 + 129],
                            start=(ktile == 0), stop=(ktile == gq)),
                            deps=[e_gate.get(("m", ktile))] + ([BG.get(cb), vbw] if ktile == 0 else []),
                            inc="pe")
                    kb.emit("vector", lambda e, cb=cb, ql=ql: e.reciprocal(rss[:, ql:ql + 1], bank(cb)[:, 128:129]), deps=[tok], inc="dv")
                    tcx = kb.emit("vector", lambda e, cb=cb, ql=ql, lq=lq, head=head: e.tensor_scalar(
                        big[:, lq * 2048 + head * 128: lq * 2048 + (head + 1) * 128],
                        bank(cb)[:, 0:128], rss[:, ql:ql + 1], None, ALU.mult), inc="dv")
                    BG[cb] = tcx
                    ST[("ctx", head, lq)] = tcx
                lastq = qc * (QCW // 128) + (QCW // 128) - 1
                for ktile in range(nkt):
                    e_gate[("e", ktile)] = ST[("ctx", head, lastq)]

    def d_phase(half):
        for g in range(NG):
            q0 = g * GQ
            ct_done = []
            for q in range(GQ):
                lq = q0 + q
                for dt in range(16):
                    pb = 6 + (dt // 4) % 2
                    tmm = kb.emit("tensor", lambda e, lq=lq, dt=dt, pb=pb: e.matmul(
                        bank(pb)[:, (dt % 4) * 128:(dt % 4) * 128 + 128],
                        big[:, lq * 2048 + dt * 128: lq * 2048 + (dt + 1) * 128], ident[:, :],
                        start=True, stop=True),
                        deps=[BG.get(pb), ST.get(("ctx", HQ - 1, lq)), TAB], inc="pe")
                    if dt % 4 == 3:
                        dt0 = dt - 3

                        def evc(e, pb=pb, dt0=dt0, q=q):
                            return e.tensor_copy(big[:, EOFF + q * 2048 + dt0 * 128: EOFF + q * 2048 + (dt0 + 4) * 128], bank(pb))
                        tv = kb.emit("vector", evc, deps=[tmm, ST.get(("ctc", q))], inc="dv")
                        BG[pb] = tv
                        ct_done.append(tv)
            for dc in range(2):
                for dtb in range(4):
                    s2 = (dc * 4 + dtb) % 2
                    ctoks = []
                    for dtl in range(4):
                        dt = dtb * 4 + dtl
                        ctoks.append(w_feed(wo_h.ap()[dt * 128:(dt + 1) * 128, dc * 1024:(dc + 1) * 1024],
                                            (s2 * 4096 + dtl * 1024, s2 * 4096 + (dtl + 1) * 1024), 1024))
                    for dtl in range(4):
                        dt = dtb * 4 + dtl
                        for q in range(GQ):
                            for hb in range(2):
                                tok = kb.emit("tensor", lambda e, dt=dt, dtl=dtl, q=q, hb=hb, s2=s2: e.matmul(
                                    bank(q * 2 + hb),
                                    big[:, EOFF + q * 2048 + dt * 128: EOFF + q * 2048 + (dt + 1) * 128],
                                    wstr[:, s2 * 4096 + dtl * 1024 + hb * 512: s2 * 4096 + dtl * 1024 + (hb + 1) * 512],
                                    start=(dt == 0), stop=(dt == 15)),
                                    deps=[ctoks[dtl]] + (ct_done + [BG.get(q * 2 + hb)] if dt == 0 else []),
                                    inc="pe")
                    WL[s2] = tok
                for q in range(GQ):
                    gq = half * HQT + q0 + q
                    s2 = q % 2
                    o2 = outs[:, s2 * 1024:(s2 + 1) * 1024]
                    tv = kb.emit("vector", lambda e, q=q, o2=o2: e.tensor_copy(o2, ps[:, q * 1024:(q + 1) * 1024]),
                                 deps=[tok, ST.get(("og", s2))], inc="dv")
                    BG[q * 2] = tv
                    BG[q * 2 + 1] = tv
                    if dc == 1:
                        ST[("ctc", q)] = tv

                    def sta(e, gq=gq, dc=dc, o2=o2):
                        return e.dma_start(out=att_h.ap()[gq * 128:(gq + 1) * 128, dc * 1024:(dc + 1) * 1024], in_=o2)
                    ST[("og", s2)] = kb.emit("sync", sta, deps=[tv], inc="fs%d" % s2, dma=True)

    def f_phase(half):
        WSD = ("cc", 1)
        WSDP = ("cc", 2)
        FSD = (("fs0", kb.cnt.get("fs0", 0)), ("fs1", kb.cnt.get("fs1", 0)))
        for tb in range(NTB):
            q0 = tb * TBQ
            ev_gate = {}
            h_tok = {}
            tok = None
            for j in range(NJ):
                for gu in range(2):
                    p2 = (2 * j + gu) % 2
                    sA, sB = 2 * p2, 2 * p2 + 1
                    hcol = gu * HF + j * 512

                    def ldw1(e, hcol=hcol, sA=sA):
                        return e.dma_start(out=wstr[:, sA * 4096:(sA + 1) * 4096].rearrange("p (dt c) -> p dt c", dt=8),
                                           in_=wtg_h.ap()[0:1024, hcol:hcol + 512].rearrange("(dt p) c -> p dt c", p=128))

                    def ldw2(e, hcol=hcol, sB=sB):
                        return e.dma_start(out=wstr[:, sB * 4096:(sB + 1) * 4096].rearrange("p (dt c) -> p dt c", dt=8),
                                           in_=wtg_h.ap()[1024:2048, hcol:hcol + 512].rearrange("(dt p) c -> p dt c", p=128))
                    wgA = kb.emit("sync", ldw1, deps=(WL.get(sA), WSD, ("ws", kb.cnt.get("ws", 0))) + WGRD(), inc="wg%d" % sA, dma=True)
                    wgB = kb.emit("scalar", ldw2, deps=(WL.get(sB), WSD, ("ws", kb.cnt.get("ws", 0))) + WGRD(), inc="wg%d" % sB, dma=True)
                    for dt in range(16):
                        sl_i = sA if dt < 8 else sB
                        wt = wgA if dt < 8 else wgB
                        for q in range(TBQ):
                            bnk = q * 2 + gu
                            tok = kb.emit("tensor", lambda e, dt=dt, q=q, bnk=bnk, sl_i=sl_i, q0=q0: e.matmul(
                                bank(bnk),
                                xnt[:, dt * Th + (q0 + q) * 128: dt * Th + (q0 + q + 1) * 128],
                                wstr[:, sl_i * 4096 + (dt % 8) * 512: sl_i * 4096 + (dt % 8 + 1) * 512],
                                start=(dt == 0), stop=(dt == 15)),
                                deps=[wt] + ([BG.get(bnk), ST.get(("xnq", half, q0 + q))] if dt == 0 else []),
                                inc="pe")
                        if dt == 7:
                            WL[sA] = tok
                    WL[sB] = tok
                for q in range(TBQ):
                    tsl = kb.emit("scalar", lambda e, q=q: e.activation(sl[:, q * 512:(q + 1) * 512], bank(q * 2), AF.Silu), deps=[tok, ev_gate.get(("sl", q))], inc="ac")
                    th = kb.emit("vector", lambda e, q=q, j=j: e.tensor_tensor(
                        big[:, q * HF + j * 512: q * HF + (j + 1) * 512],
                        sl[:, q * 512:(q + 1) * 512], bank(q * 2 + 1), ALU.mult), deps=[tsl, tok], inc="dv")
                    BG[q * 2] = tsl
                    BG[q * 2 + 1] = th
                    ev_gate[("sl", q)] = th
                    h_tok[(q, j)] = th
            hnt_done = []
            for q in range(TBQ):
                hq = big[:, q * HF:(q + 1) * HF]
                deps_h = [h_tok[(q, j)] for j in range(NJ)]
                tsq = kb.emit("scalar", lambda e, hq=hq, q=q: e.activation(wstr[:, 0:HF], hq, AF.Square, accum_out=fss[:, q:q + 1]),
                              deps=deps_h + [ST.get("fjunk")] + [WL.get(i) for i in range(4) if i * 4096 < HF], inc="ac")
                ST["fjunk"] = tsq
                for i in range(4):
                    if i * 4096 < HF:
                        WL[i] = tsq
                tms = kb.emit("vector", lambda e, q=q: e.tensor_scalar(fss[:, 4 + q:5 + q], fss[:, q:q + 1], 1.0 / HF, RMS_EPS, ALU.mult, ALU.add), deps=[tsq], inc="dv")
                kb.emit("scalar", lambda e, q=q: e.activation(fss[:, 4 + q:5 + q], fss[:, 4 + q:5 + q], AF.Ln), deps=[tms], inc="ac")
                ti = kb.emit("scalar", lambda e, q=q: e.activation(fss[:, 8 + q:9 + q], fss[:, 4 + q:5 + q], AF.Exp, scale=-0.5), inc="ac")
                thn = kb.emit("vector", lambda e, hq=hq, q=q: e.tensor_scalar(hq, hq, fss[:, 8 + q:9 + q], None, ALU.mult), deps=[ti], inc="dv")
                for ht in range(NHT):
                    pb = 6 + (ht // 4) % 2
                    tmm = kb.emit("tensor", lambda e, hq=hq, ht=ht, pb=pb: e.matmul(
                        bank(pb)[:, (ht % 4) * 128:(ht % 4) * 128 + 128],
                        hq[:, ht * 128:(ht + 1) * 128], ident[:, :],
                        start=True, stop=True),
                        deps=[thn, BG.get(pb)], inc="pe")
                    if ht % 4 == 3:
                        ht0 = ht - 3

                        def evh(e, pb=pb, ht0=ht0, hq=hq):
                            return e.tensor_copy(hq[:, ht0 * 128:(ht0 + 4) * 128], bank(pb))
                        tv = kb.emit("vector", evh, deps=[tmm], inc="dv")
                        BG[pb] = tv
                        hnt_done.append(tv)
            xtok = {}
            attok = {}
            NHG = NHT // 4
            for dc in range(2):
                for q in range(min(2, TBQ)):
                    gq = half * HQT + q0 + q

                    def ldx2(e, gq=gq, q=q, dc=dc):
                        return e.dma_start(out=xf[:, (q % 2) * 1024:(q % 2 + 1) * 1024],
                                           in_=x_h.ap()[gq * 128:(gq + 1) * 128, dc * 1024:(dc + 1) * 1024])

                    def lda(e, gq=gq, q=q, dc=dc):
                        return e.dma_start(out=xf[:, 2048 + (q % 2) * 1024: 2048 + (q % 2 + 1) * 1024],
                                           in_=att_h.ap()[gq * 128:(gq + 1) * 128, dc * 1024:(dc + 1) * 1024])
                    xtok[q] = kb.emit("sync", ldx2, deps=(ST.get("ropescr"), ST.get(("xr", q % 2))), inc="x%d" % (q % 2), dma=True)
                    attok[q] = kb.emit("sync", lda, deps=(ST.get(("ar", q % 2)), ST.get(("cst", 0)), ST.get(("cst", 1))) + FSD, inc="ab%d" % (q % 2), dma=True)
                for hg in range(NHG):
                    s4 = (dc * NHG + hg) % 4

                    def ldp2(e, hg=hg, dc=dc, s4=s4):
                        return e.dma_start(out=wstr[:, s4 * 4096:(s4 + 1) * 4096].rearrange("p (ht c) -> p ht c", ht=4),
                                           in_=wtp_h.ap()[hg * 512:(hg + 1) * 512, dc * 1024:(dc + 1) * 1024].rearrange("(ht p) c -> p ht c", p=128))
                    wpt = kb.emit("sync" if s4 % 2 == 0 else "scalar", ldp2, deps=(WL.get(s4), WSDP) + WGRD(), inc="wp%d" % s4, dma=True)
                    for htl in range(4):
                        ht = hg * 4 + htl
                        for q in range(TBQ):
                            for hb in range(2):
                                tok = kb.emit("tensor", lambda e, htl=htl, ht=ht, q=q, hb=hb, s4=s4: e.matmul(
                                    bank(q * 2 + hb),
                                    big[:, q * HF + ht * 128: q * HF + (ht + 1) * 128],
                                    wstr[:, s4 * 4096 + htl * 1024 + hb * 512: s4 * 4096 + htl * 1024 + (hb + 1) * 512],
                                    start=(ht == 0), stop=(ht == NHT - 1)),
                                    deps=[wpt] + (hnt_done + [BG.get(q * 2 + hb)] if ht == 0 else []),
                                    inc="pe")
                    WL[s4] = tok
                for q in range(TBQ):
                    gq = half * HQT + q0 + q
                    s2 = q % 2
                    o2 = outs[:, s2 * 1024:(s2 + 1) * 1024]
                    ta = kb.emit("vector", lambda e, s2=s2, q=q, o2=o2: e.tensor_tensor(
                        o2, ps[:, q * 1024:(q + 1) * 1024],
                        xf[:, s2 * 1024:(s2 + 1) * 1024], ALU.add),
                        deps=[tok, xtok[q], ST.get(("og", s2))], inc="dv")
                    tb2 = kb.emit("vector", lambda e, s2=s2, o2=o2: e.tensor_tensor(
                        o2, o2, xf[:, 2048 + s2 * 1024: 2048 + (s2 + 1) * 1024], ALU.add),
                        deps=[attok[q]], inc="dv")
                    BG[q * 2] = tb2
                    BG[q * 2 + 1] = tb2
                    ST[("xr", s2)] = ta
                    ST[("ar", s2)] = tb2

                    def sto(e, gq=gq, dc=dc, o2=o2):
                        return e.dma_start(out=out_h.ap()[gq * 128:(gq + 1) * 128, dc * 1024:(dc + 1) * 1024], in_=o2)
                    ST[("og", s2)] = kb.emit("gpsimd", sto, deps=[tb2], inc="os%d" % s2, dma=True)
                    if q + 2 < TBQ:
                        gq2 = half * HQT + q0 + q + 2

                        def ldx3(e, gq2=gq2, s2=s2, dc=dc):
                            return e.dma_start(out=xf[:, s2 * 1024:(s2 + 1) * 1024],
                                               in_=x_h.ap()[gq2 * 128:(gq2 + 1) * 128, dc * 1024:(dc + 1) * 1024])

                        def lda3(e, gq2=gq2, s2=s2, dc=dc):
                            return e.dma_start(out=xf[:, 2048 + s2 * 1024: 2048 + (s2 + 1) * 1024],
                                               in_=att_h.ap()[gq2 * 128:(gq2 + 1) * 128, dc * 1024:(dc + 1) * 1024])
                        xtok[q + 2] = kb.emit("sync", ldx3, deps=(ta,), inc="x%d" % s2, dma=True)
                        attok[q + 2] = kb.emit("sync", lda3, deps=(tb2,) + FSD, inc="ab%d" % s2, dma=True)

    per = max(1, (NWB + HQT - 1) // HQT)
    RG = [[0, 1, 2, 3, 4, 5, 6, 7]]
    kb.sem("cc")
    kb.dma_sems.add("cc")
    wi = 0
    for g in range(NG):
        for lq in range(g * GQ, (g + 1) * GQ):
            for _ in range(per):
                if wi < NWB:
                    w_batch_emit(wi)
                    wi += 1
            a_qtile(0, lq)
        if g == NG - 1:
            while wi < NWB:
                w_batch_emit(wi)
                wi += 1
        b_phase(0, gsel=g)
    kb.emit("gpsimd", lambda e: e.collective_compute("AllGather", ALU.bypass, RG, [shg_h.ap().opt()], [wtg_h.ap().opt()]), deps=[("ws", kb.cnt["ws"]), ("pe", kb.cnt["pe"])], inc="cc")
    kb.emit("gpsimd", lambda e: e.collective_compute("AllGather", ALU.bypass, RG, [shp_h.ap().opt()], [wtp_h.ap().opt()]), inc="cc")
    c_phase(0)
    d_phase(0)
    f_phase(0)
    kb.block()
    for lq in range(HQT):
        a_qtile(1, lq)
    b_phase(1)
    c_phase(1)
    d_phase(1)
    f_phase(1)
    kb.block()
    kb.emit("sync", lambda e: e.nop(), deps=[("os0", kb.cnt["os0"]), ("os1", kb.cnt.get("os1", 0)),
                                             ("fs0", kb.cnt.get("fs0", 0)), ("fs1", kb.cnt.get("fs1", 0)),
                                             ("cc", 2), ("ws", kb.cnt["ws"])])
    kb.realize()
    return nc


_NC_CACHE = {}


def _get_nc(T, HF):
    key = (T, HF)
    if key not in _NC_CACHE:
        _NC_CACHE[key] = build_nc(T, HF)
    return _NC_CACHE[key]


def _host_tables(T):
    bf = ml_dtypes.bfloat16
    inv = 1.0 / (10000.0 ** (np.arange(0, DH, 2, dtype=np.float64) / DH))
    ang = np.arange(T, dtype=np.float64)[:, None] * inv[None, :]
    cos = np.cos(ang).astype(np.float32)
    sin = np.sin(ang).astype(np.float32)
    costab = np.ascontiguousarray(cos)
    sintab = np.ascontiguousarray(sin)
    ident = np.eye(128, dtype=bf)
    k_i = np.arange(128)[:, None]
    t_i = np.arange(896)[None, :]
    maskc = ((t_i - 384) >= k_i).astype(bf)
    return costab, sintab, ident, maskc


def _deinterleave_cols(w, block=128):
    d_out = w.shape[1]
    perm = []
    for b0 in range(0, d_out, block):
        perm.extend(range(b0, b0 + block, 2))
        perm.extend(range(b0 + 1, b0 + block, 2))
    return np.ascontiguousarray(w[:, perm])


def _permute_wo_rows(wo):
    rows = []
    for h in range(HQ):
        src = 4 * (h % 4) + h // 4
        rows.extend(range(src * 128, (src + 1) * 128))
    return np.ascontiguousarray(wo[rows, :])


def kernel(L0_attn_x, L0_attn_Wq, L0_attn_Wk, L0_attn_Wv, L0_attn_Wo,
           L0_ffn_w_gate_up, L0_ffn_w_proj):
    x = np.ascontiguousarray(np.asarray(L0_attn_x, dtype=np.float32))
    wq = _deinterleave_cols(np.asarray(L0_attn_Wq, dtype=np.float32))
    wk = _deinterleave_cols(np.asarray(L0_attn_Wk, dtype=np.float32))
    wv = np.ascontiguousarray(np.asarray(L0_attn_Wv, dtype=np.float32))
    wo = _permute_wo_rows(np.asarray(L0_attn_Wo, dtype=np.float32))
    wgu = np.ascontiguousarray(np.asarray(L0_ffn_w_gate_up, dtype=np.float32))
    wpj = np.ascontiguousarray(np.asarray(L0_ffn_w_proj, dtype=np.float32))
    B, T, _ = x.shape
    HF = wpj.shape[1]
    costab, sintab, ident, maskc = _host_tables(T)
    nc = _get_nc(T, HF)
    swg, swp = D // 8, HF // 8
    in_maps = [dict(x=x[b], wq=wq, wk=wk, wv=wv, wo=wo,
                    wgu_s=np.ascontiguousarray(wgu[:, b * swg:(b + 1) * swg]),
                    wproj_s=np.ascontiguousarray(wpj[:, b * swp:(b + 1) * swp]),
                    costab=costab, sintab=sintab, ident=ident, maskc=maskc)
               for b in range(B)]
    res = run_bass_kernel_spmd(nc, in_maps, core_ids=list(range(B)))
    return np.stack([np.asarray(res.results[b]["out"]) for b in range(B)], axis=0)


# version 15
# speedup vs baseline: 1.0513x; 1.0088x over previous
import math
import numpy as np
import ml_dtypes

import concourse.bass as bass
import concourse.mybir as mybir
import concourse.bass_utils as _bu
from concourse.bass_utils import run_bass_kernel_spmd

if not getattr(_bu, "_ldw_patched", False):
    _orig_run_command = _bu.run_command

    def _run_command_ldw(argv, **kw):
        argv = [a.replace("--enable-ldw-opt=false", "--enable-ldw-opt=true")
                if isinstance(a, str) else a for a in argv]
        return _orig_run_command(argv, **kw)

    _bu.run_command = _run_command_ldw
    _bu._ldw_patched = True

F32 = mybir.dt.float32
BF16 = mybir.dt.bfloat16
AF = mybir.ActivationFunctionType
ALU = mybir.AluOpType
AX = mybir.AxisListType

D = 2048
HQ, HKV, DH = 16, 4, 128
KVW = HKV * DH
LN_EPS = 1e-5
RMS_EPS = float(np.finfo(np.float32).eps)
SM_SCALE = 1.0 / math.sqrt(DH)

ENGS = ("tensor", "vector", "scalar", "gpsimd", "sync")
CHAIN = ("vector", "scalar")


class KB:

    def __init__(self, nc):
        self.nc = nc
        self.blocks = []
        self.cur = None
        self.sems = {}
        self.cnt = {}
        self.hw = None
        self.block_start = None
        self.dma_sems = set()
        self.last = {}

    def sem(self, name, dma=False):
        if name not in self.sems:
            self.sems[name] = self.nc.alloc_semaphore(name)
            self.cnt[name] = 0
            if dma:
                self.dma_sems.add(name)
        return self.sems[name]

    def block(self):
        self.cur = {e: [] for e in ENGS}
        self.blocks.append(self.cur)
        self.hw = {}
        self.block_start = {s: self.cnt[s] for s in self.cnt if s not in self.dma_sems}

    def emit(self, eng, fn, deps=(), inc=None, dma=False):
        if dma and inc is not None:
            self.sem(inc, dma=True)
            deps = tuple(deps) + ((inc, self.cnt[inc]),)
        if eng in CHAIN and not dma:
            deps = tuple(deps) + (self.last.get(eng),)
            assert inc is not None
        if inc is not None:
            self.sem(inc, dma=dma)
        waits = []
        for d in deps:
            if d is None:
                continue
            s, v = d
            self.sem(s)
            if v <= 0 or v <= self.block_start.get(s, 0):
                continue
            k = (eng, s)
            if self.hw.get(k, 0) >= v:
                continue
            self.hw[k] = v
            waits.append((s, v))
        tok = None
        if inc is not None:
            self.cnt[inc] += 16 if dma else 1
            tok = (inc, self.cnt[inc])
            if eng in CHAIN:
                self.last[eng] = tok
        self.cur[eng].append((waits, fn, inc, 16 if dma else 1))
        return tok

    def realize(self):
        for blk in self.blocks:
            with self.nc.Block() as block:
                for eng in ENGS:
                    items = blk[eng]
                    if not items:
                        continue

                    def make_body(items=items):
                        def body(e):
                            for waits, fn, inc, by in items:
                                for (s, v) in waits:
                                    e.wait_ge(self.sems[s], v)
                                ins = fn(e)
                                if inc is not None:
                                    ins.then_inc(self.sems[inc], by)
                        return body

                    getattr(block, eng)(make_body())


def build_nc(T=2048, HF=8192):
    H2 = 2 * HF
    NQT = T // 128
    HQT = NQT // 2
    Th = T // 2
    TBQ = min(4, HQT)
    NTB = HQT // TBQ
    GQ = min(4, HQT)
    NG = HQT // GQ
    QCW = min(512, Th)
    NQC = Th // QCW
    NHT = HF // 128
    NJ = HF // 512
    SWG = D // 8
    SWP = HF // 8

    nc = bass.Bass("TRN2", target_bir_lowering=False)

    x_h = nc.dram_tensor("x", [T, D], F32, kind="ExternalInput")
    wq_h = nc.dram_tensor("wq", [D, D], F32, kind="ExternalInput")
    wk_h = nc.dram_tensor("wk", [D, KVW], F32, kind="ExternalInput")
    wv_h = nc.dram_tensor("wv", [D, KVW], F32, kind="ExternalInput")
    wo_h = nc.dram_tensor("wo", [D, D], F32, kind="ExternalInput")
    wgu_h = nc.dram_tensor("wgu_s", [H2, SWG], F32, kind="ExternalInput")
    wpj_h = nc.dram_tensor("wproj_s", [D, SWP], F32, kind="ExternalInput")
    cos_h = nc.dram_tensor("costab", [T, 64], F32, kind="ExternalInput")
    sin_h = nc.dram_tensor("sintab", [T, 64], F32, kind="ExternalInput")
    id_h = nc.dram_tensor("ident", [128, 128], BF16, kind="ExternalInput")
    mk_h = nc.dram_tensor("maskc", [128, 896], BF16, kind="ExternalInput")
    out_h = nc.dram_tensor("out", [T, D], F32, kind="ExternalOutput")

    shg_h = nc.dram_tensor("shg", [SWG, H2], BF16, kind="Internal")
    shp_h = nc.dram_tensor("shp", [SWP, D], BF16, kind="Internal")
    wtg_h = nc.dram_tensor("wtg", [D, H2], BF16, kind="Internal")
    wtp_h = nc.dram_tensor("wtp", [HF, D], BF16, kind="Internal")
    att_h = nc.dram_tensor("attb", [T, D], F32, kind="Internal")

    kt = nc.alloc_sbuf_tensor("kt", [128, HKV * T], BF16)
    vb = nc.alloc_sbuf_tensor("vb", [128, NQT * HKV * 132], BF16)
    cos_sb = nc.alloc_sbuf_tensor("cos_sb", [128, NQT * 64], F32)
    sin_sb = nc.alloc_sbuf_tensor("sin_sb", [128, NQT * 64], F32)
    ident = nc.alloc_sbuf_tensor("ident_sb", [128, 128], BF16)
    maskc = nc.alloc_sbuf_tensor("maskc_sb", [128, 896], BF16)
    xnt = nc.alloc_sbuf_tensor("xnt", [128, 16 * Th], BF16)
    big = nc.alloc_sbuf_tensor("big", [128, 32768], BF16)
    wstr = nc.alloc_sbuf_tensor("wstr", [128, 16384], BF16)
    xf = nc.alloc_sbuf_tensor("xf", [128, 4096], F32)
    outs = nc.alloc_sbuf_tensor("outs", [128, 2048], F32)
    sl = nc.alloc_sbuf_tensor("sl", [128, 2048], BF16)
    lns = nc.alloc_sbuf_tensor("lns", [128, 16], F32)
    fss = nc.alloc_sbuf_tensor("fss", [128, 16], F32)
    qsf = nc.alloc_sbuf_tensor("qsf", [128, 64], F32)
    qsb = nc.alloc_sbuf_tensor("qsb", [128, 64], BF16)
    qtb = nc.alloc_sbuf_tensor("qtb", [128, 64], BF16)
    qtn = nc.alloc_sbuf_tensor("qtn", [128, 64], BF16)
    rss = nc.alloc_sbuf_tensor("rss", [128, 8], F32)

    ps = nc.alloc_psum_tensor("ps", [128, 4096], F32)

    def bank(i, w=512):
        return ps[:, i * 512:i * 512 + w]

    kb = KB(nc)

    WSEMS = ("wg0", "wg1", "wg2", "wg3", "wp0", "wp1", "wp2", "wp3",
             "bq0", "bq1", "bq2", "bq3", "dq0", "dq1", "dq2", "dq3", "qi0", "qi1")

    def WGRD():
        return tuple((n, kb.cnt.get(n, 0)) for n in WSEMS)

    def XGRD():
        return (("x0", kb.cnt.get("x0", 0)), ("x1", kb.cnt.get("x1", 0)),
                ("ab0", kb.cnt.get("ab0", 0)), ("ab1", kb.cnt.get("ab1", 0)))

    EOFF = 16384
    QTT = 24576
    JOFF = QTT + Th
    XOFF = JOFF + 2048
    assert XOFF + 2048 <= 32768
    WQI = (8192, 10240)
    WD1, WD2 = 12288, 14336

    BG = {}
    ST = {}
    WL = {}

    kb.block()
    kb.emit("sync", lambda e: e.dma_start(out=cos_sb[:, :], in_=cos_h.ap().rearrange("(q p) c -> p q c", p=128)), inc="ld0", dma=True)
    kb.emit("sync", lambda e: e.dma_start(out=sin_sb[:, :], in_=sin_h.ap().rearrange("(q p) c -> p q c", p=128)), inc="ld0", dma=True)
    kb.emit("sync", lambda e: e.dma_start(out=ident[:, :], in_=id_h.ap()), inc="ld0", dma=True)
    kb.emit("sync", lambda e: e.dma_start(out=maskc[:, :], in_=mk_h.ap()), inc="ld0", dma=True)
    TAB = ("ld0", 64)
    kb.emit("vector", lambda e: e.memset(vb.rearrange("p (q k c) -> p q k c", k=HKV, c=132)[:, :, :, 128:132], 1.0), inc="dv")

    def quant_chunk(src_ap, in_tok):
        ng = 32
        v3 = src_ap.rearrange("p (g c) -> p g c", c=64)
        kb.emit("vector", lambda e: e.tensor_reduce(qsf[:, 0:ng], v3, AX.X, ALU.add, apply_absolute_value=True), deps=[in_tok], inc="dv")
        kb.emit("vector", lambda e: e.tensor_scalar(qsf[:, 0:ng], qsf[:, 0:ng], 1.0 / 64.0, 1e-8, ALU.mult, ALU.max), inc="dv")
        kb.emit("vector", lambda e: e.tensor_copy(qsb[:, 0:ng], qsf[:, 0:ng]), inc="dv")
        kb.emit("vector", lambda e: e.tensor_scalar_mul(qtb[:, 0:ng], qsb[:, 0:ng], 0.5), inc="dv")
        kb.emit("vector", lambda e: e.tensor_scalar_mul(qtn[:, 0:ng], qtb[:, 0:ng], -1.0), inc="dv")
        d1 = wstr[:, WD1:WD1 + 2048]
        d2 = wstr[:, WD2:WD2 + 2048]
        tpb = qtb[:, 0:ng][:, :, None].broadcast_to([128, ng, 64])
        tnb = qtn[:, 0:ng][:, :, None].broadcast_to([128, ng, 64])
        sbv = qsb[:, 0:ng][:, :, None].broadcast_to([128, ng, 64])
        kb.emit("vector", lambda e: e.tensor_tensor(d1.rearrange("p (g c) -> p g c", c=64), v3, tpb, ALU.is_gt), deps=[ST.get("wstg")], inc="dv")
        kb.emit("vector", lambda e: e.tensor_tensor(d2.rearrange("p (g c) -> p g c", c=64), v3, tnb, ALU.is_lt), inc="dv")
        kb.emit("vector", lambda e: e.tensor_tensor(d1, d1, d2, ALU.subtract), inc="dv")
        te = kb.emit("vector", lambda e: e.tensor_tensor(v3, d1.rearrange("p (g c) -> p g c", c=64), sbv, ALU.mult), inc="dv")
        return te

    NBG = H2 // 1024
    NBP = 16 // (2048 // SWP)
    NWB = NBG + NBP

    def w_batch_emit(bi):
        slot = bi % 2
        qo = WQI[slot]
        src = wstr[:, qo:qo + 2048]
        if bi < NBG:
            b = bi
            K = SWG // 128

            def ld(e, b=b, qo=qo):
                return e.dma_start(out=wstr[:, qo:qo + 2048].rearrange("p (s c) -> p s c", c=SWG),
                                   in_=wgu_h.ap()[b * 1024:(b + 1) * 1024, :].rearrange("(s p) c -> p s c", p=128))
        else:
            b = bi - NBG
            nbp = 2048 // SWP
            K = SWP // 128

            def ld(e, b=b, qo=qo, nbp=nbp):
                return e.dma_start(out=wstr[:, qo:qo + 2048].rearrange("p (s c) -> p s c", c=SWP),
                                   in_=wpj_h.ap()[b * nbp * 128:(b + 1) * nbp * 128, :].rearrange("(s p) c -> p s c", p=128))
        tin = kb.emit("gpsimd", ld, deps=(ST.get(("wcons", slot)), WL.get(2), WL.get(3)), inc="qi%d" % slot, dma=True)
        te = quant_chunk(src, tin)
        stage = wstr[:, WD1:WD1 + 2048]
        stgv = stage.rearrange("p (k s c) -> p k s c", k=K, c=128)
        nS = 16 // K
        evs = []
        for tt in range(16):
            pb = 6 + (tt // 4) % 2
            last_mm = kb.emit("tensor", lambda e, tt=tt, pb=pb, src=src: e.matmul(
                bank(pb)[:, (tt % 4) * 128:(tt % 4) * 128 + 128],
                src[:, tt * 128:(tt + 1) * 128], ident[:, :],
                start=True, stop=True),
                deps=[te, TAB, BG.get(pb)], inc="pe")
            if tt % 4 == 3:
                tt0 = tt - 3
                if K >= 4:
                    s0, k0 = tt0 // K, tt0 % K
                    o = stgv[:, k0:k0 + 4, s0:s0 + 1, :]
                    i = bank(pb).rearrange("p (t o c) -> p t o c", t=4, o=1)
                else:
                    s0 = tt0 // K
                    o = stgv.transpose([0, 2, 1, 3])[:, s0:s0 + 2, :, :]
                    i = bank(pb).rearrange("p (s k c) -> p s k c", s=2, k=2)

                def ev(e, o=o, i=i):
                    return e.copy(o, i)
                tv = kb.emit("scalar", ev, deps=[last_mm, ST.get("wstg")], inc="ac")
                BG[pb] = tv
                evs.append(tv)
        ST[("wcons", slot)] = last_mm
        WL[2] = last_mm
        if bi < NBG:
            def wr(e, b=b, K=K, stage=stage):
                o = shg_h.ap()[:, b * 1024:(b + 1) * 1024].rearrange("(k p) h -> p k h", p=128)
                return e.dma_start(out=o, in_=stage.rearrange("p (k sc) -> p k sc", k=K))
        else:
            def wr(e, b=b, K=K, nbp=nbp, stage=stage):
                o = shp_h.ap()[:, b * nbp * 128:(b + 1) * nbp * 128].rearrange("(k p) c -> p k c", p=128)
                return e.dma_start(out=o, in_=stage.rearrange("p (k sc) -> p k sc", k=K))
        ST["wstg"] = kb.emit("sync", wr, deps=evs, inc="ws", dma=True)
        WL[3] = ST["wstg"]

    def a_qtile(half, lq):
        gq = half * HQT + lq

        def ldx(e, gq=gq):
            return e.dma_start(out=outs[:, :],
                               in_=x_h.ap()[gq * 128:(gq + 1) * 128, :])
        xtok = kb.emit("sync", ldx, deps=(ST.get("xac"), ST.get(("og", 0)), ST.get(("og", 1))), inc="xa", dma=True)
        xs = outs[:, :]
        tcp = kb.emit("scalar", lambda e, xs=xs: e.activation(big[:, JOFF:JOFF + 2048], xs, AF.Copy, accum_out=lns[:, 0:1]), deps=[xtok, ST.get("junk")], inc="ac")
        tsq = kb.emit("scalar", lambda e, xs=xs: e.activation(big[:, JOFF:JOFF + 2048], xs, AF.Square, accum_out=lns[:, 1:2]), deps=[xtok], inc="ac")
        kb.emit("vector", lambda e: e.tensor_scalar(lns[:, 2:3], lns[:, 0:1], 1.0 / D, None, ALU.mult), deps=[tcp], inc="dv")
        kb.emit("vector", lambda e: e.tensor_scalar(lns[:, 3:4], lns[:, 1:2], 1.0 / D, None, ALU.mult), deps=[tsq], inc="dv")
        kb.emit("vector", lambda e: e.tensor_tensor(lns[:, 4:5], lns[:, 2:3], lns[:, 2:3], ALU.mult), inc="dv")
        tm4 = kb.emit("vector", lambda e: e.scalar_tensor_tensor(lns[:, 5:6], lns[:, 3:4], LN_EPS, lns[:, 4:5], ALU.add, ALU.subtract), inc="dv")
        tl1 = kb.emit("scalar", lambda e: e.activation(lns[:, 6:7], lns[:, 5:6], AF.Ln), deps=[tm4], inc="ac")
        tl2 = kb.emit("scalar", lambda e: e.activation(lns[:, 7:8], lns[:, 6:7], AF.Exp, scale=-0.5), inc="ac")
        txn = kb.emit("vector", lambda e, xs=xs: e.tensor_scalar(big[:, XOFF:XOFF + 2048], xs, lns[:, 2:3], lns[:, 7:8], ALU.subtract, ALU.mult), deps=[tl2], inc="dv")
        ST["junk"] = txn
        ST["xac"] = txn
        tv = None
        for dt in range(16):
            pb = 6 + (dt // 4) % 2
            tmm = kb.emit("tensor", lambda e, dt=dt, pb=pb: e.matmul(
                bank(pb)[:, (dt % 4) * 128:(dt % 4) * 128 + 128],
                big[:, XOFF + dt * 128:XOFF + (dt + 1) * 128], ident[:, :],
                start=True, stop=True), deps=[txn, TAB, BG.get(pb)], inc="pe")
            if dt % 4 == 3:
                dt0 = dt - 3

                def evx(e, pb=pb, dt0=dt0, lq=lq):
                    o = xnt.rearrange("p (dt c) -> p dt c", dt=16)[:, dt0:dt0 + 4, lq * 128:(lq + 1) * 128]
                    return e.copy(o, bank(pb).rearrange("p (t c) -> p t c", t=4))
                tv = kb.emit("scalar", evx, deps=[tmm], inc="ac")
                BG[pb] = tv
        ST[("xnq", half, lq)] = tv
        ST[("xnt", half)] = tv

    def w_feed(dram_ap, dst_cols, width):
        par = ST.get("cstp", 0)
        ST["cstp"] = 1 - par
        stg = xf[:, 2048 + par * 1024: 2048 + par * 1024 + width]

        def ld(e, dram_ap=dram_ap, stg=stg):
            return e.dma_start(out=stg, in_=dram_ap)
        lt = kb.emit("sync", ld, deps=(ST.get(("cst", par)),
                                       ("ab0", kb.cnt.get("ab0", 0)), ("ab1", kb.cnt.get("ab1", 0)),
                                       ST.get(("ar", 0)), ST.get(("ar", 1))), inc="aw%d" % par, dma=True)
        reg = dst_cols[0] // 4096
        ct = kb.emit("scalar", lambda e, stg=stg, dst_cols=dst_cols: e.copy(wstr[:, dst_cols[0]:dst_cols[1]], stg),
                     deps=[lt, WL.get(reg)], inc="ac")
        ST[("cst", par)] = ct
        return ct

    def b_phase(half, gsel=None):
        for g in (range(NG) if gsel is None else [gsel]):
            q0 = g * GQ
            for dqc in range(2):
                for dtb in range(4):
                    s4 = (dqc * 4 + dtb) % 4

                    def ldq(e, dtb=dtb, dqc=dqc, s4=s4):
                        return e.dma_start(out=wstr[:, s4 * 4096:(s4 + 1) * 4096].rearrange("p (dt c) -> p dt c", dt=4),
                                           in_=wq_h.ap()[dtb * 512:(dtb + 1) * 512, dqc * 1024:(dqc + 1) * 1024].rearrange("(dt p) c -> p dt c", p=128))
                    awt = kb.emit("gpsimd", ldq, deps=(WL.get(s4),) + WGRD(), inc="bq%d" % s4, dma=True)
                    for dtl in range(4):
                        dt = dtb * 4 + dtl
                        for q in range(GQ):
                            for hb in range(2):
                                tok = kb.emit("tensor", lambda e, dt=dt, dtl=dtl, q=q, hb=hb, s4=s4, q0=q0: e.matmul(
                                    bank(q * 2 + hb),
                                    xnt[:, dt * Th + (q0 + q) * 128: dt * Th + (q0 + q + 1) * 128],
                                    wstr[:, s4 * 4096 + dtl * 1024 + hb * 512: s4 * 4096 + dtl * 1024 + (hb + 1) * 512],
                                    start=(dt == 0), stop=(dt == 15)),
                                    deps=[awt, ST[("xnq", half, q0 + q)]] + ([BG.get(q * 2 + hb)] if dt == 0 else []),
                                    inc="pe")
                    WL[s4] = tok
                for q in range(GQ):
                    lq = q0 + q
                    gq = half * HQT + lq
                    pv = ps[:, q * 1024:(q + 1) * 1024].rearrange("p (h x) -> p h x", h=8)
                    pv4 = ps[:, q * 1024:(q + 1) * 1024].rearrange("p (h t x) -> p h t x", h=8, t=2)
                    cosv = cos_sb[:, gq * 64:(gq + 1) * 64][:, None, None, :].broadcast_to([128, 8, 2, 64])
                    sinv = sin_sb[:, gq * 64:(gq + 1) * 64][:, None, :].broadcast_to([128, 8, 64])
                    t1v = xf[:, 0:1024].rearrange("p (h x) -> p h x", h=8)
                    t1v4 = xf[:, 0:1024].rearrange("p (h t x) -> p h t x", h=8, t=2)
                    t2v = xf[:, 1024:2048].rearrange("p (h x) -> p h x", h=8)
                    qcd = big[:, lq * 2048 + dqc * 1024: lq * 2048 + (dqc + 1) * 1024].rearrange("p (h x) -> p h x", h=8)
                    kb.emit("vector", lambda e, pv4=pv4, cosv=cosv, t1v4=t1v4: e.tensor_tensor(t1v4, pv4, cosv, ALU.mult), deps=[tok, TAB] + list(XGRD()), inc="dv")
                    kb.emit("vector", lambda e, pv=pv, sinv=sinv, t2v=t2v: e.tensor_tensor(t2v[:, :, 0:64], pv[:, :, 64:128], sinv, ALU.mult), inc="dv")
                    kb.emit("vector", lambda e, pv=pv, sinv=sinv, t2v=t2v: e.tensor_tensor(t2v[:, :, 64:128], pv[:, :, 0:64], sinv, ALU.mult), inc="dv")
                    kb.emit("vector", lambda e, qcd=qcd, t1v=t1v, t2v=t2v: e.tensor_tensor(qcd[:, :, 0:64], t1v[:, :, 0:64], t2v[:, :, 0:64], ALU.subtract), inc="dv")
                    r5 = kb.emit("vector", lambda e, qcd=qcd, t1v=t1v, t2v=t2v: e.tensor_tensor(qcd[:, :, 64:128], t1v[:, :, 64:128], t2v[:, :, 64:128], ALU.add), inc="dv")
                    BG[q * 2] = r5
                    BG[q * 2 + 1] = r5
                    ST["ropescr"] = r5
            for dtb in range(4):
                s4 = dtb % 4

                def ldk(e, dtb=dtb, s4=s4):
                    return e.dma_start(out=wstr[:, s4 * 4096:(s4 + 1) * 4096].rearrange("p (dt two c) -> p dt two c", dt=4, two=2)[:, :, 0, :],
                                       in_=wk_h.ap()[dtb * 512:(dtb + 1) * 512, :].rearrange("(dt p) c -> p dt c", p=128))

                def ldv(e, dtb=dtb, s4=s4):
                    return e.dma_start(out=wstr[:, s4 * 4096:(s4 + 1) * 4096].rearrange("p (dt two c) -> p dt two c", dt=4, two=2)[:, :, 1, :],
                                       in_=wv_h.ap()[dtb * 512:(dtb + 1) * 512, :].rearrange("(dt p) c -> p dt c", p=128))
                kb.emit("gpsimd", ldk, deps=(WL.get(s4),) + WGRD(), inc="bq%d" % s4, dma=True)
                awt = kb.emit("gpsimd", ldv, deps=(), inc="bq%d" % s4, dma=True)
                for dtl in range(4):
                    dt = dtb * 4 + dtl
                    for q in range(GQ):
                        for kvh in range(2):
                            tok = kb.emit("tensor", lambda e, dt=dt, dtl=dtl, q=q, kvh=kvh, s4=s4, q0=q0: e.matmul(
                                bank(q * 2 + kvh),
                                xnt[:, dt * Th + (q0 + q) * 128: dt * Th + (q0 + q + 1) * 128],
                                wstr[:, s4 * 4096 + dtl * 1024 + kvh * 512: s4 * 4096 + dtl * 1024 + (kvh + 1) * 512],
                                start=(dt == 0), stop=(dt == 15)),
                                deps=[awt] + ([BG.get(q * 2 + kvh)] if dt == 0 else []),
                                inc="pe")
                WL[s4] = tok
            for q in range(GQ):
                lq = q0 + q
                gq = half * HQT + lq
                pkv = bank(q * 2).rearrange("p (h x) -> p h x", h=HKV)
                pk4 = bank(q * 2).rearrange("p (h t x) -> p h t x", h=HKV, t=2)
                cosv = cos_sb[:, gq * 64:(gq + 1) * 64][:, None, None, :].broadcast_to([128, HKV, 2, 64])
                sinv = sin_sb[:, gq * 64:(gq + 1) * 64][:, None, :].broadcast_to([128, HKV, 64])
                t1v = xf[:, 0:512].rearrange("p (h x) -> p h x", h=HKV)
                t1v4 = xf[:, 0:512].rearrange("p (h t x) -> p h t x", h=HKV, t=2)
                t2v = xf[:, 1024:1536].rearrange("p (h x) -> p h x", h=HKV)
                kdst = sl[:, 0:512].rearrange("p (h x) -> p h x", h=HKV)
                kb.emit("vector", lambda e, pk4=pk4, cosv=cosv, t1v4=t1v4: e.tensor_tensor(t1v4, pk4, cosv, ALU.mult), deps=[tok] + list(XGRD()), inc="dv")
                kb.emit("vector", lambda e, pkv=pkv, sinv=sinv, t2v=t2v: e.tensor_tensor(t2v[:, :, 0:64], pkv[:, :, 64:128], sinv, ALU.mult), inc="dv")
                r3 = kb.emit("vector", lambda e, pkv=pkv, sinv=sinv, t2v=t2v: e.tensor_tensor(t2v[:, :, 64:128], pkv[:, :, 0:64], sinv, ALU.mult), inc="dv")
                kb.emit("vector", lambda e, kdst=kdst, t1v=t1v, t2v=t2v: e.tensor_tensor(kdst[:, :, 0:64], t1v[:, :, 0:64], t2v[:, :, 0:64], ALU.subtract), deps=[ST.get("ktm")], inc="dv")
                r5 = kb.emit("vector", lambda e, kdst=kdst, t1v=t1v, t2v=t2v: e.tensor_tensor(kdst[:, :, 64:128], t1v[:, :, 64:128], t2v[:, :, 64:128], ALU.add), inc="dv")
                ST["ropescr"] = r5
                vdst = vb[:, gq * HKV * 132: (gq + 1) * HKV * 132].rearrange("p (h x) -> p h x", x=132)[:, :, 0:128]
                tvv = kb.emit("scalar", lambda e, vdst=vdst, q=q: e.copy(vdst, bank(q * 2 + 1).rearrange("p (h x) -> p h x", h=HKV)), deps=[tok], inc="ac")
                BG[q * 2 + 1] = tvv
                ST["vbw"] = tvv
                last_mm = None
                for kv in range(HKV):
                    last_mm = kb.emit("tensor", lambda e, kv=kv, q=q: e.matmul(
                        bank(q * 2)[:, kv * 128:(kv + 1) * 128],
                        sl[:, kv * 128:(kv + 1) * 128], ident[:, :],
                        start=True, stop=True), deps=[r5, r3], inc="pe")

                def evk(e, gq=gq, q=q):
                    o = kt.rearrange("p (k t) -> p k t", k=HKV)[:, :, gq * 128:(gq + 1) * 128]
                    return e.tensor_copy(o, bank(q * 2).rearrange("p (k c) -> p k c", k=HKV))
                tke = kb.emit("vector", evk, deps=[last_mm], inc="dv")
                BG[q * 2] = tke
                ST["ktm"] = tke
        ST[("bdone", half)] = ST["ktm"]

    def c_phase(half):
        e_gate = {}
        scores_last = None
        bdone = ST[("bdone", half)]
        vbw = ST["vbw"]
        for head in range(HQ):
            kv = head % HKV
            ev_last = None
            for lq0 in range(0, HQT, 4):
                n = min(4, HQT - lq0)
                pb = 6 + (lq0 // 4) % 2
                last_mm = None
                for i in range(n):
                    lq = lq0 + i
                    last_mm = kb.emit("tensor", lambda e, lq=lq, head=head, pb=pb, i=i: e.matmul(
                        bank(pb)[:, i * 128:(i + 1) * 128],
                        big[:, lq * 2048 + head * 128: lq * 2048 + (head + 1) * 128], ident[:, :],
                        start=True, stop=True),
                        deps=[BG.get(pb), ST.get(("ctx", head - 1, lq)), bdone, TAB], inc="pe")

                def evq(e, pb=pb, lq0=lq0, n=n):
                    return e.tensor_copy(big[:, QTT + lq0 * 128: QTT + (lq0 + n) * 128], bank(pb)[:, 0:n * 128])
                ev_last = kb.emit("vector", evq, deps=[last_mm, scores_last], inc="dv")
                BG[pb] = ev_last
            for qc in range(NQC):
                gqc = half * NQC + qc
                nkt = (gqc + 1) * QCW // 128
                kt0_cross = gqc * QCW // 128
                for ktile in range(nkt):
                    pb = ktile % 4
                    tmm = kb.emit("tensor", lambda e, ktile=ktile, kv=kv, qc=qc, pb=pb: e.matmul(
                        bank(pb, QCW),
                        kt[:, kv * T + ktile * 128: kv * T + (ktile + 1) * 128],
                        big[:, QTT + qc * QCW: QTT + (qc + 1) * QCW],
                        start=True, stop=True),
                        deps=[ev_last, BG.get(pb)], inc="pe")
                    scores_last = tmm
                    if ktile % 2 == 1 or ktile == nkt - 1:
                        k0 = ktile - (ktile % 2)
                        n2 = ktile - k0 + 1
                        tex = kb.emit("scalar", lambda e, k0=k0, n2=n2: e.activation(
                            big[:, EOFF + k0 * QCW: EOFF + (k0 + n2) * QCW].rearrange("p (b c) -> p b c", b=n2),
                            ps.rearrange("p (b c) -> p b c", b=8)[:, k0 % 4:k0 % 4 + n2, 0:QCW], AF.Exp, scale=SM_SCALE),
                            deps=[tmm, e_gate.get(("e", k0)), e_gate.get(("e", k0 + n2 - 1))], inc="ac")
                        for kk in range(k0, k0 + n2):
                            BG[kk % 4] = tex
                            etok = tex
                            if kk >= kt0_cross:
                                o_rel = (kk - kt0_cross) * 128
                                etok = kb.emit("vector", lambda e, kk=kk, o_rel=o_rel: e.tensor_tensor(
                                    big[:, EOFF + kk * QCW: EOFF + (kk + 1) * QCW],
                                    big[:, EOFF + kk * QCW: EOFF + (kk + 1) * QCW],
                                    maskc[:, 384 - o_rel: 384 - o_rel + QCW], ALU.mult), deps=[etok, TAB], inc="dv")
                            e_gate[("m", kk)] = etok
                for ql in range(QCW // 128):
                    lq = qc * (QCW // 128) + ql
                    gq = half * HQT + lq
                    cb = 4 + ql % 2
                    for ktile in range(gq + 1):
                        tok = kb.emit("tensor", lambda e, ktile=ktile, ql=ql, kv=kv, gq=gq, cb=cb, qc=qc: e.matmul(
                            bank(cb)[:, 0:129],
                            big[:, EOFF + ktile * QCW + ql * 128: EOFF + ktile * QCW + (ql + 1) * 128],
                            vb[:, ktile * HKV * 132 + kv * 132: ktile * HKV * 132 + kv * 132 + 129],
                            start=(ktile == 0), stop=(ktile == gq)),
                            deps=[e_gate.get(("m", ktile))] + ([BG.get(cb), vbw] if ktile == 0 else []),
                            inc="pe")
                    kb.emit("vector", lambda e, cb=cb, ql=ql: e.reciprocal(rss[:, ql:ql + 1], bank(cb)[:, 128:129]), deps=[tok], inc="dv")
                    tcx = kb.emit("vector", lambda e, cb=cb, ql=ql, lq=lq, head=head: e.tensor_scalar(
                        big[:, lq * 2048 + head * 128: lq * 2048 + (head + 1) * 128],
                        bank(cb)[:, 0:128], rss[:, ql:ql + 1], None, ALU.mult), inc="dv")
                    BG[cb] = tcx
                    ST[("ctx", head, lq)] = tcx
                lastq = qc * (QCW // 128) + (QCW // 128) - 1
                for ktile in range(nkt):
                    e_gate[("e", ktile)] = ST[("ctx", head, lastq)]

    def d_phase(half):
        for g in range(NG):
            q0 = g * GQ
            ct_done = []
            for q in range(GQ):
                lq = q0 + q
                for dt in range(16):
                    pb = 6 + (dt // 4) % 2
                    tmm = kb.emit("tensor", lambda e, lq=lq, dt=dt, pb=pb: e.matmul(
                        bank(pb)[:, (dt % 4) * 128:(dt % 4) * 128 + 128],
                        big[:, lq * 2048 + dt * 128: lq * 2048 + (dt + 1) * 128], ident[:, :],
                        start=True, stop=True),
                        deps=[BG.get(pb), ST.get(("ctx", HQ - 1, lq)), TAB], inc="pe")
                    if dt % 4 == 3:
                        dt0 = dt - 3

                        def evc(e, pb=pb, dt0=dt0, q=q):
                            return e.tensor_copy(big[:, EOFF + q * 2048 + dt0 * 128: EOFF + q * 2048 + (dt0 + 4) * 128], bank(pb))
                        tv = kb.emit("vector", evc, deps=[tmm, ST.get(("ctc", q))], inc="dv")
                        BG[pb] = tv
                        ct_done.append(tv)
            for dc in range(2):
                for dtb in range(4):
                    s4 = (dc * 4 + dtb) % 4
                    if half == 1:
                        def ldwo(e, dtb=dtb, dc=dc, s4=s4):
                            return e.dma_start(out=wstr[:, s4 * 4096:(s4 + 1) * 4096].rearrange("p (dt c) -> p dt c", dt=4),
                                               in_=wo_h.ap()[dtb * 512:(dtb + 1) * 512, dc * 1024:(dc + 1) * 1024].rearrange("(dt p) c -> p dt c", p=128))
                        wot = kb.emit("gpsimd", ldwo, deps=(WL.get(s4),) + WGRD(), inc="dq%d" % s4, dma=True)
                        ctoks = [wot] * 4
                    else:
                        ctoks = []
                        for dtl in range(4):
                            dt = dtb * 4 + dtl
                            ctoks.append(w_feed(wo_h.ap()[dt * 128:(dt + 1) * 128, dc * 1024:(dc + 1) * 1024],
                                                (s4 * 4096 + dtl * 1024, s4 * 4096 + dtl * 1024 + 1024), 1024))
                    for dtl in range(4):
                        dt = dtb * 4 + dtl
                        for q in range(GQ):
                            for hb in range(2):
                                tok = kb.emit("tensor", lambda e, dt=dt, dtl=dtl, q=q, hb=hb, s4=s4: e.matmul(
                                    bank(q * 2 + hb),
                                    big[:, EOFF + q * 2048 + dt * 128: EOFF + q * 2048 + (dt + 1) * 128],
                                    wstr[:, s4 * 4096 + dtl * 1024 + hb * 512: s4 * 4096 + dtl * 1024 + (hb + 1) * 512],
                                    start=(dt == 0), stop=(dt == 15)),
                                    deps=[ctoks[dtl]] + (ct_done + [BG.get(q * 2 + hb)] if dt == 0 else []),
                                    inc="pe")
                    WL[s4] = tok
                for q in range(GQ):
                    gq = half * HQT + q0 + q
                    s2 = q % 2
                    o2 = outs[:, s2 * 1024:(s2 + 1) * 1024]
                    tv = kb.emit("vector", lambda e, q=q, o2=o2: e.tensor_copy(o2, ps[:, q * 1024:(q + 1) * 1024]),
                                 deps=[tok, ST.get(("og", s2))], inc="dv")
                    BG[q * 2] = tv
                    BG[q * 2 + 1] = tv
                    if dc == 1:
                        ST[("ctc", q)] = tv

                    def sta(e, gq=gq, dc=dc, o2=o2):
                        return e.dma_start(out=att_h.ap()[gq * 128:(gq + 1) * 128, dc * 1024:(dc + 1) * 1024], in_=o2)
                    ST[("og", s2)] = kb.emit("sync", sta, deps=[tv], inc="fs%d" % s2, dma=True)

    def f_phase(half):
        WSD = ("cc", 1)
        WSDP = ("cc", 2)
        FSD = (("fs0", kb.cnt.get("fs0", 0)), ("fs1", kb.cnt.get("fs1", 0)))
        for tb in range(NTB):
            q0 = tb * TBQ
            ev_gate = {}
            h_tok = {}
            tok = None
            for j in range(NJ):
                for gu in range(2):
                    p2 = (2 * j + gu) % 2
                    sA, sB = 2 * p2, 2 * p2 + 1
                    hcol = gu * HF + j * 512

                    def ldw1(e, hcol=hcol, sA=sA):
                        return e.dma_start(out=wstr[:, sA * 4096:(sA + 1) * 4096].rearrange("p (dt c) -> p dt c", dt=8),
                                           in_=wtg_h.ap()[0:1024, hcol:hcol + 512].rearrange("(dt p) c -> p dt c", p=128))

                    def ldw2(e, hcol=hcol, sB=sB):
                        return e.dma_start(out=wstr[:, sB * 4096:(sB + 1) * 4096].rearrange("p (dt c) -> p dt c", dt=8),
                                           in_=wtg_h.ap()[1024:2048, hcol:hcol + 512].rearrange("(dt p) c -> p dt c", p=128))
                    wgA = kb.emit("sync", ldw1, deps=(WL.get(sA), WSD, ("ws", kb.cnt.get("ws", 0))) + WGRD(), inc="wg%d" % sA, dma=True)
                    wgB = kb.emit("scalar", ldw2, deps=(WL.get(sB), WSD, ("ws", kb.cnt.get("ws", 0))) + WGRD(), inc="wg%d" % sB, dma=True)
                    for dt in range(16):
                        sl_i = sA if dt < 8 else sB
                        wt = wgA if dt < 8 else wgB
                        for q in range(TBQ):
                            bnk = q * 2 + gu
                            tok = kb.emit("tensor", lambda e, dt=dt, q=q, bnk=bnk, sl_i=sl_i, q0=q0: e.matmul(
                                bank(bnk),
                                xnt[:, dt * Th + (q0 + q) * 128: dt * Th + (q0 + q + 1) * 128],
                                wstr[:, sl_i * 4096 + (dt % 8) * 512: sl_i * 4096 + (dt % 8 + 1) * 512],
                                start=(dt == 0), stop=(dt == 15)),
                                deps=[wt] + ([BG.get(bnk), ST.get(("xnq", half, q0 + q))] if dt == 0 else []),
                                inc="pe")
                        if dt == 7:
                            WL[sA] = tok
                    WL[sB] = tok
                for q in range(TBQ):
                    tsl = kb.emit("scalar", lambda e, q=q: e.activation(sl[:, q * 512:(q + 1) * 512], bank(q * 2), AF.Silu), deps=[tok, ev_gate.get(("sl", q))], inc="ac")
                    th = kb.emit("vector", lambda e, q=q, j=j: e.tensor_tensor(
                        big[:, q * HF + j * 512: q * HF + (j + 1) * 512],
                        sl[:, q * 512:(q + 1) * 512], bank(q * 2 + 1), ALU.mult), deps=[tsl, tok], inc="dv")
                    BG[q * 2] = tsl
                    BG[q * 2 + 1] = th
                    ev_gate[("sl", q)] = th
                    h_tok[(q, j)] = th
            hnt_done = []
            for q in range(TBQ):
                hq = big[:, q * HF:(q + 1) * HF]
                deps_h = [h_tok[(q, j)] for j in range(NJ)]
                tsq = kb.emit("scalar", lambda e, hq=hq, q=q: e.activation(wstr[:, 0:HF], hq, AF.Square, accum_out=fss[:, q:q + 1]),
                              deps=deps_h + [ST.get("fjunk")] + [WL.get(i) for i in range(4) if i * 4096 < HF], inc="ac")
                ST["fjunk"] = tsq
                for i in range(4):
                    if i * 4096 < HF:
                        WL[i] = tsq
                tms = kb.emit("vector", lambda e, q=q: e.tensor_scalar(fss[:, 4 + q:5 + q], fss[:, q:q + 1], 1.0 / HF, RMS_EPS, ALU.mult, ALU.add), deps=[tsq], inc="dv")
                kb.emit("scalar", lambda e, q=q: e.activation(fss[:, 4 + q:5 + q], fss[:, 4 + q:5 + q], AF.Ln), deps=[tms], inc="ac")
                ti = kb.emit("scalar", lambda e, q=q: e.activation(fss[:, 8 + q:9 + q], fss[:, 4 + q:5 + q], AF.Exp, scale=-0.5), inc="ac")
                thn = kb.emit("vector", lambda e, hq=hq, q=q: e.tensor_scalar(hq, hq, fss[:, 8 + q:9 + q], None, ALU.mult), deps=[ti], inc="dv")
                for ht in range(NHT):
                    pb = 6 + (ht // 4) % 2
                    tmm = kb.emit("tensor", lambda e, hq=hq, ht=ht, pb=pb: e.matmul(
                        bank(pb)[:, (ht % 4) * 128:(ht % 4) * 128 + 128],
                        hq[:, ht * 128:(ht + 1) * 128], ident[:, :],
                        start=True, stop=True),
                        deps=[thn, BG.get(pb)], inc="pe")
                    if ht % 4 == 3:
                        ht0 = ht - 3

                        def evh(e, pb=pb, ht0=ht0, hq=hq):
                            return e.tensor_copy(hq[:, ht0 * 128:(ht0 + 4) * 128], bank(pb))
                        tv = kb.emit("vector", evh, deps=[tmm], inc="dv")
                        BG[pb] = tv
                        hnt_done.append(tv)
            xtok = {}
            attok = {}
            NHG = NHT // 4
            for dc in range(2):
                for q in range(min(2, TBQ)):
                    gq = half * HQT + q0 + q

                    def ldx2(e, gq=gq, q=q, dc=dc):
                        return e.dma_start(out=xf[:, (q % 2) * 1024:(q % 2 + 1) * 1024],
                                           in_=x_h.ap()[gq * 128:(gq + 1) * 128, dc * 1024:(dc + 1) * 1024])

                    def lda(e, gq=gq, q=q, dc=dc):
                        return e.dma_start(out=xf[:, 2048 + (q % 2) * 1024: 2048 + (q % 2 + 1) * 1024],
                                           in_=att_h.ap()[gq * 128:(gq + 1) * 128, dc * 1024:(dc + 1) * 1024])
                    xtok[q] = kb.emit("sync", ldx2, deps=(ST.get("ropescr"), ST.get(("xr", q % 2))), inc="x%d" % (q % 2), dma=True)
                    attok[q] = kb.emit("sync", lda, deps=(ST.get(("ar", q % 2)), ST.get(("cst", 0)), ST.get(("cst", 1))) + FSD, inc="ab%d" % (q % 2), dma=True)
                for hg in range(NHG):
                    s4 = (dc * NHG + hg) % 4

                    def ldp2(e, hg=hg, dc=dc, s4=s4):
                        return e.dma_start(out=wstr[:, s4 * 4096:(s4 + 1) * 4096].rearrange("p (ht c) -> p ht c", ht=4),
                                           in_=wtp_h.ap()[hg * 512:(hg + 1) * 512, dc * 1024:(dc + 1) * 1024].rearrange("(ht p) c -> p ht c", p=128))
                    wpt = kb.emit("sync" if s4 % 2 == 0 else "scalar", ldp2, deps=(WL.get(s4), WSDP) + WGRD(), inc="wp%d" % s4, dma=True)
                    for htl in range(4):
                        ht = hg * 4 + htl
                        for q in range(TBQ):
                            for hb in range(2):
                                tok = kb.emit("tensor", lambda e, htl=htl, ht=ht, q=q, hb=hb, s4=s4: e.matmul(
                                    bank(q * 2 + hb),
                                    big[:, q * HF + ht * 128: q * HF + (ht + 1) * 128],
                                    wstr[:, s4 * 4096 + htl * 1024 + hb * 512: s4 * 4096 + htl * 1024 + (hb + 1) * 512],
                                    start=(ht == 0), stop=(ht == NHT - 1)),
                                    deps=[wpt] + (hnt_done + [BG.get(q * 2 + hb)] if ht == 0 else []),
                                    inc="pe")
                    WL[s4] = tok
                for q in range(TBQ):
                    gq = half * HQT + q0 + q
                    s2 = q % 2
                    o2 = outs[:, s2 * 1024:(s2 + 1) * 1024]
                    ta = kb.emit("vector", lambda e, s2=s2, q=q, o2=o2: e.tensor_tensor(
                        o2, ps[:, q * 1024:(q + 1) * 1024],
                        xf[:, s2 * 1024:(s2 + 1) * 1024], ALU.add),
                        deps=[tok, xtok[q], ST.get(("og", s2))], inc="dv")
                    tb2 = kb.emit("vector", lambda e, s2=s2, o2=o2: e.tensor_tensor(
                        o2, o2, xf[:, 2048 + s2 * 1024: 2048 + (s2 + 1) * 1024], ALU.add),
                        deps=[attok[q]], inc="dv")
                    BG[q * 2] = tb2
                    BG[q * 2 + 1] = tb2
                    ST[("xr", s2)] = ta
                    ST[("ar", s2)] = tb2

                    def sto(e, gq=gq, dc=dc, o2=o2):
                        return e.dma_start(out=out_h.ap()[gq * 128:(gq + 1) * 128, dc * 1024:(dc + 1) * 1024], in_=o2)
                    ST[("og", s2)] = kb.emit("scalar", sto, deps=[tb2], inc="os%d" % s2, dma=True)
                    if q + 2 < TBQ:
                        gq2 = half * HQT + q0 + q + 2

                        def ldx3(e, gq2=gq2, s2=s2, dc=dc):
                            return e.dma_start(out=xf[:, s2 * 1024:(s2 + 1) * 1024],
                                               in_=x_h.ap()[gq2 * 128:(gq2 + 1) * 128, dc * 1024:(dc + 1) * 1024])

                        def lda3(e, gq2=gq2, s2=s2, dc=dc):
                            return e.dma_start(out=xf[:, 2048 + s2 * 1024: 2048 + (s2 + 1) * 1024],
                                               in_=att_h.ap()[gq2 * 128:(gq2 + 1) * 128, dc * 1024:(dc + 1) * 1024])
                        xtok[q + 2] = kb.emit("sync", ldx3, deps=(ta,), inc="x%d" % s2, dma=True)
                        attok[q + 2] = kb.emit("sync", lda3, deps=(tb2,) + FSD, inc="ab%d" % s2, dma=True)

    per = max(1, (NWB + HQT - 1) // HQT)
    RG = [[0, 1, 2, 3, 4, 5, 6, 7]]
    kb.sem("cc")
    kb.dma_sems.add("cc")
    wi = 0
    for g in range(NG):
        for lq in range(g * GQ, (g + 1) * GQ):
            for _ in range(per):
                if wi < NWB:
                    w_batch_emit(wi)
                    wi += 1
            a_qtile(0, lq)
        if g == NG - 1:
            while wi < NWB:
                w_batch_emit(wi)
                wi += 1
        b_phase(0, gsel=g)
    kb.emit("gpsimd", lambda e: e.collective_compute("AllGather", ALU.bypass, RG, [shg_h.ap().opt()], [wtg_h.ap().opt()]), deps=[("ws", kb.cnt["ws"]), ("pe", kb.cnt["pe"])], inc="cc")
    kb.emit("gpsimd", lambda e: e.collective_compute("AllGather", ALU.bypass, RG, [shp_h.ap().opt()], [wtp_h.ap().opt()]), inc="cc")
    c_phase(0)
    d_phase(0)
    f_phase(0)
    kb.block()
    for lq in range(HQT):
        a_qtile(1, lq)
    b_phase(1)
    c_phase(1)
    d_phase(1)
    f_phase(1)
    kb.block()
    kb.emit("sync", lambda e: e.nop(), deps=[("os0", kb.cnt["os0"]), ("os1", kb.cnt.get("os1", 0)),
                                             ("fs0", kb.cnt.get("fs0", 0)), ("fs1", kb.cnt.get("fs1", 0)),
                                             ("cc", 2), ("ws", kb.cnt["ws"])])
    kb.realize()
    return nc


_NC_CACHE = {}


def _get_nc(T, HF):
    key = (T, HF)
    if key not in _NC_CACHE:
        _NC_CACHE[key] = build_nc(T, HF)
    return _NC_CACHE[key]


def _host_tables(T):
    bf = ml_dtypes.bfloat16
    inv = 1.0 / (10000.0 ** (np.arange(0, DH, 2, dtype=np.float64) / DH))
    ang = np.arange(T, dtype=np.float64)[:, None] * inv[None, :]
    cos = np.cos(ang).astype(np.float32)
    sin = np.sin(ang).astype(np.float32)
    costab = np.ascontiguousarray(cos)
    sintab = np.ascontiguousarray(sin)
    ident = np.eye(128, dtype=bf)
    k_i = np.arange(128)[:, None]
    t_i = np.arange(896)[None, :]
    maskc = ((t_i - 384) >= k_i).astype(bf)
    return costab, sintab, ident, maskc


def _deinterleave_cols(w, block=128):
    d_out = w.shape[1]
    perm = []
    for b0 in range(0, d_out, block):
        perm.extend(range(b0, b0 + block, 2))
        perm.extend(range(b0 + 1, b0 + block, 2))
    return np.ascontiguousarray(w[:, perm])


def _permute_wo_rows(wo):
    rows = []
    for h in range(HQ):
        src = 4 * (h % 4) + h // 4
        rows.extend(range(src * 128, (src + 1) * 128))
    return np.ascontiguousarray(wo[rows, :])


def kernel(L0_attn_x, L0_attn_Wq, L0_attn_Wk, L0_attn_Wv, L0_attn_Wo,
           L0_ffn_w_gate_up, L0_ffn_w_proj):
    x = np.ascontiguousarray(np.asarray(L0_attn_x, dtype=np.float32))
    wq = _deinterleave_cols(np.asarray(L0_attn_Wq, dtype=np.float32))
    wk = _deinterleave_cols(np.asarray(L0_attn_Wk, dtype=np.float32))
    wv = np.ascontiguousarray(np.asarray(L0_attn_Wv, dtype=np.float32))
    wo = _permute_wo_rows(np.asarray(L0_attn_Wo, dtype=np.float32))
    wgu = np.ascontiguousarray(np.asarray(L0_ffn_w_gate_up, dtype=np.float32))
    wpj = np.ascontiguousarray(np.asarray(L0_ffn_w_proj, dtype=np.float32))
    B, T, _ = x.shape
    HF = wpj.shape[1]
    costab, sintab, ident, maskc = _host_tables(T)
    nc = _get_nc(T, HF)
    swg, swp = D // 8, HF // 8
    in_maps = [dict(x=x[b], wq=wq, wk=wk, wv=wv, wo=wo,
                    wgu_s=np.ascontiguousarray(wgu[:, b * swg:(b + 1) * swg]),
                    wproj_s=np.ascontiguousarray(wpj[:, b * swp:(b + 1) * swp]),
                    costab=costab, sintab=sintab, ident=ident, maskc=maskc)
               for b in range(B)]
    res = run_bass_kernel_spmd(nc, in_maps, core_ids=list(range(B)))
    return np.stack([np.asarray(res.results[b]["out"]) for b in range(B)], axis=0)


# version 17 (final)
# speedup vs baseline: 1.0697x; 1.0175x over previous
import math
import numpy as np
import ml_dtypes

import concourse.bass as bass
import concourse.mybir as mybir
import concourse.bass_utils as _bu
from concourse.bass_utils import run_bass_kernel_spmd

if not getattr(_bu, "_ldw_patched", False):
    _orig_run_command = _bu.run_command

    def _run_command_ldw(argv, **kw):
        argv = [a.replace("--enable-ldw-opt=false", "--enable-ldw-opt=true")
                if isinstance(a, str) else a for a in argv]
        return _orig_run_command(argv, **kw)

    _bu.run_command = _run_command_ldw
    _bu._ldw_patched = True

F32 = mybir.dt.float32
BF16 = mybir.dt.bfloat16
AF = mybir.ActivationFunctionType
ALU = mybir.AluOpType
AX = mybir.AxisListType

D = 2048
HQ, HKV, DH = 16, 4, 128
KVW = HKV * DH
LN_EPS = 1e-5
RMS_EPS = float(np.finfo(np.float32).eps)
SM_SCALE = 1.0 / math.sqrt(DH)

ENGS = ("tensor", "vector", "scalar", "gpsimd", "sync")
CHAIN = ("vector", "scalar")


class KB:

    def __init__(self, nc):
        self.nc = nc
        self.blocks = []
        self.cur = None
        self.sems = {}
        self.cnt = {}
        self.hw = None
        self.block_start = None
        self.dma_sems = set()
        self.last = {}

    def sem(self, name, dma=False):
        if name not in self.sems:
            self.sems[name] = self.nc.alloc_semaphore(name)
            self.cnt[name] = 0
            if dma:
                self.dma_sems.add(name)
        return self.sems[name]

    def block(self):
        self.cur = {e: [] for e in ENGS}
        self.blocks.append(self.cur)
        self.hw = {}
        self.block_start = {s: self.cnt[s] for s in self.cnt if s not in self.dma_sems}

    def emit(self, eng, fn, deps=(), inc=None, dma=False):
        if dma and inc is not None:
            self.sem(inc, dma=True)
            deps = tuple(deps) + ((inc, self.cnt[inc]),)
        if eng in CHAIN and not dma:
            deps = tuple(deps) + (self.last.get(eng),)
            assert inc is not None
        if inc is not None:
            self.sem(inc, dma=dma)
        waits = []
        for d in deps:
            if d is None:
                continue
            s, v = d
            self.sem(s)
            if v <= 0 or v <= self.block_start.get(s, 0):
                continue
            k = (eng, s)
            if self.hw.get(k, 0) >= v:
                continue
            self.hw[k] = v
            waits.append((s, v))
        tok = None
        if inc is not None:
            self.cnt[inc] += 16 if dma else 1
            tok = (inc, self.cnt[inc])
            if eng in CHAIN:
                self.last[eng] = tok
        self.cur[eng].append((waits, fn, inc, 16 if dma else 1))
        return tok

    def realize(self):
        for blk in self.blocks:
            with self.nc.Block() as block:
                for eng in ENGS:
                    items = blk[eng]
                    if not items:
                        continue

                    def make_body(items=items):
                        def body(e):
                            for waits, fn, inc, by in items:
                                for (s, v) in waits:
                                    e.wait_ge(self.sems[s], v)
                                ins = fn(e)
                                if inc is not None:
                                    ins.then_inc(self.sems[inc], by)
                        return body

                    getattr(block, eng)(make_body())


def build_nc(T=2048, HF=8192):
    H2 = 2 * HF
    NQT = T // 128
    HQT = NQT // 2
    Th = T // 2
    TBQ = min(4, HQT)
    NTB = HQT // TBQ
    GQ = min(4, HQT)
    NG = HQT // GQ
    QCW = min(512, Th)
    NQC = Th // QCW
    NHT = HF // 128
    NJ = HF // 512
    SWG = D // 8
    SWP = HF // 8

    nc = bass.Bass("TRN2", target_bir_lowering=False)

    x_h = nc.dram_tensor("x", [T, D], F32, kind="ExternalInput")
    wq_h = nc.dram_tensor("wq", [D, D], F32, kind="ExternalInput")
    wk_h = nc.dram_tensor("wk", [D, KVW], F32, kind="ExternalInput")
    wv_h = nc.dram_tensor("wv", [D, KVW], F32, kind="ExternalInput")
    wo_h = nc.dram_tensor("wo", [D, D], F32, kind="ExternalInput")
    wgu_h = nc.dram_tensor("wgu_s", [H2, SWG], F32, kind="ExternalInput")
    wpj_h = nc.dram_tensor("wproj_s", [D, SWP], F32, kind="ExternalInput")
    cos_h = nc.dram_tensor("costab", [T, 64], F32, kind="ExternalInput")
    sin_h = nc.dram_tensor("sintab", [T, 64], F32, kind="ExternalInput")
    id_h = nc.dram_tensor("ident", [128, 128], BF16, kind="ExternalInput")
    mk_h = nc.dram_tensor("maskc", [128, 128], BF16, kind="ExternalInput")
    out_h = nc.dram_tensor("out", [T, D], F32, kind="ExternalOutput")

    shg_h = nc.dram_tensor("shg", [SWG, H2], BF16, kind="Internal")
    shp_h = nc.dram_tensor("shp", [SWP, D], BF16, kind="Internal")
    wtg_h = nc.dram_tensor("wtg", [D, H2], BF16, kind="Internal")
    wtp_h = nc.dram_tensor("wtp", [HF, D], BF16, kind="Internal")
    att_h = nc.dram_tensor("attb", [T, D], F32, kind="Internal")

    kt = nc.alloc_sbuf_tensor("kt", [128, HKV * T], BF16)
    vb = nc.alloc_sbuf_tensor("vb", [128, NQT * HKV * 132], BF16)
    cos_sb = nc.alloc_sbuf_tensor("cos_sb", [128, NQT * 64], F32)
    sin_sb = nc.alloc_sbuf_tensor("sin_sb", [128, NQT * 64], F32)
    ident = nc.alloc_sbuf_tensor("ident_sb", [128, 128], BF16)
    maskc = nc.alloc_sbuf_tensor("maskc_sb", [128, 128], BF16)
    xnt = nc.alloc_sbuf_tensor("xnt", [128, 16 * Th], BF16)
    qtb_sb = nc.alloc_sbuf_tensor("qtb_sb", [128, 4 * Th], BF16)
    big = nc.alloc_sbuf_tensor("big", [128, 32768], BF16)
    wstr = nc.alloc_sbuf_tensor("wstr", [128, 16384], BF16)
    xf = nc.alloc_sbuf_tensor("xf", [128, 4096], F32)
    outs = nc.alloc_sbuf_tensor("outs", [128, 2048], F32)
    sl = nc.alloc_sbuf_tensor("sl", [128, 2048], BF16)
    lns = nc.alloc_sbuf_tensor("lns", [128, 16], F32)
    fss = nc.alloc_sbuf_tensor("fss", [128, 16], F32)
    qsf = nc.alloc_sbuf_tensor("qsf", [128, 64], F32)
    qsb = nc.alloc_sbuf_tensor("qsb", [128, 64], BF16)
    qtb = nc.alloc_sbuf_tensor("qtb", [128, 64], BF16)
    qtn = nc.alloc_sbuf_tensor("qtn", [128, 64], BF16)
    rss = nc.alloc_sbuf_tensor("rss", [128, 8], F32)

    ps = nc.alloc_psum_tensor("ps", [128, 4096], F32)

    def bank(i, w=512):
        return ps[:, i * 512:i * 512 + w]

    kb = KB(nc)

    WSEMS = ("wg0", "wg1", "wg2", "wg3", "wp0", "wp1", "wp2", "wp3",
             "bq0", "bq1", "bq2", "bq3", "dq0", "dq1", "dq2", "dq3", "qi0", "qi1")

    def WGRD():
        return tuple((n, kb.cnt.get(n, 0)) for n in WSEMS)

    def XGRD():
        return (("x0", kb.cnt.get("x0", 0)), ("x1", kb.cnt.get("x1", 0)),
                ("ab0", kb.cnt.get("ab0", 0)), ("ab1", kb.cnt.get("ab1", 0)))

    EOFF = 16384
    QTT = 24576
    JOFF = QTT + Th
    XOFF = JOFF + 2048
    assert XOFF + 2048 <= 32768
    WQI = (8192, 10240)
    WD1, WD2 = 12288, 14336

    BG = {}
    ST = {}
    WL = {}

    kb.block()
    kb.emit("sync", lambda e: e.dma_start(out=cos_sb[:, :], in_=cos_h.ap().rearrange("(q p) c -> p q c", p=128)), inc="ld0", dma=True)
    kb.emit("sync", lambda e: e.dma_start(out=sin_sb[:, :], in_=sin_h.ap().rearrange("(q p) c -> p q c", p=128)), inc="ld0", dma=True)
    kb.emit("sync", lambda e: e.dma_start(out=ident[:, :], in_=id_h.ap()), inc="ld0", dma=True)
    kb.emit("sync", lambda e: e.dma_start(out=maskc[:, :], in_=mk_h.ap()), inc="ld0", dma=True)
    TAB = ("ld0", 64)
    kb.emit("vector", lambda e: e.memset(vb.rearrange("p (q k c) -> p q k c", k=HKV, c=132)[:, :, :, 128:132], 1.0), inc="dv")

    def quant_chunk(src_ap, in_tok):
        ng = 32
        v3 = src_ap.rearrange("p (g c) -> p g c", c=64)
        kb.emit("vector", lambda e: e.tensor_reduce(qsf[:, 0:ng], v3, AX.X, ALU.add, apply_absolute_value=True), deps=[in_tok], inc="dv")
        kb.emit("vector", lambda e: e.tensor_scalar(qsf[:, 0:ng], qsf[:, 0:ng], 1.0 / 64.0, 1e-8, ALU.mult, ALU.max), inc="dv")
        kb.emit("vector", lambda e: e.tensor_copy(qsb[:, 0:ng], qsf[:, 0:ng]), inc="dv")
        kb.emit("vector", lambda e: e.tensor_scalar_mul(qtb[:, 0:ng], qsb[:, 0:ng], 0.5), inc="dv")
        kb.emit("vector", lambda e: e.tensor_scalar_mul(qtn[:, 0:ng], qtb[:, 0:ng], -1.0), inc="dv")
        d1 = wstr[:, WD1:WD1 + 2048]
        d2 = wstr[:, WD2:WD2 + 2048]
        tpb = qtb[:, 0:ng][:, :, None].broadcast_to([128, ng, 64])
        tnb = qtn[:, 0:ng][:, :, None].broadcast_to([128, ng, 64])
        sbv = qsb[:, 0:ng][:, :, None].broadcast_to([128, ng, 64])
        kb.emit("vector", lambda e: e.tensor_tensor(d1.rearrange("p (g c) -> p g c", c=64), v3, tpb, ALU.is_gt), deps=[ST.get("wstg")], inc="dv")
        kb.emit("vector", lambda e: e.tensor_tensor(d2.rearrange("p (g c) -> p g c", c=64), v3, tnb, ALU.is_lt), inc="dv")
        kb.emit("vector", lambda e: e.tensor_tensor(d1, d1, d2, ALU.subtract), inc="dv")
        te = kb.emit("vector", lambda e: e.tensor_tensor(v3, d1.rearrange("p (g c) -> p g c", c=64), sbv, ALU.mult), inc="dv")
        return te

    NBG = H2 // 1024
    NBP = 16 // (2048 // SWP)
    NWB = NBG + NBP

    def w_batch_emit(bi):
        slot = bi % 2
        qo = WQI[slot]
        src = wstr[:, qo:qo + 2048]
        if bi < NBG:
            b = bi
            K = SWG // 128

            def ld(e, b=b, qo=qo):
                return e.dma_start(out=wstr[:, qo:qo + 2048].rearrange("p (s c) -> p s c", c=SWG),
                                   in_=wgu_h.ap()[b * 1024:(b + 1) * 1024, :].rearrange("(s p) c -> p s c", p=128))
        else:
            b = bi - NBG
            nbp = 2048 // SWP
            K = SWP // 128

            def ld(e, b=b, qo=qo, nbp=nbp):
                return e.dma_start(out=wstr[:, qo:qo + 2048].rearrange("p (s c) -> p s c", c=SWP),
                                   in_=wpj_h.ap()[b * nbp * 128:(b + 1) * nbp * 128, :].rearrange("(s p) c -> p s c", p=128))
        tin = kb.emit("gpsimd", ld, deps=(ST.get(("wcons", slot)), WL.get(2), WL.get(3)), inc="qi%d" % slot, dma=True)
        te = quant_chunk(src, tin)
        stage = wstr[:, WD1:WD1 + 2048]
        stgv = stage.rearrange("p (k s c) -> p k s c", k=K, c=128)
        nS = 16 // K
        evs = []
        for tt in range(16):
            pb = 6 + (tt // 4) % 2
            last_mm = kb.emit("tensor", lambda e, tt=tt, pb=pb, src=src: e.matmul(
                bank(pb)[:, (tt % 4) * 128:(tt % 4) * 128 + 128],
                src[:, tt * 128:(tt + 1) * 128], ident[:, :],
                start=True, stop=True),
                deps=[te, TAB, BG.get(pb)], inc="pe")
            if tt % 4 == 3:
                tt0 = tt - 3
                if K >= 4:
                    s0, k0 = tt0 // K, tt0 % K
                    o = stgv[:, k0:k0 + 4, s0:s0 + 1, :]
                    i = bank(pb).rearrange("p (t o c) -> p t o c", t=4, o=1)
                else:
                    s0 = tt0 // K
                    o = stgv.transpose([0, 2, 1, 3])[:, s0:s0 + 2, :, :]
                    i = bank(pb).rearrange("p (s k c) -> p s k c", s=2, k=2)

                def ev(e, o=o, i=i):
                    return e.copy(o, i)
                tv = kb.emit("scalar", ev, deps=[last_mm, ST.get("wstg")], inc="ac")
                BG[pb] = tv
                evs.append(tv)
        ST[("wcons", slot)] = last_mm
        WL[2] = last_mm
        if bi < NBG:
            def wr(e, b=b, K=K, stage=stage):
                o = shg_h.ap()[:, b * 1024:(b + 1) * 1024].rearrange("(k p) h -> p k h", p=128)
                return e.dma_start(out=o, in_=stage.rearrange("p (k sc) -> p k sc", k=K))
        else:
            def wr(e, b=b, K=K, nbp=nbp, stage=stage):
                o = shp_h.ap()[:, b * nbp * 128:(b + 1) * nbp * 128].rearrange("(k p) c -> p k c", p=128)
                return e.dma_start(out=o, in_=stage.rearrange("p (k sc) -> p k sc", k=K))
        ST["wstg"] = kb.emit("sync", wr, deps=evs, inc="ws", dma=True)
        WL[3] = ST["wstg"]

    def a_qtile(half, lq):
        gq = half * HQT + lq

        def ldx(e, gq=gq):
            return e.dma_start(out=outs[:, :],
                               in_=x_h.ap()[gq * 128:(gq + 1) * 128, :])
        xtok = kb.emit("sync", ldx, deps=(ST.get("xac"), ST.get(("og", 0)), ST.get(("og", 1))), inc="xa", dma=True)
        xs = outs[:, :]
        tcp = kb.emit("scalar", lambda e, xs=xs: e.activation(big[:, JOFF:JOFF + 2048], xs, AF.Copy, accum_out=lns[:, 0:1]), deps=[xtok, ST.get("junk")], inc="ac")
        tsq = kb.emit("scalar", lambda e, xs=xs: e.activation(big[:, JOFF:JOFF + 2048], xs, AF.Square, accum_out=lns[:, 1:2]), deps=[xtok], inc="ac")
        kb.emit("vector", lambda e: e.tensor_scalar(lns[:, 2:3], lns[:, 0:1], 1.0 / D, None, ALU.mult), deps=[tcp], inc="dv")
        kb.emit("vector", lambda e: e.tensor_scalar(lns[:, 3:4], lns[:, 1:2], 1.0 / D, None, ALU.mult), deps=[tsq], inc="dv")
        kb.emit("vector", lambda e: e.tensor_tensor(lns[:, 4:5], lns[:, 2:3], lns[:, 2:3], ALU.mult), inc="dv")
        tm4 = kb.emit("vector", lambda e: e.scalar_tensor_tensor(lns[:, 5:6], lns[:, 3:4], LN_EPS, lns[:, 4:5], ALU.add, ALU.subtract), inc="dv")
        tl1 = kb.emit("scalar", lambda e: e.activation(lns[:, 6:7], lns[:, 5:6], AF.Ln), deps=[tm4], inc="ac")
        tl2 = kb.emit("scalar", lambda e: e.activation(lns[:, 7:8], lns[:, 6:7], AF.Exp, scale=-0.5), inc="ac")
        txn = kb.emit("vector", lambda e, xs=xs: e.tensor_scalar(big[:, XOFF:XOFF + 2048], xs, lns[:, 2:3], lns[:, 7:8], ALU.subtract, ALU.mult), deps=[tl2], inc="dv")
        ST["junk"] = txn
        ST["xac"] = txn
        tv = None
        for dt in range(16):
            pb = 6 + (dt // 4) % 2
            tmm = kb.emit("tensor", lambda e, dt=dt, pb=pb: e.matmul(
                bank(pb)[:, (dt % 4) * 128:(dt % 4) * 128 + 128],
                big[:, XOFF + dt * 128:XOFF + (dt + 1) * 128], ident[:, :],
                start=True, stop=True), deps=[txn, TAB, BG.get(pb)], inc="pe")
            if dt % 4 == 3:
                dt0 = dt - 3

                def evx(e, pb=pb, dt0=dt0, lq=lq):
                    o = xnt.rearrange("p (dt c) -> p dt c", dt=16)[:, dt0:dt0 + 4, lq * 128:(lq + 1) * 128]
                    return e.copy(o, bank(pb).rearrange("p (t c) -> p t c", t=4))
                tv = kb.emit("scalar", evx, deps=[tmm], inc="ac")
                BG[pb] = tv
        ST[("xnq", half, lq)] = tv
        ST[("xnt", half)] = tv

    def w_feed(dram_ap, dst_cols, width):
        par = ST.get("cstp", 0)
        ST["cstp"] = 1 - par
        stg = xf[:, 2048 + par * 1024: 2048 + par * 1024 + width]

        def ld(e, dram_ap=dram_ap, stg=stg):
            return e.dma_start(out=stg, in_=dram_ap)
        lt = kb.emit("sync", ld, deps=(ST.get(("cst", par)),
                                       ("ab0", kb.cnt.get("ab0", 0)), ("ab1", kb.cnt.get("ab1", 0)),
                                       ST.get(("ar", 0)), ST.get(("ar", 1))), inc="aw%d" % par, dma=True)
        reg = dst_cols[0] // 4096
        ct = kb.emit("scalar", lambda e, stg=stg, dst_cols=dst_cols: e.copy(wstr[:, dst_cols[0]:dst_cols[1]], stg),
                     deps=[lt, WL.get(reg)], inc="ac")
        ST[("cst", par)] = ct
        return ct

    def b_phase(half, gsel=None):
        for g in (range(NG) if gsel is None else [gsel]):
            q0 = g * GQ
            for dqc in range(2):
                for dtb in range(4):
                    s4 = (dqc * 4 + dtb) % 4

                    def ldq(e, dtb=dtb, dqc=dqc, s4=s4):
                        return e.dma_start(out=wstr[:, s4 * 4096:(s4 + 1) * 4096].rearrange("p (dt c) -> p dt c", dt=4),
                                           in_=wq_h.ap()[dtb * 512:(dtb + 1) * 512, dqc * 1024:(dqc + 1) * 1024].rearrange("(dt p) c -> p dt c", p=128))
                    awt = kb.emit("gpsimd", ldq, deps=(WL.get(s4),) + WGRD(), inc="bq%d" % s4, dma=True)
                    for dtl in range(4):
                        dt = dtb * 4 + dtl
                        for q in range(GQ):
                            for hb in range(2):
                                tok = kb.emit("tensor", lambda e, dt=dt, dtl=dtl, q=q, hb=hb, s4=s4, q0=q0: e.matmul(
                                    bank(q * 2 + hb),
                                    xnt[:, dt * Th + (q0 + q) * 128: dt * Th + (q0 + q + 1) * 128],
                                    wstr[:, s4 * 4096 + dtl * 1024 + hb * 512: s4 * 4096 + dtl * 1024 + (hb + 1) * 512],
                                    start=(dt == 0), stop=(dt == 15)),
                                    deps=[awt, ST[("xnq", half, q0 + q)]] + ([BG.get(q * 2 + hb)] if dt == 0 else []),
                                    inc="pe")
                    WL[s4] = tok
                for q in range(GQ):
                    lq = q0 + q
                    gq = half * HQT + lq
                    pv = ps[:, q * 1024:(q + 1) * 1024].rearrange("p (h x) -> p h x", h=8)
                    pv4 = ps[:, q * 1024:(q + 1) * 1024].rearrange("p (h t x) -> p h t x", h=8, t=2)
                    cosv = cos_sb[:, gq * 64:(gq + 1) * 64][:, None, None, :].broadcast_to([128, 8, 2, 64])
                    sinv = sin_sb[:, gq * 64:(gq + 1) * 64][:, None, :].broadcast_to([128, 8, 64])
                    t1v = xf[:, 0:1024].rearrange("p (h x) -> p h x", h=8)
                    t1v4 = xf[:, 0:1024].rearrange("p (h t x) -> p h t x", h=8, t=2)
                    t2v = xf[:, 1024:2048].rearrange("p (h x) -> p h x", h=8)
                    qcd = big[:, lq * 2048 + dqc * 1024: lq * 2048 + (dqc + 1) * 1024].rearrange("p (h x) -> p h x", h=8)
                    kb.emit("vector", lambda e, pv4=pv4, cosv=cosv, t1v4=t1v4: e.tensor_tensor(t1v4, pv4, cosv, ALU.mult), deps=[tok, TAB] + list(XGRD()), inc="dv")
                    kb.emit("vector", lambda e, pv=pv, sinv=sinv, t2v=t2v: e.tensor_tensor(t2v[:, :, 0:64], pv[:, :, 64:128], sinv, ALU.mult), inc="dv")
                    kb.emit("vector", lambda e, pv=pv, sinv=sinv, t2v=t2v: e.tensor_tensor(t2v[:, :, 64:128], pv[:, :, 0:64], sinv, ALU.mult), inc="dv")
                    kb.emit("vector", lambda e, qcd=qcd, t1v=t1v, t2v=t2v: e.tensor_tensor(qcd[:, :, 0:64], t1v[:, :, 0:64], t2v[:, :, 0:64], ALU.subtract), inc="dv")
                    r5 = kb.emit("vector", lambda e, qcd=qcd, t1v=t1v, t2v=t2v: e.tensor_tensor(qcd[:, :, 64:128], t1v[:, :, 64:128], t2v[:, :, 64:128], ALU.add), inc="dv")
                    BG[q * 2] = r5
                    BG[q * 2 + 1] = r5
                    ST["ropescr"] = r5
            for dtb in range(4):
                s4 = dtb % 4

                def ldk(e, dtb=dtb, s4=s4):
                    return e.dma_start(out=wstr[:, s4 * 4096:(s4 + 1) * 4096].rearrange("p (dt two c) -> p dt two c", dt=4, two=2)[:, :, 0, :],
                                       in_=wk_h.ap()[dtb * 512:(dtb + 1) * 512, :].rearrange("(dt p) c -> p dt c", p=128))

                def ldv(e, dtb=dtb, s4=s4):
                    return e.dma_start(out=wstr[:, s4 * 4096:(s4 + 1) * 4096].rearrange("p (dt two c) -> p dt two c", dt=4, two=2)[:, :, 1, :],
                                       in_=wv_h.ap()[dtb * 512:(dtb + 1) * 512, :].rearrange("(dt p) c -> p dt c", p=128))
                kb.emit("gpsimd", ldk, deps=(WL.get(s4),) + WGRD(), inc="bq%d" % s4, dma=True)
                awt = kb.emit("gpsimd", ldv, deps=(), inc="bq%d" % s4, dma=True)
                for dtl in range(4):
                    dt = dtb * 4 + dtl
                    for q in range(GQ):
                        for kvh in range(2):
                            tok = kb.emit("tensor", lambda e, dt=dt, dtl=dtl, q=q, kvh=kvh, s4=s4, q0=q0: e.matmul(
                                bank(q * 2 + kvh),
                                xnt[:, dt * Th + (q0 + q) * 128: dt * Th + (q0 + q + 1) * 128],
                                wstr[:, s4 * 4096 + dtl * 1024 + kvh * 512: s4 * 4096 + dtl * 1024 + (kvh + 1) * 512],
                                start=(dt == 0), stop=(dt == 15)),
                                deps=[awt] + ([BG.get(q * 2 + kvh)] if dt == 0 else []),
                                inc="pe")
                WL[s4] = tok
            for q in range(GQ):
                lq = q0 + q
                gq = half * HQT + lq
                pkv = bank(q * 2).rearrange("p (h x) -> p h x", h=HKV)
                pk4 = bank(q * 2).rearrange("p (h t x) -> p h t x", h=HKV, t=2)
                cosv = cos_sb[:, gq * 64:(gq + 1) * 64][:, None, None, :].broadcast_to([128, HKV, 2, 64])
                sinv = sin_sb[:, gq * 64:(gq + 1) * 64][:, None, :].broadcast_to([128, HKV, 64])
                t1v = xf[:, 0:512].rearrange("p (h x) -> p h x", h=HKV)
                t1v4 = xf[:, 0:512].rearrange("p (h t x) -> p h t x", h=HKV, t=2)
                t2v = xf[:, 1024:1536].rearrange("p (h x) -> p h x", h=HKV)
                kdst = sl[:, 0:512].rearrange("p (h x) -> p h x", h=HKV)
                kb.emit("vector", lambda e, pk4=pk4, cosv=cosv, t1v4=t1v4: e.tensor_tensor(t1v4, pk4, cosv, ALU.mult), deps=[tok] + list(XGRD()), inc="dv")
                kb.emit("vector", lambda e, pkv=pkv, sinv=sinv, t2v=t2v: e.tensor_tensor(t2v[:, :, 0:64], pkv[:, :, 64:128], sinv, ALU.mult), inc="dv")
                r3 = kb.emit("vector", lambda e, pkv=pkv, sinv=sinv, t2v=t2v: e.tensor_tensor(t2v[:, :, 64:128], pkv[:, :, 0:64], sinv, ALU.mult), inc="dv")
                kb.emit("vector", lambda e, kdst=kdst, t1v=t1v, t2v=t2v: e.tensor_tensor(kdst[:, :, 0:64], t1v[:, :, 0:64], t2v[:, :, 0:64], ALU.subtract), deps=[ST.get("ktm")], inc="dv")
                r5 = kb.emit("vector", lambda e, kdst=kdst, t1v=t1v, t2v=t2v: e.tensor_tensor(kdst[:, :, 64:128], t1v[:, :, 64:128], t2v[:, :, 64:128], ALU.add), inc="dv")
                ST["ropescr"] = r5
                vdst = vb[:, gq * HKV * 132: (gq + 1) * HKV * 132].rearrange("p (h x) -> p h x", x=132)[:, :, 0:128]
                tvv = kb.emit("scalar", lambda e, vdst=vdst, q=q: e.copy(vdst, bank(q * 2 + 1).rearrange("p (h x) -> p h x", h=HKV)), deps=[tok], inc="ac")
                BG[q * 2 + 1] = tvv
                ST["vbw"] = tvv
                last_mm = None
                for kv in range(HKV):
                    last_mm = kb.emit("tensor", lambda e, kv=kv, q=q: e.matmul(
                        bank(q * 2)[:, kv * 128:(kv + 1) * 128],
                        sl[:, kv * 128:(kv + 1) * 128], ident[:, :],
                        start=True, stop=True), deps=[r5, r3], inc="pe")

                def evk(e, gq=gq, q=q):
                    o = kt.rearrange("p (k t) -> p k t", k=HKV)[:, :, gq * 128:(gq + 1) * 128]
                    return e.tensor_copy(o, bank(q * 2).rearrange("p (k c) -> p k c", k=HKV))
                tke = kb.emit("vector", evk, deps=[last_mm], inc="dv")
                BG[q * 2] = tke
                ST["ktm"] = tke
        ST[("bdone", half)] = ST["ktm"]

    def c_phase(half):
        e_gate = {}
        bdone = ST[("bdone", half)]
        vbw = ST["vbw"]
        qt_done = {}
        qslot_last = {}
        for head in range(HQ):
            if head % 4 == 0:
                for h2 in range(head, head + 4):
                    hs = h2 % 4
                    for lq0 in range(0, HQT, 4):
                        n = min(4, HQT - lq0)
                        pb = 6 + ((h2 * ((HQT + 3) // 4) + lq0 // 4) % 2)
                        last_mm = None
                        for i in range(n):
                            lq = lq0 + i
                            last_mm = kb.emit("tensor", lambda e, lq=lq, h2=h2, pb=pb, i=i: e.matmul(
                                bank(pb)[:, i * 128:(i + 1) * 128],
                                big[:, lq * 2048 + h2 * 128: lq * 2048 + (h2 + 1) * 128], ident[:, :],
                                start=True, stop=True),
                                deps=[BG.get(pb), bdone, TAB, qslot_last.get(hs)], inc="pe")

                        def evq(e, pb=pb, lq0=lq0, n=n, hs=hs):
                            return e.tensor_copy(qtb_sb[:, hs * Th + lq0 * 128: hs * Th + (lq0 + n) * 128], bank(pb)[:, 0:n * 128])
                        tv = kb.emit("vector", evq, deps=[last_mm], inc="dv")
                        BG[pb] = tv
                    qt_done[h2] = tv
            kv = head % HKV
            for qc in range(NQC):
                gqc = half * NQC + qc
                nkt = (gqc + 1) * QCW // 128
                kt0_cross = gqc * QCW // 128
                for ktile in range(nkt):
                    pb = ktile % 4
                    tmm = kb.emit("tensor", lambda e, ktile=ktile, kv=kv, qc=qc, pb=pb, head=head: e.matmul(
                        bank(pb, QCW),
                        kt[:, kv * T + ktile * 128: kv * T + (ktile + 1) * 128],
                        qtb_sb[:, (head % 4) * Th + qc * QCW: (head % 4) * Th + (qc + 1) * QCW],
                        start=True, stop=True),
                        deps=[qt_done[head], BG.get(pb)], inc="pe")
                    qslot_last[head % 4] = tmm
                    if ktile % 2 == 1 or ktile == nkt - 1:
                        k0 = ktile - (ktile % 2)
                        n2 = ktile - k0 + 1
                        tex = kb.emit("scalar", lambda e, k0=k0, n2=n2: e.activation(
                            big[:, EOFF + k0 * QCW: EOFF + (k0 + n2) * QCW].rearrange("p (b c) -> p b c", b=n2),
                            ps.rearrange("p (b c) -> p b c", b=8)[:, k0 % 4:k0 % 4 + n2, 0:QCW], AF.Exp, scale=SM_SCALE),
                            deps=[tmm, e_gate.get(("e", k0)), e_gate.get(("e", k0 + n2 - 1))], inc="ac")
                        for kk in range(k0, k0 + n2):
                            BG[kk % 4] = tex
                            e_gate[("m", kk)] = tex
                dmask = {}
                for ql in range(QCW // 128):
                    gq = gqc * (QCW // 128) + ql
                    dmask[ql] = kb.emit("vector", lambda e, gq=gq, qc=qc, ql=ql: e.tensor_tensor(
                        big[:, EOFF + gq * QCW + ql * 128: EOFF + gq * QCW + (ql + 1) * 128],
                        big[:, EOFF + gq * QCW + ql * 128: EOFF + gq * QCW + (ql + 1) * 128],
                        maskc[:, :], ALU.mult), deps=[e_gate.get(("m", gq)), TAB], inc="dv")
                for ql in range(QCW // 128):
                    lq = qc * (QCW // 128) + ql
                    gq = half * HQT + lq
                    cb = 4 + ql % 2
                    for ktile in range(gq + 1):
                        tok = kb.emit("tensor", lambda e, ktile=ktile, ql=ql, kv=kv, gq=gq, cb=cb, qc=qc: e.matmul(
                            bank(cb)[:, 0:129],
                            big[:, EOFF + ktile * QCW + ql * 128: EOFF + ktile * QCW + (ql + 1) * 128],
                            vb[:, ktile * HKV * 132 + kv * 132: ktile * HKV * 132 + kv * 132 + 129],
                            start=(ktile == 0), stop=(ktile == gq)),
                            deps=[dmask[ql] if ktile == gq else e_gate.get(("m", ktile))] + ([BG.get(cb), vbw] if ktile == 0 else []),
                            inc="pe")
                    kb.emit("vector", lambda e, cb=cb, ql=ql: e.reciprocal(rss[:, ql:ql + 1], bank(cb)[:, 128:129]), deps=[tok], inc="dv")
                    tcx = kb.emit("vector", lambda e, cb=cb, ql=ql, lq=lq, head=head: e.tensor_scalar(
                        big[:, lq * 2048 + head * 128: lq * 2048 + (head + 1) * 128],
                        bank(cb)[:, 0:128], rss[:, ql:ql + 1], None, ALU.mult), inc="dv")
                    BG[cb] = tcx
                    ST[("ctx", head, lq)] = tcx
                lastq = qc * (QCW // 128) + (QCW // 128) - 1
                for ktile in range(nkt):
                    e_gate[("e", ktile)] = ST[("ctx", head, lastq)]

    def d_phase(half):
        for g in range(NG):
            q0 = g * GQ
            ct_done = []
            for q in range(GQ):
                lq = q0 + q
                for dt in range(16):
                    pb = 6 + (dt // 4) % 2
                    tmm = kb.emit("tensor", lambda e, lq=lq, dt=dt, pb=pb: e.matmul(
                        bank(pb)[:, (dt % 4) * 128:(dt % 4) * 128 + 128],
                        big[:, lq * 2048 + dt * 128: lq * 2048 + (dt + 1) * 128], ident[:, :],
                        start=True, stop=True),
                        deps=[BG.get(pb), ST.get(("ctx", HQ - 1, lq)), TAB], inc="pe")
                    if dt % 4 == 3:
                        dt0 = dt - 3

                        def evc(e, pb=pb, dt0=dt0, q=q):
                            return e.tensor_copy(big[:, EOFF + q * 2048 + dt0 * 128: EOFF + q * 2048 + (dt0 + 4) * 128], bank(pb))
                        tv = kb.emit("vector", evc, deps=[tmm, ST.get(("ctc", q))], inc="dv")
                        BG[pb] = tv
                        ct_done.append(tv)
            for dc in range(2):
                for dtb in range(4):
                    s4 = (dc * 4 + dtb) % 4
                    if half == 1:
                        def ldwo(e, dtb=dtb, dc=dc, s4=s4):
                            return e.dma_start(out=wstr[:, s4 * 4096:(s4 + 1) * 4096].rearrange("p (dt c) -> p dt c", dt=4),
                                               in_=wo_h.ap()[dtb * 512:(dtb + 1) * 512, dc * 1024:(dc + 1) * 1024].rearrange("(dt p) c -> p dt c", p=128))
                        wot = kb.emit("gpsimd", ldwo, deps=(WL.get(s4),) + WGRD(), inc="dq%d" % s4, dma=True)
                        ctoks = [wot] * 4
                    else:
                        ctoks = []
                        for dtl in range(4):
                            dt = dtb * 4 + dtl
                            ctoks.append(w_feed(wo_h.ap()[dt * 128:(dt + 1) * 128, dc * 1024:(dc + 1) * 1024],
                                                (s4 * 4096 + dtl * 1024, s4 * 4096 + dtl * 1024 + 1024), 1024))
                    for dtl in range(4):
                        dt = dtb * 4 + dtl
                        for q in range(GQ):
                            for hb in range(2):
                                tok = kb.emit("tensor", lambda e, dt=dt, dtl=dtl, q=q, hb=hb, s4=s4: e.matmul(
                                    bank(q * 2 + hb),
                                    big[:, EOFF + q * 2048 + dt * 128: EOFF + q * 2048 + (dt + 1) * 128],
                                    wstr[:, s4 * 4096 + dtl * 1024 + hb * 512: s4 * 4096 + dtl * 1024 + (hb + 1) * 512],
                                    start=(dt == 0), stop=(dt == 15)),
                                    deps=[ctoks[dtl]] + (ct_done + [BG.get(q * 2 + hb)] if dt == 0 else []),
                                    inc="pe")
                    WL[s4] = tok
                for q in range(GQ):
                    gq = half * HQT + q0 + q
                    s2 = q % 2
                    o2 = outs[:, s2 * 1024:(s2 + 1) * 1024]
                    tv = kb.emit("vector", lambda e, q=q, o2=o2: e.tensor_copy(o2, ps[:, q * 1024:(q + 1) * 1024]),
                                 deps=[tok, ST.get(("og", s2))], inc="dv")
                    BG[q * 2] = tv
                    BG[q * 2 + 1] = tv
                    if dc == 1:
                        ST[("ctc", q)] = tv

                    def sta(e, gq=gq, dc=dc, o2=o2):
                        return e.dma_start(out=att_h.ap()[gq * 128:(gq + 1) * 128, dc * 1024:(dc + 1) * 1024], in_=o2)
                    ST[("og", s2)] = kb.emit("sync", sta, deps=[tv], inc="fs%d" % s2, dma=True)

    def f_phase(half):
        WSD = ("cc", 1)
        WSDP = ("cc", 2)
        FSD = (("fs0", kb.cnt.get("fs0", 0)), ("fs1", kb.cnt.get("fs1", 0)))
        for tb in range(NTB):
            q0 = tb * TBQ
            ev_gate = {}
            h_tok = {}
            tok = None
            for j in range(NJ):
                for gu in range(2):
                    p2 = (2 * j + gu) % 2
                    sA, sB = 2 * p2, 2 * p2 + 1
                    hcol = gu * HF + j * 512

                    def ldw1(e, hcol=hcol, sA=sA):
                        return e.dma_start(out=wstr[:, sA * 4096:(sA + 1) * 4096].rearrange("p (dt c) -> p dt c", dt=8),
                                           in_=wtg_h.ap()[0:1024, hcol:hcol + 512].rearrange("(dt p) c -> p dt c", p=128))

                    def ldw2(e, hcol=hcol, sB=sB):
                        return e.dma_start(out=wstr[:, sB * 4096:(sB + 1) * 4096].rearrange("p (dt c) -> p dt c", dt=8),
                                           in_=wtg_h.ap()[1024:2048, hcol:hcol + 512].rearrange("(dt p) c -> p dt c", p=128))
                    wgA = kb.emit("sync", ldw1, deps=(WL.get(sA), WSD, ("ws", kb.cnt.get("ws", 0))) + WGRD(), inc="wg%d" % sA, dma=True)
                    wgB = kb.emit("scalar", ldw2, deps=(WL.get(sB), WSD, ("ws", kb.cnt.get("ws", 0))) + WGRD(), inc="wg%d" % sB, dma=True)
                    for dt in range(16):
                        sl_i = sA if dt < 8 else sB
                        wt = wgA if dt < 8 else wgB
                        for q in range(TBQ):
                            bnk = q * 2 + gu
                            tok = kb.emit("tensor", lambda e, dt=dt, q=q, bnk=bnk, sl_i=sl_i, q0=q0: e.matmul(
                                bank(bnk),
                                xnt[:, dt * Th + (q0 + q) * 128: dt * Th + (q0 + q + 1) * 128],
                                wstr[:, sl_i * 4096 + (dt % 8) * 512: sl_i * 4096 + (dt % 8 + 1) * 512],
                                start=(dt == 0), stop=(dt == 15)),
                                deps=[wt] + ([BG.get(bnk), ST.get(("xnq", half, q0 + q))] if dt == 0 else []),
                                inc="pe")
                        if dt == 7:
                            WL[sA] = tok
                    WL[sB] = tok
                for q in range(TBQ):
                    tsl = kb.emit("scalar", lambda e, q=q: e.activation(sl[:, q * 512:(q + 1) * 512], bank(q * 2), AF.Silu), deps=[tok, ev_gate.get(("sl", q))], inc="ac")
                    th = kb.emit("vector", lambda e, q=q, j=j: e.tensor_tensor(
                        big[:, q * HF + j * 512: q * HF + (j + 1) * 512],
                        sl[:, q * 512:(q + 1) * 512], bank(q * 2 + 1), ALU.mult), deps=[tsl, tok], inc="dv")
                    BG[q * 2] = tsl
                    BG[q * 2 + 1] = th
                    ev_gate[("sl", q)] = th
                    h_tok[(q, j)] = th
            hnt_done = []
            for q in range(TBQ):
                hq = big[:, q * HF:(q + 1) * HF]
                deps_h = [h_tok[(q, j)] for j in range(NJ)]
                tsq = kb.emit("scalar", lambda e, hq=hq, q=q: e.activation(wstr[:, 0:HF], hq, AF.Square, accum_out=fss[:, q:q + 1]),
                              deps=deps_h + [ST.get("fjunk")] + [WL.get(i) for i in range(4) if i * 4096 < HF], inc="ac")
                ST["fjunk"] = tsq
                for i in range(4):
                    if i * 4096 < HF:
                        WL[i] = tsq
                tms = kb.emit("vector", lambda e, q=q: e.tensor_scalar(fss[:, 4 + q:5 + q], fss[:, q:q + 1], 1.0 / HF, RMS_EPS, ALU.mult, ALU.add), deps=[tsq], inc="dv")
                kb.emit("scalar", lambda e, q=q: e.activation(fss[:, 4 + q:5 + q], fss[:, 4 + q:5 + q], AF.Ln), deps=[tms], inc="ac")
                ti = kb.emit("scalar", lambda e, q=q: e.activation(fss[:, 8 + q:9 + q], fss[:, 4 + q:5 + q], AF.Exp, scale=-0.5), inc="ac")
                thn = kb.emit("vector", lambda e, hq=hq, q=q: e.tensor_scalar(hq, hq, fss[:, 8 + q:9 + q], None, ALU.mult), deps=[ti], inc="dv")
                for ht in range(NHT):
                    pb = 6 + (ht // 4) % 2
                    tmm = kb.emit("tensor", lambda e, hq=hq, ht=ht, pb=pb: e.matmul(
                        bank(pb)[:, (ht % 4) * 128:(ht % 4) * 128 + 128],
                        hq[:, ht * 128:(ht + 1) * 128], ident[:, :],
                        start=True, stop=True),
                        deps=[thn, BG.get(pb)], inc="pe")
                    if ht % 4 == 3:
                        ht0 = ht - 3

                        def evh(e, pb=pb, ht0=ht0, hq=hq):
                            return e.tensor_copy(hq[:, ht0 * 128:(ht0 + 4) * 128], bank(pb))
                        tv = kb.emit("vector", evh, deps=[tmm], inc="dv")
                        BG[pb] = tv
                        hnt_done.append(tv)
            xtok = {}
            attok = {}
            NHG = NHT // 4
            for dc in range(2):
                for q in range(min(2, TBQ)):
                    gq = half * HQT + q0 + q

                    def ldx2(e, gq=gq, q=q, dc=dc):
                        return e.dma_start(out=xf[:, (q % 2) * 1024:(q % 2 + 1) * 1024],
                                           in_=x_h.ap()[gq * 128:(gq + 1) * 128, dc * 1024:(dc + 1) * 1024])

                    def lda(e, gq=gq, q=q, dc=dc):
                        return e.dma_start(out=xf[:, 2048 + (q % 2) * 1024: 2048 + (q % 2 + 1) * 1024],
                                           in_=att_h.ap()[gq * 128:(gq + 1) * 128, dc * 1024:(dc + 1) * 1024])
                    xtok[q] = kb.emit("sync", ldx2, deps=(ST.get("ropescr"), ST.get(("xr", q % 2))), inc="x%d" % (q % 2), dma=True)
                    attok[q] = kb.emit("sync", lda, deps=(ST.get(("ar", q % 2)), ST.get(("cst", 0)), ST.get(("cst", 1))) + FSD, inc="ab%d" % (q % 2), dma=True)
                for hg in range(NHG):
                    s4 = (dc * NHG + hg) % 4

                    def ldp2(e, hg=hg, dc=dc, s4=s4):
                        return e.dma_start(out=wstr[:, s4 * 4096:(s4 + 1) * 4096].rearrange("p (ht c) -> p ht c", ht=4),
                                           in_=wtp_h.ap()[hg * 512:(hg + 1) * 512, dc * 1024:(dc + 1) * 1024].rearrange("(ht p) c -> p ht c", p=128))
                    wpt = kb.emit("sync" if s4 % 2 == 0 else "scalar", ldp2, deps=(WL.get(s4), WSDP) + WGRD(), inc="wp%d" % s4, dma=True)
                    for htl in range(4):
                        ht = hg * 4 + htl
                        for q in range(TBQ):
                            for hb in range(2):
                                tok = kb.emit("tensor", lambda e, htl=htl, ht=ht, q=q, hb=hb, s4=s4: e.matmul(
                                    bank(q * 2 + hb),
                                    big[:, q * HF + ht * 128: q * HF + (ht + 1) * 128],
                                    wstr[:, s4 * 4096 + htl * 1024 + hb * 512: s4 * 4096 + htl * 1024 + (hb + 1) * 512],
                                    start=(ht == 0), stop=(ht == NHT - 1)),
                                    deps=[wpt] + (hnt_done + [BG.get(q * 2 + hb)] if ht == 0 else []),
                                    inc="pe")
                    WL[s4] = tok
                for q in range(TBQ):
                    gq = half * HQT + q0 + q
                    s2 = q % 2
                    o2 = outs[:, s2 * 1024:(s2 + 1) * 1024]
                    ta = kb.emit("vector", lambda e, s2=s2, q=q, o2=o2: e.tensor_tensor(
                        o2, ps[:, q * 1024:(q + 1) * 1024],
                        xf[:, s2 * 1024:(s2 + 1) * 1024], ALU.add),
                        deps=[tok, xtok[q], ST.get(("og", s2))], inc="dv")
                    tb2 = kb.emit("vector", lambda e, s2=s2, o2=o2: e.tensor_tensor(
                        o2, o2, xf[:, 2048 + s2 * 1024: 2048 + (s2 + 1) * 1024], ALU.add),
                        deps=[attok[q]], inc="dv")
                    BG[q * 2] = tb2
                    BG[q * 2 + 1] = tb2
                    ST[("xr", s2)] = ta
                    ST[("ar", s2)] = tb2

                    def sto(e, gq=gq, dc=dc, o2=o2):
                        return e.dma_start(out=out_h.ap()[gq * 128:(gq + 1) * 128, dc * 1024:(dc + 1) * 1024], in_=o2)
                    ST[("og", s2)] = kb.emit("scalar", sto, deps=[tb2], inc="os%d" % s2, dma=True)
                    if q + 2 < TBQ:
                        gq2 = half * HQT + q0 + q + 2

                        def ldx3(e, gq2=gq2, s2=s2, dc=dc):
                            return e.dma_start(out=xf[:, s2 * 1024:(s2 + 1) * 1024],
                                               in_=x_h.ap()[gq2 * 128:(gq2 + 1) * 128, dc * 1024:(dc + 1) * 1024])

                        def lda3(e, gq2=gq2, s2=s2, dc=dc):
                            return e.dma_start(out=xf[:, 2048 + s2 * 1024: 2048 + (s2 + 1) * 1024],
                                               in_=att_h.ap()[gq2 * 128:(gq2 + 1) * 128, dc * 1024:(dc + 1) * 1024])
                        xtok[q + 2] = kb.emit("sync", ldx3, deps=(ta,), inc="x%d" % s2, dma=True)
                        attok[q + 2] = kb.emit("sync", lda3, deps=(tb2,) + FSD, inc="ab%d" % s2, dma=True)

    per = max(1, (NWB + HQT - 1) // HQT)
    RG = [[0, 1, 2, 3, 4, 5, 6, 7]]
    kb.sem("cc")
    kb.dma_sems.add("cc")
    wi = 0
    for g in range(NG):
        for lq in range(g * GQ, (g + 1) * GQ):
            for _ in range(per):
                if wi < NWB:
                    w_batch_emit(wi)
                    wi += 1
            a_qtile(0, lq)
        if g == NG - 1:
            while wi < NWB:
                w_batch_emit(wi)
                wi += 1
        b_phase(0, gsel=g)
    kb.emit("gpsimd", lambda e: e.collective_compute("AllGather", ALU.bypass, RG, [shg_h.ap().opt()], [wtg_h.ap().opt()]), deps=[("ws", kb.cnt["ws"]), ("pe", kb.cnt["pe"])], inc="cc")
    kb.emit("gpsimd", lambda e: e.collective_compute("AllGather", ALU.bypass, RG, [shp_h.ap().opt()], [wtp_h.ap().opt()]), inc="cc")
    c_phase(0)
    d_phase(0)
    f_phase(0)
    kb.block()
    for lq in range(HQT):
        a_qtile(1, lq)
    b_phase(1)
    c_phase(1)
    d_phase(1)
    f_phase(1)
    kb.block()
    kb.emit("sync", lambda e: e.nop(), deps=[("os0", kb.cnt["os0"]), ("os1", kb.cnt.get("os1", 0)),
                                             ("fs0", kb.cnt.get("fs0", 0)), ("fs1", kb.cnt.get("fs1", 0)),
                                             ("cc", 2), ("ws", kb.cnt["ws"])])
    kb.realize()
    return nc


_NC_CACHE = {}


def _get_nc(T, HF):
    key = (T, HF)
    if key not in _NC_CACHE:
        _NC_CACHE[key] = build_nc(T, HF)
    return _NC_CACHE[key]


def _host_tables(T):
    bf = ml_dtypes.bfloat16
    inv = 1.0 / (10000.0 ** (np.arange(0, DH, 2, dtype=np.float64) / DH))
    ang = np.arange(T, dtype=np.float64)[:, None] * inv[None, :]
    cos = np.cos(ang).astype(np.float32)
    sin = np.sin(ang).astype(np.float32)
    costab = np.ascontiguousarray(cos)
    sintab = np.ascontiguousarray(sin)
    ident = np.eye(128, dtype=bf)
    k_i = np.arange(128)[:, None]
    t_i = np.arange(128)[None, :]
    maskc = (t_i >= k_i).astype(bf)
    return costab, sintab, ident, maskc


def _deinterleave_cols(w, block=128):
    d_out = w.shape[1]
    perm = []
    for b0 in range(0, d_out, block):
        perm.extend(range(b0, b0 + block, 2))
        perm.extend(range(b0 + 1, b0 + block, 2))
    return np.ascontiguousarray(w[:, perm])


def _permute_wo_rows(wo):
    rows = []
    for h in range(HQ):
        src = 4 * (h % 4) + h // 4
        rows.extend(range(src * 128, (src + 1) * 128))
    return np.ascontiguousarray(wo[rows, :])


def kernel(L0_attn_x, L0_attn_Wq, L0_attn_Wk, L0_attn_Wv, L0_attn_Wo,
           L0_ffn_w_gate_up, L0_ffn_w_proj):
    x = np.ascontiguousarray(np.asarray(L0_attn_x, dtype=np.float32))
    wq = _deinterleave_cols(np.asarray(L0_attn_Wq, dtype=np.float32))
    wk = _deinterleave_cols(np.asarray(L0_attn_Wk, dtype=np.float32))
    wv = np.ascontiguousarray(np.asarray(L0_attn_Wv, dtype=np.float32))
    wo = _permute_wo_rows(np.asarray(L0_attn_Wo, dtype=np.float32))
    wgu = np.ascontiguousarray(np.asarray(L0_ffn_w_gate_up, dtype=np.float32))
    wpj = np.ascontiguousarray(np.asarray(L0_ffn_w_proj, dtype=np.float32))
    B, T, _ = x.shape
    HF = wpj.shape[1]
    costab, sintab, ident, maskc = _host_tables(T)
    nc = _get_nc(T, HF)
    swg, swp = D // 8, HF // 8
    in_maps = [dict(x=x[b], wq=wq, wk=wk, wv=wv, wo=wo,
                    wgu_s=np.ascontiguousarray(wgu[:, b * swg:(b + 1) * swg]),
                    wproj_s=np.ascontiguousarray(wpj[:, b * swp:(b + 1) * swp]),
                    costab=costab, sintab=sintab, ident=ident, maskc=maskc)
               for b in range(B)]
    res = run_bass_kernel_spmd(nc, in_maps, core_ids=list(range(B)))
    return np.stack([np.asarray(res.results[b]["out"]) for b in range(B)], axis=0)
